# version 06
# baseline (speedup 1.0000x reference)
import numpy as np
import ml_dtypes
import concourse.bass as bass
import concourse.mybir as mybir
from concourse.bass_utils import run_bass_kernel_spmd

F32 = mybir.dt.float32
BF16 = mybir.dt.bfloat16
AF = mybir.ActivationFunctionType
ALU = mybir.AluOpType
AX = mybir.AxisListType

ENGINES = ("tensor", "vector", "scalar", "gpsimd", "sync")
QUEUES = ("sync", "gpsimd", "scalar")

DEBUG = False
DEBUG_B = 0
NT = 16
D = 1024
SEQ = 2048
NB = 2



class Op:
    __slots__ = ("idx", "engine", "fn", "deps", "is_dma", "queue", "inc_no",
                 "needed", "guar", "waits", "tag")

    def __init__(self, idx, engine, fn, is_dma=False, queue=None, tag=""):
        self.idx = idx
        self.engine = engine
        self.fn = fn
        self.deps = []
        self.is_dma = is_dma
        self.queue = queue
        self.inc_no = None
        self.needed = is_dma
        self.guar = None
        self.waits = None
        self.tag = tag


class Prog:

    def __init__(self, nc):
        self.nc = nc
        self.ops = []
        self.last_writer = {}
        self.readers = {}
        self.alias_pending = {}

    def alias(self, new_key, old_keys):
        self.alias_pending.setdefault(new_key, []).extend(old_keys)

    def _dep(self, op, d):
        if d is op:
            return
        if (d.engine == "tensor" and op.engine == "tensor"
                and not d.is_dma and not op.is_dma):
            return
        op.deps.append(d)

    def _absorb_alias(self, op, key):
        olds = self.alias_pending.pop(key, None)
        if olds:
            for ok in olds:
                lw = self.last_writer.get(ok)
                if lw is not None:
                    self._dep(op, lw)
                for r in self.readers.get(ok, ()):
                    self._dep(op, r)

    def op(self, engine, fn, reads=(), writes=(), tag=""):
        o = Op(len(self.ops), engine, fn, tag=tag)
        self._wire(o, reads, writes)
        self.ops.append(o)
        return o

    def dma(self, queue, out, in_, reads=(), writes=(), tag="", **kw):
        def fn(eng, out=out, in_=in_, kw=kw):
            return eng.dma_start(out=out, in_=in_, **kw)
        o = Op(len(self.ops), queue, fn, is_dma=True, queue=queue, tag=tag)
        self._wire(o, reads, writes)
        self.ops.append(o)
        return o

    def _wire(self, o, reads, writes):
        for r in reads:
            self._absorb_alias(o, r)
            lw = self.last_writer.get(r)
            if lw is not None:
                self._dep(o, lw)
        for w in writes:
            self._absorb_alias(o, w)
            lw = self.last_writer.get(w)
            if lw is not None:
                self._dep(o, lw)
            for rd in self.readers.get(w, ()):
                self._dep(o, rd)
        for w in writes:
            self.last_writer[w] = o
            self.readers[w] = []
        for r in reads:
            self.readers.setdefault(r, []).append(o)

    def finalize_and_emit(self, block):
        for o in self.ops:
            for d in o.deps:
                d.needed = True
        counts = {}
        for o in self.ops:
            if o.needed:
                grp = ("q", o.queue) if o.is_dma else ("e", o.engine)
                step = 16 if o.is_dma else 1
                counts[grp] = counts.get(grp, 0) + step
                o.inc_no = counts[grp]
        self.final_counts = counts
        for o in self.ops:
            need = {}
            for d in o.deps:
                grp = ("q", d.queue) if d.is_dma else ("e", d.engine)
                v = d.inc_no
                if v is not None and v > need.get(grp, 0):
                    need[grp] = v
                if d.guar:
                    for g, gv in d.guar.items():
                        if gv > need.get(g, 0):
                            need[g] = gv
            o.waits = need
            g = dict(need)
            if o.inc_no is not None:
                grp = ("q", o.queue) if o.is_dma else ("e", o.engine)
                if o.inc_no > g.get(grp, 0):
                    g[grp] = o.inc_no
            o.guar = g

        sems = {}
        self._sem_cms = []
        for e in ("tensor", "vector", "scalar", "gpsimd"):
            cm = self.nc.semaphore(f"sem_{e}")
            sems[("e", e)] = cm.__enter__()
            self._sem_cms.append(cm)
        for q in QUEUES:
            cm = self.nc.semaphore(f"q_{q}")
            sems[("q", q)] = cm.__enter__()
            self._sem_cms.append(cm)
        self.sems = sems

        per_engine = {e: [] for e in ENGINES}
        for o in self.ops:
            per_engine[o.engine].append(o)
        self.n_ops = {e: len(v) for e, v in per_engine.items()}

        def make_body(engine, ops):
            def body(eng):
                frontier = {}
                for o in ops:
                    for grp, v in sorted(o.waits.items()):
                        if frontier.get(grp, 0) >= v:
                            continue
                        eng.wait_ge(sems[grp], v)
                        frontier[grp] = v
                    inst = o.fn(eng)
                    if o.inc_no is not None:
                        step = 16 if o.is_dma else 1
                        grp = ("q", o.queue) if o.is_dma else ("e", o.engine)
                        inst.then_inc(sems[grp], step)
            return body

        final = dict(self.final_counts)

        def body_sync(eng):
            make_body("sync", per_engine["sync"])(eng)
            for grp, v in sorted(final.items()):
                if grp != ("q", "sync"):
                    eng.wait_ge(sems[grp], v)
            if ("q", "sync") in final:
                eng.wait_ge(sems[("q", "sync")], final[("q", "sync")])

        block.sync(body_sync)
        for e, mk in (("tensor", block.tensor), ("vector", block.vector),
                      ("scalar", block.scalar), ("gpsimd", block.gpsimd)):
            if per_engine[e]:
                mk(make_body(e, per_engine[e]))


class Arena:

    def __init__(self, nc, name, bytes_pp):
        self.t = nc.alloc_sbuf_tensor(name, [128, bytes_pp // 2], BF16)
        self.name = name
        self.bytes = bytes_pp
        self.pos = 0
        self.live = []
        self.dead = []
        self.n = 0

    def alloc(self, prog, nelem, dtype, tag=""):
        dsz = 4 if dtype == F32 else 2
        nbytes = (nelem * dsz + 63) // 64 * 64
        assert self.pos + nbytes <= self.bytes, (
            f"arena {self.name}: {self.pos}+{nbytes}>{self.bytes} ({tag})")
        start, end = self.pos, self.pos + nbytes
        self.pos = end
        key = (self.name, self.n, tag)
        self.n += 1
        olds = [k for (s, e, k) in self.dead if s < end and e > start]
        if olds:
            prog.alias(key, olds)
            self.dead = [(s, e, k) for (s, e, k) in self.dead
                         if not (start <= s and e <= end)]
        self.live.append((start, end, key))
        ap = self.t.ap()[:, start // 2: start // 2 + nelem * dsz // 2]
        if dtype == F32:
            ap = ap.bitcast(F32)
        return ap, key

    def reset(self):
        self.dead.extend(self.live)
        self.live = []
        self.pos = 0

    def mark(self):
        return self.pos

    def pop(self, mark):
        keep = [x for x in self.live if x[0] < mark]
        retire = [x for x in self.live if x[0] >= mark]
        self.dead.extend(retire)
        self.live = keep
        self.pos = mark


def ts(i, w):
    return slice(i * w, (i + 1) * w)



EPS_LN = 1e-5
EPS_RMS = 1.1920929e-07


class Ctx:
    def __init__(self, nc, prog):
        self.nc = nc
        self.prog = prog
        self.x_t = nc.alloc_sbuf_tensor("x_res", [128, NT * D], F32)
        self.wa_t = nc.alloc_sbuf_tensor("warena", [128, 32 * 1024], BF16)
        self.scr = Arena(nc, "scratch", 78 * 1024)
        self.pb_t = [nc.alloc_psum_tensor(f"pb{i}", [128, 1024], F32)
                     for i in range(4)]
        self.pb = [t.ap() for t in self.pb_t]
        self.pbk = [("pb", i) for i in range(4)]
        self.ident_t = nc.alloc_sbuf_tensor("ident", [128, 128], F32)
        self.ident = self.ident_t.ap()
        self.ones_t = nc.alloc_sbuf_tensor("ones_bf", [1, 128], BF16)
        self.ones = self.ones_t.ap()
        self.iota_t = nc.alloc_sbuf_tensor("iotas", [128, 260], F32)
        self.cpool_t = nc.alloc_sbuf_tensor("cpool", [128, 8], F32)
        self._consts = {}
        self._bank = 0
        self.w = {}
        self.wt = {}

    def x(self, t):
        return self.x_t.ap()[:, ts(t, D)]

    def xk(self, t):
        return ("x", t)

    def wa(self, slot, n=1):
        return self.wa_t.ap()[:, slot * 1024:(slot + n) * 1024]

    def wa_f32(self, slot, n=1):
        return self.wa_t.ap()[:, slot * 1024:(slot + n) * 1024].bitcast(F32)

    def wak(self, slot, n=1):
        return [("wa", s) for s in range(slot, slot + n)]

    def bank(self):
        i = self._bank % 4
        self._bank += 1
        return i

    def debug_dump(self, name, ap, reads, dtype=None):
        if not DEBUG:
            return
        nelem = 1
        shp = list(ap.shape)
        t = self.nc.dram_tensor("dbg_" + name, shp, ap.dtype,
                                kind="ExternalOutput")
        self.prog.dma("sync", out=t.ap(), in_=ap, reads=reads,
                      tag="dbg_" + name)

    def constf(self, v):
        if v not in self._consts:
            i = len(self._consts)
            assert i < 8
            ap = self.cpool_t.ap()[:, i:i + 1]
            self.prog.op("vector", lambda e, a=ap, vv=v: e.memset(a, vv),
                         writes=[("cpool",)], tag=f"const{i}")
            self._consts[v] = ap
        return self._consts[v]

    def emit_consts(self):
        p = self.prog
        p.dma("sync", out=self.ident, in_=self.w["ident_f32"],
              writes=[("ident",)], tag="ident")
        p.op("vector", lambda e: e.memset(self.ones, 1.0),
             writes=[("ones",)], tag="ones")
        it = self.iota_t.ap()
        p.op("gpsimd", lambda e: e.iota(
            it[:, 0:256], pattern=[[1, 256]], base=0, channel_multiplier=0,
            allow_small_or_imprecise_dtypes=True),
            writes=[("iota",)], tag="iota_row")
        p.op("gpsimd", lambda e: e.iota(
            it[:, 256:257], pattern=[[1, 1]], base=0, channel_multiplier=1,
            allow_small_or_imprecise_dtypes=True),
            writes=[("iota",)], tag="iota_c0")
        p.op("gpsimd", lambda e: e.iota(
            it[:, 257:258], pattern=[[1, 1]], base=128, channel_multiplier=1,
            allow_small_or_imprecise_dtypes=True),
            writes=[("iota",)], tag="iota_c1")

    @property
    def iota_row(self):
        return self.iota_t.ap()[:, 0:256]

    @property
    def iota_c0(self):
        return self.iota_t.ap()[:, 256:257]

    @property
    def iota_c1(self):
        return self.iota_t.ap()[:, 257:258]


def emit_ln(C, rms=False, nat=False, nat_ap=None, nat_keys=None):
    p, scr = C.prog, C.scr
    st, stk = scr.alloc(p, 128, F32, "ln_stats")
    s_sum, s_sq = st[:, 0:16], st[:, 16:32]
    mean, sval = st[:, 32:48], st[:, 48:64]
    t1, t2, t3 = st[:, 64:80], st[:, 80:96], st[:, 96:112]
    xnf = []
    for i in range(2):
        a, k = scr.alloc(p, D, F32, f"xnf{i}")
        xnf.append((a, k))
    xnT, xnTk = scr.alloc(p, 8 * SEQ, BF16, "xnT")
    out = {"xnT": xnT, "xnTk": xnTk}
    if nat:
        if nat_ap is not None:
            out["nat"] = nat_ap
            out["natks"] = nat_keys
        else:
            xnat, xnatk = scr.alloc(p, NT * D, BF16, "xn_nat")
            out["nat"] = xnat
            out["natks"] = [xnatk]

    for t in range(NT):
        xt = C.x(t)
        p.op("vector", lambda e, o=s_sum[:, t:t + 1], i_=xt:
             e.reduce_sum(o, i_, axis=AX.X),
             reads=[C.xk(t)], writes=[stk], tag=f"ln_sum{t}")
        dump = xnf[t % 2][0]
        p.op("scalar", lambda e, o=dump, i_=xt, a=s_sq[:, t:t + 1]:
             e.activation(o, i_, AF.Square, accum_out=a),
             reads=[C.xk(t)], writes=[stk, xnf[t % 2][1]], tag=f"ln_sq{t}")
    k = 1.0 / D
    p.op("vector", lambda e: e.tensor_scalar_mul(mean, s_sum, k),
         reads=[stk], writes=[stk], tag="ln_mean")
    p.op("vector", lambda e: e.tensor_scalar_mul(t1, s_sq, k),
         reads=[stk], writes=[stk], tag="ln_ex2")
    p.op("vector", lambda e: e.tensor_mul(t2, mean, mean),
         reads=[stk], writes=[stk], tag="ln_m2")
    p.op("vector", lambda e: e.tensor_sub(t3, t1, t2),
         reads=[stk], writes=[stk], tag="ln_var")
    p.op("scalar", lambda e, bb=C.constf(EPS_LN): e.activation(
        t1, t3, AF.Ln, bias=bb),
         reads=[stk, ("cpool",)], writes=[stk], tag="ln_lnv")
    p.op("scalar", lambda e: e.activation(sval, t1, AF.Exp, scale=-0.5),
         reads=[stk], writes=[stk], tag="ln_rstd")
    if rms:
        p.op("vector", lambda e: e.tensor_scalar_add(t2, t3, EPS_LN),
             reads=[stk], writes=[stk], tag="rms_vp")
        p.op("vector", lambda e: e.reciprocal(t1, t2),
             reads=[stk], writes=[stk], tag="rms_rv")
        p.op("vector", lambda e: e.tensor_mul(t2, t3, t1),
             reads=[stk], writes=[stk], tag="rms_q")
        p.op("scalar", lambda e, bb=C.constf(EPS_RMS): e.activation(
            t1, t2, AF.Ln, bias=bb),
             reads=[stk, ("cpool",)], writes=[stk], tag="rms_ln")
        p.op("scalar", lambda e: e.activation(t2, t1, AF.Exp, scale=-0.5),
             reads=[stk], writes=[stk], tag="rms_exp")
        p.op("vector", lambda e: e.tensor_mul(sval, sval, t2),
             reads=[stk], writes=[stk], tag="rms_sval")

    for t in range(NT):
        xf, xfk = xnf[t % 2]
        p.op("vector", lambda e, o=xf, i_=C.x(t), m=mean[:, t:t + 1],
             s=sval[:, t:t + 1]:
             e.tensor_scalar(o, i_, m, s, ALU.subtract, ALU.mult),
             reads=[C.xk(t), stk], writes=[xfk], tag=f"ln_xn{t}")
        if nat:
            natt = out["nat"][:, ts(t, D)]
            natw = (out["natks"] if nat_ap is None
                    else [out["natks"][t * D // 1024 + j]
                          for j in range((D + 1023) // 1024)])
            p.op("vector", lambda e, o=natt, i_=xf:
                 e.tensor_copy(o, i_),
                 reads=[xfk], writes=natw, tag=f"ln_nat{t}")
        for dc in range(8):
            bi = C.bank()
            p.op("tensor", lambda e, o=C.pb[bi][:, 0:128],
                 i_=xf[:, ts(dc, 128)]:
                 e.transpose(o, i_, C.ident),
                 reads=[xfk, ("ident",)], writes=[C.pbk[bi]],
                 tag=f"ln_tp{t}_{dc}")
            emit_copy(C, "scalar" if (dc % 2) else "vector",
                      xnT[:, dc * SEQ + t * 128: dc * SEQ + (t + 1) * 128],
                      C.pb[bi][:, 0:128], [C.pbk[bi]], [xnTk],
                      tag=f"ln_ev{t}_{dc}")
    return out


def emit_copy(C, engine, out, in_, reads, writes, tag=""):
    if engine == "scalar":
        C.prog.op("scalar", lambda e, o=out, i_=in_: e.copy(o, i_),
                  reads=reads, writes=writes, tag=tag)
    else:
        C.prog.op(engine, lambda e, o=out, i_=in_: e.tensor_copy(o, i_),
                  reads=reads, writes=writes, tag=tag)



def emit_l1ffn(C, b):
    p, scr, nc = C.prog, C.scr, C.nc
    scr.reset()
    ln = emit_ln(C, rms=False, nat=False)
    xnT, xnTk = ln["xnT"], ln["xnTk"]

    c_nat, c_natk = scr.alloc(p, NT * 16, F32, "c_nat")
    b1c, b1ck = scr.alloc(p, 256, F32, "b1cols")
    mark0 = scr.mark()

    rw, rwk = scr.alloc(p, 8 * 16, BF16, "rw")
    for dc in range(8):
        p.dma("gpsimd", out=rw[:, ts(dc, 16)],
              in_=C.w["L1_ffn_router_w"][ts(dc, 128), :],
              writes=[rwk], tag=f"rw{dc}")
    brow, browk = scr.alloc(p, 16, BF16, "brow")
    p.dma("gpsimd", out=brow[0:1, :], in_=C.w["rb_row"],
          writes=[browk], tag="brow")

    logits, logitsk = scr.alloc(p, NT * 16, F32, "logits")
    cT, cTk = scr.alloc(p, SEQ, BF16, "cT")
    tops, topsk = scr.alloc(p, 32, F32, "tops")
    top8, negm = tops[:, 0:8], tops[:, 8:9]
    s4c, rsc = tops[:, 9:10], tops[:, 10:11]
    prb, prbk = scr.alloc(p, 48, F32, "prb")
    probs_t, mask_t, pm_t = prb[:, 0:16], prb[:, 16:32], prb[:, 32:48]

    for t in range(NT):
        bi = C.bank()
        ps = C.pb[bi][:, 0:16]
        for dc in range(8):
            p.op("tensor", lambda e, o=ps,
                 l=xnT[:, dc * SEQ + t * 128: dc * SEQ + (t + 1) * 128],
                 r=rw[:, ts(dc, 16)], s=(dc == 0):
                 e.matmul(o, l, r, start=s, stop=False),
                 reads=[xnTk, rwk], writes=[C.pbk[bi]], tag=f"rlog{t}_{dc}")
        p.op("tensor", lambda e, o=ps: e.matmul(
            o, C.ones, brow[0:1, :], start=False, stop=True),
            reads=[("ones",), browk], writes=[C.pbk[bi]], tag=f"rbias{t}")
        p.op("vector", lambda e, o=logits[:, ts(t, 16)], i_=ps:
             e.tensor_copy(o, i_),
             reads=[C.pbk[bi]], writes=[logitsk], tag=f"rlogev{t}")
        lg = logits[:, ts(t, 16)]
        p.op("vector", lambda e, o=top8, i_=lg: e.max(out=o, in_=i_),
             reads=[logitsk], writes=[topsk], tag=f"rtop8{t}")
        p.op("vector", lambda e: e.tensor_scalar_mul(negm, top8[:, 0:1], -1.0),
             reads=[topsk], writes=[topsk], tag=f"rnegm{t}")
        p.op("scalar", lambda e, i_=lg: e.activation(
            probs_t, i_, AF.Exp, bias=negm, accum_out=s4c),
            reads=[logitsk, topsk], writes=[prbk, topsk], tag=f"rexp{t}")
        p.op("vector", lambda e, i_=lg: e.tensor_single_scalar(
            mask_t, i_, top8[:, 3:4], ALU.is_ge),
            reads=[logitsk, topsk], writes=[prbk], tag=f"rmask{t}")
        p.op("vector", lambda e: e.tensor_mul(pm_t, probs_t, mask_t),
             reads=[prbk], writes=[prbk], tag=f"rpm{t}")
        p.op("vector", lambda e: e.reduce_sum(s4c, pm_t, axis=AX.X),
             reads=[prbk], writes=[topsk], tag=f"rs4{t}")
        p.op("vector", lambda e: e.reciprocal(rsc, s4c),
             reads=[topsk], writes=[topsk], tag=f"rrs{t}")
        p.op("vector", lambda e, o=c_nat[:, ts(t, 16)]:
             e.tensor_scalar_mul(o, pm_t, rsc),
             reads=[prbk, topsk], writes=[c_natk], tag=f"rc{t}")
        bj = C.bank()
        p.op("tensor", lambda e, o=C.pb[bj][0:16, 0:128],
             i_=c_nat[:, ts(t, 16)]:
             e.transpose(o, i_, C.ident),
             reads=[c_natk, ("ident",)], writes=[C.pbk[bj]], tag=f"rcT{t}")
        emit_copy(C, "scalar", cT[0:16, t * 128:(t + 1) * 128],
                  C.pb[bj][0:16, 0:128], [C.pbk[bj]], [cTk], tag=f"rcTev{t}")

    if b == DEBUG_B:
        C.debug_dump("xnT", xnT, [xnTk])
        C.debug_dump("logits", logits, [logitsk])
        C.debug_dump("c_nat", c_nat, [c_natk])

    b2sb, b2k = scr.alloc(p, D, BF16, "b2")
    p.dma("gpsimd", out=b2sb[0:16, :], in_=C.w["L1_ffn_b2"],
          writes=[b2k], tag="b2")
    for t in range(NT):
        bi = C.bank()
        for dh in range(2):
            p.op("tensor", lambda e, o=C.pb[bi][:, ts(dh, 512)],
                 l=cT[0:16, ts(t, 128)], r=b2sb[0:16, ts(dh, 512)]:
                 e.matmul(o, l, r, start=True, stop=True),
                 reads=[cTk, b2k], writes=[C.pbk[bi]], tag=f"b2mm{t}_{dh}")
        for dh in range(2):
            p.op("vector", lambda e, o=C.x(t)[:, ts(dh, 512)],
                 i_=C.pb[bi][:, ts(dh, 512)]:
                 e.tensor_add(o, i_, o),
                 reads=[C.pbk[bi], C.xk(t)], writes=[C.xk(t)],
                 tag=f"b2add{t}_{dh}")

    if b == DEBUG_B:
        C.debug_dump("x_postb2", C.x_t.ap(), [C.xk(t) for t in range(NT)])

    t16a, t16ak = scr.alloc(p, 128, F32, "t16a")
    for e_ in range(16):
        p.dma("sync", out=t16a[0:16, :],
              in_=C.w["b1_resh"][ts(e_, 16), :],
              writes=[t16ak], tag=f"b1ld{e_}")
        bi = C.bank()
        p.op("tensor", lambda e, o=C.pb[bi][:, 0:16], i_=t16a[0:16, :]:
             e.transpose(o, i_, C.ident[0:16, 0:16]),
             reads=[t16ak, ("ident",)], writes=[C.pbk[bi]], tag=f"b1tp{e_}")
        p.op("vector", lambda e, o=b1c[:, ts(e_, 16)], i_=C.pb[bi][:, 0:16]:
             e.tensor_copy(o, i_),
             reads=[C.pbk[bi]], writes=[b1ck], tag=f"b1ev{e_}")

    scr.pop(mark0)
    hmid, _hk = scr.alloc(p, 16 * 1024, BF16, "hmid")
    hmidks = [_hk for _ in range(16)]
    w1h = C.w["L1_ffn_w1"]
    w2h = C.w["L1_ffn_w2"]
    for e_ in range(16):
        for dc in range(8):
            p.dma("gpsimd", out=C.wa(2 * dc, 2),
                  in_=w1h[e_, ts(dc, 128), :],
                  writes=C.wak(2 * dc, 2), tag=f"w1_{e_}_{dc}")
        for df in range(16):
            p.dma("gpsimd", out=C.wa(16 + df, 1),
                  in_=w2h[e_, ts(df, 128), :],
                  writes=C.wak(16 + df, 1), tag=f"w2_{e_}_{df}")
        for H in range(2):
            for df in range(16):
                bi = df % 2
                for dc in range(8):
                    for th in range(2):
                        p.op("tensor", lambda e, o=C.pb[bi][:, ts(th, 512)],
                             l=C.wa(2 * dc, 2)[:, ts(df, 128)],
                             r=xnT[:, dc * SEQ + H * 1024 + th * 512:
                                   dc * SEQ + H * 1024 + (th + 1) * 512],
                             s=(dc == 0), st=(dc == 7):
                             e.matmul(o, l, r, start=s, stop=st),
                             reads=C.wak(2 * dc, 2) + [xnTk],
                             writes=[C.pbk[bi]], tag=f"w1mm{e_}_{H}_{df}")
                p.op("scalar", lambda e, o=hmid[:, ts(df, 1024)],
                     i_=C.pb[bi][:, 0:1024], bb=b1c[:, e_ * 16 + df:
                                                    e_ * 16 + df + 1]:
                     e.activation(o, i_, AF.Gelu, bias=bb),
                     reads=[C.pbk[bi], b1ck], writes=[hmidks[df]],
                     tag=f"gelu{e_}_{H}_{df}")
            for t128 in range(8):
                tt = H * 8 + t128
                bj = 2 + (t128 % 2)
                for df in range(16):
                    for dh in range(2):
                        p.op("tensor", lambda e, o=C.pb[bj][:, ts(dh, 512)],
                             l=hmid[:, df * 1024 + t128 * 128:
                                    df * 1024 + (t128 + 1) * 128],
                             r=C.wa(16 + df, 1)[:, ts(dh, 512)],
                             s=(df == 0), st=(df == 15):
                             e.matmul(o, l, r, start=s, stop=st),
                             reads=[hmidks[df]] + C.wak(16 + df, 1),
                             writes=[C.pbk[bj]], tag=f"w2mm{e_}_{tt}_{df}")
                for dh in range(2):
                    p.op("vector", lambda e, o=C.x(tt)[:, ts(dh, 512)],
                         i_=C.pb[bj][:, ts(dh, 512)],
                         cc=c_nat[:, tt * 16 + e_: tt * 16 + e_ + 1]:
                         e.scalar_tensor_tensor(o, i_, cc, o,
                                                ALU.mult, ALU.add),
                         reads=[C.pbk[bj], C.xk(tt), c_natk],
                         writes=[C.xk(tt)], tag=f"stt{e_}_{tt}_{dh}")



PHASE_EMIT = {}
PHASE_EMIT["l1ffn"] = emit_l1ffn

PHASE_WEIGHTS = {
    "l0attn": ["L0_attn_c_attn_w", "cattn_b_resh", "L0_attn_c_proj_w",
               "cproj_b_resh", "L0_attn_wk_c", "L0_attn_wk_e",
               "L0_attn_wv_c", "L0_attn_wv_e"],
    "l0ffn": ["L0_ffn_router_w1", "L0_ffn_router_w2", "L0_ffn_w1",
              "L0_ffn_w2", "L0_ffn_w3"],
    "l1attn": ["Wqkv_perm", "bqkv_resh", "L1_attn_W_o", "bo_resh"],
    "l1ffn": ["L1_ffn_router_w", "rb_row", "L1_ffn_w1", "b1_resh",
              "L1_ffn_w2", "L1_ffn_b2"],
}

WEIGHT_SHAPES = {
    "L0_attn_c_attn_w": (1024, 3072), "cattn_b_resh": (24, 128),
    "L0_attn_c_proj_w": (1024, 1024), "cproj_b_resh": (8, 128),
    "L0_attn_wk_c": (64, 32), "L0_attn_wk_e": (32, 64),
    "L0_attn_wv_c": (64, 32), "L0_attn_wv_e": (32, 64),
    "L0_ffn_router_w1": (1024, 128), "L0_ffn_router_w2": (128, 16),
    "L0_ffn_w1": (16, 1024, 2048), "L0_ffn_w2": (16, 1024, 2048),
    "L0_ffn_w3": (16, 2048, 1024),
    "Wqkv_perm": (3072, 1024), "bqkv_resh": (24, 128),
    "L1_attn_W_o": (1024, 1024), "bo_resh": (8, 128),
    "L1_ffn_router_w": (1024, 16), "rb_row": (1, 16),
    "L1_ffn_w1": (16, 1024, 2048), "b1_resh": (256, 128),
    "L1_ffn_w2": (16, 2048, 1024), "L1_ffn_b2": (16, 1024),
}


def build_program(phases):
    nc = bass.Bass("TRN2", target_bir_lowering=False)
    x_in = nc.dram_tensor("x_in", [NB, SEQ, D], F32, kind="ExternalInput")
    y = nc.dram_tensor("y", [NB, SEQ, D], F32, kind="ExternalOutput")
    names = ["ident_f32"]
    for ph in phases:
        names += PHASE_WEIGHTS[ph]
    prog = Prog(nc)
    C = Ctx(nc, prog)
    C.w["ident_f32"] = nc.dram_tensor(
        "ident_f32", [128, 128], F32, kind="ExternalInput").ap()
    for n in names:
        if n == "ident_f32":
            continue
        C.w[n] = nc.dram_tensor(n, list(WEIGHT_SHAPES[n]), F32,
                                kind="ExternalInput").ap()
    C.emit_consts()
    p = prog
    for b in range(NB):
        src = x_in.ap()[b].rearrange("(c p) d -> p c d", p=128)
        dst_x = C.x_t.ap().rearrange("p (c d) -> p c d", d=D)
        p.dma("sync", out=dst_x, in_=src,
              writes=[C.xk(t) for t in range(NT)], tag=f"xload{b}")
        for ph in phases:
            PHASE_EMIT[ph](C, b)
        dst = y.ap()[b].rearrange("(c p) d -> p c d", p=128)
        src_x = C.x_t.ap().rearrange("p (c d) -> p c d", d=D)
        p.dma("sync", out=dst, in_=src_x,
              reads=[C.xk(t) for t in range(NT)], tag=f"xstore{b}")
        C.scr.reset()
    with nc.Block() as block:
        prog.finalize_and_emit(block)
    return nc, names, prog


def _host_prep(inputs):
    d = {k: np.asarray(v) for k, v in inputs.items()}
    arrs = {}
    arrs["ident_f32"] = np.eye(128, dtype=np.float32)
    f = np.float32
    H_, Dh = 16, 64
    old = np.arange(3 * H_ * Dh).reshape(H_, Dh, 3)
    perm = np.transpose(old, (2, 0, 1)).reshape(-1)
    arrs["Wqkv_perm"] = np.ascontiguousarray(
        d["L1_attn_W_qkv"].astype(f)[perm])
    arrs["bqkv_resh"] = np.ascontiguousarray(
        d["L1_attn_b_qkv"].astype(f)[perm].reshape(24, 128))
    arrs["cattn_b_resh"] = np.ascontiguousarray(
        d["L0_attn_c_attn_b"].astype(f).reshape(24, 128))
    arrs["cproj_b_resh"] = np.ascontiguousarray(
        d["L0_attn_c_proj_b"].astype(f).reshape(8, 128))
    arrs["bo_resh"] = np.ascontiguousarray(
        d["L1_attn_b_o"].astype(f).reshape(8, 128))
    arrs["b1_resh"] = np.ascontiguousarray(
        d["L1_ffn_b1"].astype(f).reshape(256, 128))
    arrs["rb_row"] = np.ascontiguousarray(
        d["L1_ffn_router_b"].astype(f).reshape(1, 16))
    for k, v in d.items():
        arrs.setdefault(k, np.ascontiguousarray(v.astype(f)))
    return arrs


PHASE_GROUPS = [["l0attn", "l0ffn", "l1attn", "l1ffn"]]

_prog_cache = {}


def _get_program(group):
    key = tuple(group)
    if key not in _prog_cache:
        _prog_cache[key] = build_program(list(group))
    return _prog_cache[key]


def kernel(**inputs):
    arrs = _host_prep(inputs)
    x = arrs["L0_attn_hidden_states"]
    shards = [np.ascontiguousarray(x[2 * c:2 * c + 2]) for c in range(8)]
    for group in PHASE_GROUPS:
        nc, names, _ = _get_program(group)
        in_maps = []
        for c in range(8):
            m = {"x_in": shards[c]}
            for n in names:
                m[n] = arrs[n]
            in_maps.append(m)
        res = run_bass_kernel_spmd(nc, in_maps, core_ids=list(range(8)))
        shards = [np.asarray(r["y"]) for r in res.results]
    return np.concatenate(shards, axis=0)



def emit_l1attn(C, b):
    p, scr = C.prog, C.scr
    scr.reset()
    ln = emit_ln(C, rms=True, nat=False)
    xnT, xnTk = ln["xnT"], ln["xnTk"]

    S_ZT, S_QT, S_KT, S_VN, S_PT = 0, 16, 18, 20, 22
    S_WQKV, S_WOT, S_WOSL = 26, 29, 30

    t24, t24k = scr.alloc(p, 128, F32, "t24")
    bqk, bqkk = scr.alloc(p, 32, F32, "bqkcols")
    bo, bok = scr.alloc(p, 8, F32, "bocols")
    p.dma("sync", out=t24[0:24, :], in_=C.w["bqkv_resh"], writes=[t24k],
          tag="bqkld")
    bi = C.bank()
    p.op("tensor", lambda e, o=C.pb[bi][:, 0:24]: e.transpose(
        o, t24[0:24, :], C.ident[0:24, 0:24]),
        reads=[t24k, ("ident",)], writes=[C.pbk[bi]], tag="bqktp")
    p.op("vector", lambda e, i_=C.pb[bi][:, 0:24]: e.tensor_copy(
        bqk[:, 0:24], i_), reads=[C.pbk[bi]], writes=[bqkk], tag="bqkev")
    p.dma("sync", out=t24[0:8, :], in_=C.w["bo_resh"], writes=[t24k],
          tag="bold")
    bi = C.bank()
    p.op("tensor", lambda e, o=C.pb[bi][:, 0:8]: e.transpose(
        o, t24[0:8, :], C.ident[0:8, 0:8]),
        reads=[t24k, ("ident",)], writes=[C.pbk[bi]], tag="botp")
    p.op("vector", lambda e, i_=C.pb[bi][:, 0:8]: e.tensor_copy(
        bo[:, 0:8], i_), reads=[C.pbk[bi]], writes=[bok], tag="boev")

    zero, zerok = scr.alloc(p, 512, F32, "maskzero")
    p.op("vector", lambda e: e.memset(zero, 0.0), writes=[zerok],
         tag="maskzero")
    masks, maskks = [], []
    for i in range(4):
        m, mk = scr.alloc(p, 512, F32, f"mask{i}")
        masks.append(m)
        maskks.append(mk)
        p.op("gpsimd", lambda e, o=m, dj=i * 128: e.affine_select(
            o, zero, pattern=[[1, 512]], compare_op=ALU.is_ge,
            fill=-1e30, base=-dj, channel_multiplier=-1),
            reads=[zerok], writes=[mk], tag=f"mask{i}")

    vf32, vf32k = scr.alloc(p, 2048, F32, "vf32")
    wqsl, wqslk = scr.alloc(p, 1024, F32, "wqslab")
    otmp, otmpk = scr.alloc(p, 2048, F32, "otmp")
    mredh, mredhk, vnsh, vnshk = [], [], [], []
    for h in range(4):
        mm, mk_ = scr.alloc(p, 16, F32, f"mred{h}")
        mredh.append(mm)
        mredhk.append(mk_)
        vv, vk_ = scr.alloc(p, 128, BF16, f"vns{h}")
        vnsh.append(vv)
        vnshk.append(vk_)

    qT, qTk = C.wa(S_QT, 2), [("wa", S_QT), ("wa", S_QT + 1)]
    kT, kTk = C.wa(S_KT, 2), [("wa", S_KT), ("wa", S_KT + 1)]
    vnat, vnatk = C.wa(S_VN, 2), [("wa", S_VN), ("wa", S_VN + 1)]
    pth = [C.wa(S_PT, 2), C.wa(S_PT + 2, 2)]
    pthk = [[("wa", S_PT), ("wa", S_PT + 1)],
            [("wa", S_PT + 2), ("wa", S_PT + 3)]]
    ztb = C.wa(S_ZT, 16)
    ztbk = [("wa", S_ZT + i) for i in range(16)]
    wqkvT = [C.wa(S_WQKV + i, 1) for i in range(3)]
    wqkvTk = [("wa", S_WQKV + i) for i in range(3)]
    wot, wotk = C.wa(S_WOT, 1), [("wa", S_WOT)]
    wosl, woslk = C.wa_f32(S_WOSL, 2), [("wa", S_WOSL), ("wa", S_WOSL + 1)]

    Wq = C.w["Wqkv_perm"]
    Wo = C.w["L1_attn_W_o"]

    for pair in range(8):
        for c in range(3):
            p.dma("sync", out=wqsl,
                  in_=Wq[c * 1024 + pair * 128:
                         c * 1024 + (pair + 1) * 128, :],
                  writes=[wqslk], tag=f"wq{pair}_{c}")
            for dc in range(8):
                bi = C.bank()
                p.op("tensor", lambda e, o=C.pb[bi][:, 0:128],
                     i_=wqsl[:, ts(dc, 128)]:
                     e.transpose(o, i_, C.ident),
                     reads=[wqslk, ("ident",)], writes=[C.pbk[bi]],
                     tag=f"wqtp{pair}_{c}_{dc}")
                emit_copy(C, "vector", wqkvT[c][:, ts(dc, 128)],
                          C.pb[bi][:, 0:128], [C.pbk[bi]], [wqkvTk[c]],
                          tag=f"wqev{pair}_{c}_{dc}")
        for c, outb, outk in ((0, qT, qTk), (1, kT, kTk),
                              (2, vf32, [vf32k])):
            for g in range(2):
                bi = C.bank()
                for dc in range(8):
                    for th in range(2):
                        cb = g * 1024 + th * 512
                        p.op("tensor",
                             lambda e, o=C.pb[bi][:, ts(th, 512)],
                             l=wqkvT[c][:, ts(dc, 128)],
                             r=xnT[:, dc * SEQ + cb: dc * SEQ + cb + 512],
                             s=(dc == 0), st=(dc == 7):
                             e.matmul(o, l, r, start=s, stop=st),
                             reads=[wqkvTk[c], xnTk], writes=[C.pbk[bi]],
                             tag=f"qkv{pair}_{c}_{g}")
                bb = bqk[:, c * 8 + pair: c * 8 + pair + 1]
                p.op("scalar", lambda e, o=outb[:, ts(g, 1024)],
                     i_=C.pb[bi][:, 0:1024], bb=bb:
                     e.activation(o, i_, AF.Identity, bias=bb),
                     reads=[C.pbk[bi], bqkk], writes=outk,
                     tag=f"qkvev{pair}_{c}_{g}")
        for kk in range(16):
            bi = C.bank()
            p.op("tensor", lambda e, o=C.pb[bi][:, 0:128],
                 i_=vf32[:, ts(kk, 128)]:
                 e.transpose(o, i_, C.ident),
                 reads=[vf32k, ("ident",)], writes=[C.pbk[bi]],
                 tag=f"vtp{pair}_{kk}")
            emit_copy(C, "vector", vnat[:, ts(kk, 128)],
                      C.pb[bi][:, 0:128], [C.pbk[bi]], vnatk,
                      tag=f"vev{pair}_{kk}")
        def l1_S_first(kb, h):
            hs = slice(h * 64, (h + 1) * 64)
            sp = C.pb[0] if h == 0 else C.pb[3]
            spk = C.pbk[0] if h == 0 else C.pbk[3]
            c0 = kb // 8
            delta = kb * 128 - c0 * 1024
            hb = (delta // 512) * 512
            for th in range(hb // 512, 2):
                p.op("tensor", lambda e, o=sp[:, ts(th, 512)],
                     l=kT[hs, ts(kb, 128)],
                     r=qT[hs, c0 * 1024 + th * 512:
                          c0 * 1024 + (th + 1) * 512]:
                     e.matmul(o, l, r, start=True, stop=True),
                     reads=kTk + qTk, writes=[spk],
                     tag=f"S{pair}_{h}_{kb}_{c0}_{th}")

        def l1_rest(kb, h):
            hs = slice(h * 64, (h + 1) * 64)
            sp = C.pb[0] if h == 0 else C.pb[3]
            spk = C.pbk[0] if h == 0 else C.pbk[3]
            hr = h * 2 + kb % 2
            mr = mredh[hr]
            mrk = mredhk[hr]
            pt = pth[h]
            ptk = pthk[h]
            c0 = kb // 8
            delta = kb * 128 - c0 * 1024
            nch = 2 - c0
            for ci, qc in enumerate(range(c0, 2)):
                hb = (delta // 512) * 512 if qc == c0 else 0
                lo = hb
                if qc != c0:
                    for th in range(2):
                        p.op("tensor", lambda e, o=sp[:, ts(th, 512)],
                             l=kT[hs, ts(kb, 128)],
                             r=qT[hs, qc * 1024 + th * 512:
                                  qc * 1024 + (th + 1) * 512]:
                             e.matmul(o, l, r, start=True, stop=True),
                             reads=kTk + qTk, writes=[spk],
                             tag=f"S{pair}_{h}_{kb}_{qc}_{th}")
                if qc == c0:
                    mi = (delta % 512) // 128
                    p.op("vector", lambda e,
                         o=sp[:, (delta // 512) * 512:
                              (delta // 512) * 512 + 512],
                         m=masks[mi]:
                         e.tensor_add(o, o, m),
                         reads=[spk, maskks[mi]], writes=[spk],
                         tag=f"Sm{pair}_{h}_{kb}")
                    p.op("vector", lambda e,
                         o=mr[:, ci:ci + 1],
                         i_=sp[:, (delta // 512) * 512:
                               (delta // 512) * 512 + 512]:
                         e.reduce_max(o, i_, axis=AX.X),
                         reads=[spk], writes=[mrk],
                         tag=f"Smx{pair}_{h}_{kb}_{qc}")
                    if hb == 0:
                        p.op("vector", lambda e,
                             o=mr[:, 4:5], i_=sp[:, 512:1024:4]:
                             e.reduce_max(o, i_, axis=AX.X),
                             reads=[spk], writes=[mrk],
                             tag=f"Smx2{pair}_{h}_{kb}_{qc}")
                        p.op("vector", lambda e, mr=mr, ci=ci:
                             e.tensor_max(mr[:, ci:ci + 1],
                                          mr[:, ci:ci + 1], mr[:, 4:5]),
                             reads=[mrk], writes=[mrk],
                             tag=f"Smxc{pair}_{h}_{kb}_{qc}")
                else:
                    p.op("vector", lambda e, o=mr[:, ci:ci + 1],
                         i_=sp[:, 0:1024:4]:
                         e.reduce_max(o, i_, axis=AX.X),
                         reads=[spk], writes=[mrk],
                         tag=f"Smx{pair}_{h}_{kb}_{qc}")
                p.op("vector", lambda e, mr=mr, ci=ci: e.tensor_scalar_mul(
                    mr[:, 5:6], mr[:, ci:ci + 1], -1.0),
                    reads=[mrk], writes=[mrk],
                    tag=f"Sng{pair}_{h}_{kb}_{qc}")
                p.op("scalar", lambda e,
                     o=pt[:, qc * 1024 + lo: (qc + 1) * 1024],
                     i_=sp[:, lo:1024], bb=mr[:, 5:6],
                     a=mr[:, 8 + ci:9 + ci]:
                     e.activation(o, i_, AF.Exp, bias=bb, accum_out=a),
                     reads=[spk, mrk], writes=ptk + [mrk],
                     tag=f"Sex{pair}_{h}_{kb}_{qc}")
            if nch == 2:
                p.op("vector", lambda e, mr=mr: e.tensor_max(
                    mr[:, 6:7], mr[:, 0:1], mr[:, 1:2]),
                    reads=[mrk], writes=[mrk], tag=f"Scm{pair}_{h}_{kb}")
                p.op("vector", lambda e, mr=mr: e.tensor_scalar_mul(
                    mr[:, 7:8], mr[:, 6:7], -1.0),
                    reads=[mrk], writes=[mrk], tag=f"Scn{pair}_{h}_{kb}")
                p.op("scalar", lambda e, mr=mr: e.activation(
                    mr[:, 10:12], mr[:, 0:2], AF.Exp, bias=mr[:, 7:8]),
                    reads=[mrk], writes=[mrk], tag=f"Sfc{pair}_{h}_{kb}")
                p.op("vector", lambda e, mr=mr: e.tensor_mul(
                    mr[:, 12:14], mr[:, 8:10], mr[:, 10:12]),
                    reads=[mrk], writes=[mrk], tag=f"Sds{pair}_{h}_{kb}")
                p.op("vector", lambda e, mr=mr: e.tensor_add(
                    mr[:, 14:15], mr[:, 12:13], mr[:, 13:14]),
                    reads=[mrk], writes=[mrk], tag=f"Sda{pair}_{h}_{kb}")
                p.op("vector", lambda e, mr=mr: e.reciprocal(
                    mr[:, 15:16], mr[:, 14:15]),
                    reads=[mrk], writes=[mrk], tag=f"Sdr{pair}_{h}_{kb}")
                p.op("vector", lambda e, mr=mr: e.tensor_scalar_mul(
                    mr[:, 10:12], mr[:, 10:12], mr[:, 15:16]),
                    reads=[mrk], writes=[mrk], tag=f"Ssc{pair}_{h}_{kb}")
            else:
                p.op("vector", lambda e, mr=mr: e.reciprocal(
                    mr[:, 11:12], mr[:, 8:9]),
                    reads=[mrk], writes=[mrk], tag=f"Sdr{pair}_{h}_{kb}")
            for ci, qc in enumerate(range(c0, 2)):
                sccol = mr[:, 10 + ci:11 + ci] if nch == 2 \
                    else mr[:, 11:12]
                p.op("vector", lambda e, o=vnsh[hr][:, ts(ci, 64)],
                     i_=vnat[:, kb * 128 + h * 64:
                             kb * 128 + h * 64 + 64], sc=sccol:
                     e.tensor_scalar_mul(o, i_, sc),
                     reads=vnatk + [mrk], writes=[vnshk[hr]],
                     tag=f"vs{pair}_{h}_{kb}_{qc}")
                zp = C.pb[1 + qc]
                zpk = C.pbk[1 + qc]
                hb_c = (delta // 512) * 512 if qc == c0 else 0
                for th in range(hb_c // 512, 2):
                    tpos = (0, 64) if h == 1 else None
                    p.op("tensor", lambda e,
                         o=zp[hs, ts(th, 512)],
                         l=vnsh[hr][:, ts(ci, 64)],
                         r=pt[:, qc * 1024 + th * 512:
                              qc * 1024 + (th + 1) * 512],
                         s=(kb == 0), stt=(kb == 8 * qc + 4 * th + 3),
                         tp=tpos:
                         e.matmul(o, l, r, start=s, stop=stt,
                                  tile_position=tp),
                         reads=[vnshk[hr]] + ptk, writes=[zpk],
                         tag=f"PV{pair}_{h}_{kb}_{qc}_{th}")

        prev = None
        for kb in range(16):
            for h in range(2):
                l1_S_first(kb, h)
                if prev is not None:
                    l1_rest(*prev)
                prev = (kb, h)
        l1_rest(*prev)
        for h in range(2):
            hs = slice(h * 64, (h + 1) * 64)
            for half in range(2):
                p.op("vector", lambda e,
                     o=ztb[hs, pair * 2048 + half * 1024:
                           pair * 2048 + half * 1024 + 1024],
                     i_=C.pb[1 + half][hs, 0:1024]:
                     e.tensor_copy(o, i_),
                     reads=[C.pbk[1 + half]], writes=ztbk,
                     tag=f"zev{pair}_{h}_{half}")
    if b == DEBUG_B:
        C.debug_dump("ztb_l1", ztb, ztbk)

    for fc in range(8):
        p.dma("sync", out=wosl, in_=Wo[ts(fc, 128), :],
              writes=woslk, tag=f"wo{fc}")
        for zc in range(8):
            bi = C.bank()
            p.op("tensor", lambda e, o=C.pb[bi][:, 0:128],
                 i_=wosl[:, ts(zc, 128)]:
                 e.transpose(o, i_, C.ident),
                 reads=woslk + [("ident",)], writes=[C.pbk[bi]],
                 tag=f"wotp{fc}_{zc}")
            emit_copy(C, "vector", wot[:, ts(zc, 128)],
                      C.pb[bi][:, 0:128], [C.pbk[bi]], wotk,
                      tag=f"wotev{fc}_{zc}")
        po = [C.pb[2 * (fc % 2)], C.pb[2 * (fc % 2) + 1]]
        pok = [C.pbk[2 * (fc % 2)], C.pbk[2 * (fc % 2) + 1]]
        for zc in range(8):
            for th in range(4):
                p.op("tensor", lambda e,
                     o=po[th // 2][:, (th % 2) * 512:(th % 2) * 512 + 512],
                     l=wot[:, ts(zc, 128)],
                     r=ztb[:, zc * 2048 + th * 512: zc * 2048 + (th + 1) * 512],
                     s=(zc == 0), st=(zc == 7):
                     e.matmul(o, l, r, start=s, stop=st),
                     reads=wotk + ztbk, writes=[pok[th // 2]],
                     tag=f"omm{fc}_{zc}_{th}")
        for half in range(2):
            p.op("scalar", lambda e, o=otmp[:, ts(half, 1024)],
                 i_=po[half][:, 0:1024], bb=bo[:, fc:fc + 1]:
                 e.activation(o, i_, AF.Identity, bias=bb),
                 reads=[pok[half], bok], writes=[otmpk],
                 tag=f"oev{fc}_{half}")
        for tk in range(16):
            bi = 2 * ((fc + 1) % 2) + (tk % 2)
            p.op("tensor", lambda e, o=C.pb[bi][:, 0:128],
                 i_=otmp[:, ts(tk, 128)]:
                 e.transpose(o, i_, C.ident),
                 reads=[otmpk, ("ident",)], writes=[C.pbk[bi]],
                 tag=f"otp{fc}_{tk}")
            p.op("vector", lambda e, o=C.x(tk)[:, ts(fc, 128)],
                 i_=C.pb[bi][:, 0:128]:
                 e.tensor_add(o, i_, o),
                 reads=[C.pbk[bi], C.xk(tk)], writes=[C.xk(tk)],
                 tag=f"oadd{fc}_{tk}")


PHASE_EMIT["l1attn"] = emit_l1attn


def emit_l0attn(C, b):
    p, scr = C.prog, C.scr
    scr.reset()
    ln = emit_ln(C, rms=False, nat=False)
    xnT, xnTk = ln["xnT"], ln["xnTk"]

    S_ZT, S_QT, S_KT, S_VR, S_VN, S_PT, S_WQKV, S_CP = \
        0, 16, 18, 20, 22, 24, 26, 29

    t24, t24k = scr.alloc(p, 128, F32, "t24")
    bqk, bqkk = scr.alloc(p, 40, F32, "bqkcols")
    bpr, bprk = scr.alloc(p, 8, F32, "bprcols")
    p.dma("sync", out=t24[0:24, :], in_=C.w["cattn_b_resh"], writes=[t24k],
          tag="battld")
    bi = C.bank()
    p.op("tensor", lambda e, o=C.pb[bi][:, 0:24]: e.transpose(
        o, t24[0:24, :], C.ident[0:24, 0:24]),
        reads=[t24k, ("ident",)], writes=[C.pbk[bi]], tag="battp")
    p.op("vector", lambda e, i_=C.pb[bi][:, 0:24]: e.tensor_copy(
        bqk[:, 0:24], i_), reads=[C.pbk[bi]], writes=[bqkk], tag="battev")
    p.op("vector", lambda e: e.tensor_scalar_mul(
        bqk[:, 24:32], bqk[:, 0:8], 0.125),
        reads=[bqkk], writes=[bqkk], tag="bqscale")
    p.dma("sync", out=t24[0:8, :], in_=C.w["cproj_b_resh"], writes=[t24k],
          tag="bprld")
    bi = C.bank()
    p.op("tensor", lambda e, o=C.pb[bi][:, 0:8]: e.transpose(
        o, t24[0:8, :], C.ident[0:8, 0:8]),
        reads=[t24k, ("ident",)], writes=[C.pbk[bi]], tag="bprtp")
    p.op("vector", lambda e, i_=C.pb[bi][:, 0:8]: e.tensor_copy(
        bpr[:, 0:8], i_), reads=[C.pbk[bi]], writes=[bprk], tag="bprev")

    lat, latk = scr.alloc(p, 4 * 64, BF16, "latw")
    wkc, wke = lat[:, 0:32], lat[0:32, 32:96]
    wvc, wve = lat[:, 96:128], lat[0:32, 128:192]
    p.dma("gpsimd", out=lat[0:64, 0:32], in_=C.w["L0_attn_wk_c"],
          writes=[latk], tag="wkc")
    p.dma("gpsimd", out=lat[64:128, 0:32], in_=C.w["L0_attn_wk_c"],
          writes=[latk], tag="wkc2")
    p.dma("gpsimd", out=lat[0:32, 32:96], in_=C.w["L0_attn_wk_e"],
          writes=[latk], tag="wke")
    p.dma("gpsimd", out=lat[0:64, 96:128], in_=C.w["L0_attn_wv_c"],
          writes=[latk], tag="wvc")
    p.dma("gpsimd", out=lat[64:128, 96:128], in_=C.w["L0_attn_wv_c"],
          writes=[latk], tag="wvc2")
    p.dma("gpsimd", out=lat[0:32, 128:192], in_=C.w["L0_attn_wv_e"],
          writes=[latk], tag="wve")

    zero, zerok = scr.alloc(p, 128, F32, "maskzero")
    p.op("vector", lambda e: e.memset(zero, 0.0), writes=[zerok],
         tag="l0maskzero")
    dmask, dmaskk = scr.alloc(p, 128, F32, "l0diagmask")
    p.op("gpsimd", lambda e: e.affine_select(
        dmask, zero, pattern=[[1, 128]], compare_op=ALU.is_ge,
        fill=-1e30, base=0, channel_multiplier=-1),
        reads=[zerok], writes=[dmaskk], tag="l0dmask")

    vf32, vf32k = scr.alloc(p, 2048, F32, "vf32")
    klatb, klatbk = scr.alloc(p, 2 * 512, BF16, "klat")
    otmp, otmpk = scr.alloc(p, 2048, F32, "otmp")
    mred, mredk = scr.alloc(p, 16, F32, "mred")
    znat, znatk = scr.alloc(p, 1024, F32, "znat")
    mrr, mrrk, vaur, vaurk = [], [], [], []
    for i in range(4):
        a_, k_ = scr.alloc(p, 8, F32, f"mrr{i}")
        mrr.append(a_)
        mrrk.append(k_)
        a_, k_ = scr.alloc(p, 80, BF16, f"vaur{i}")
        vaur.append(a_)
        vaurk.append(k_)

    qT, qTk = C.wa(S_QT, 2), [("wa", S_QT), ("wa", S_QT + 1)]
    kT, kTk = C.wa(S_KT, 2), [("wa", S_KT), ("wa", S_KT + 1)]
    vraw, vrawk = C.wa(S_VR, 2), [("wa", S_VR), ("wa", S_VR + 1)]
    vnat, vnatk = C.wa(S_VN, 2), [("wa", S_VN), ("wa", S_VN + 1)]
    ptr = [C.wa(S_PT, 1), C.wa(S_PT + 1, 1)]
    ptrk = [("wa", S_PT), ("wa", S_PT + 1)]
    wqkvT = [C.wa(S_WQKV + i, 1) for i in range(3)]
    wqkvTk = [("wa", S_WQKV + i) for i in range(3)]
    cpr, cprk = [C.wa(S_CP, 1), C.wa(S_CP + 1, 1)], \
        [("wa", S_CP), ("wa", S_CP + 1)]

    CW = C.w["L0_attn_c_attn_w"]
    CP = C.w["L0_attn_c_proj_w"]
    ztb = C.wa(S_ZT, 16)
    ztbk = [("wa", S_ZT + i) for i in range(16)]

    for pair in range(8):
        for c in range(3):
            for dc in range(8):
                p.dma("gpsimd", out=wqkvT[c][:, ts(dc, 128)],
                      in_=CW[ts(dc, 128), c * 1024 + pair * 128:
                             c * 1024 + (pair + 1) * 128],
                      writes=[wqkvTk[c]], tag=f"cw{pair}_{c}_{dc}")
        for c, outb, outk in ((0, qT, qTk), (1, kT, kTk),
                              (2, vraw, vrawk)):
            for g in range(2):
                bi = C.bank()
                for dc in range(8):
                    for th in range(2):
                        cb = g * 1024 + th * 512
                        p.op("tensor",
                             lambda e, o=C.pb[bi][:, ts(th, 512)],
                             l=wqkvT[c][:, ts(dc, 128)],
                             r=xnT[:, dc * SEQ + cb: dc * SEQ + cb + 512],
                             s=(dc == 0), st=(dc == 7):
                             e.matmul(o, l, r, start=s, stop=st),
                             reads=[wqkvTk[c], xnTk], writes=[C.pbk[bi]],
                             tag=f"aqkv{pair}_{c}_{g}")
                if c == 0:
                    bb = bqk[:, 24 + pair: 25 + pair]
                    sc = 0.125
                else:
                    bb = bqk[:, c * 8 + pair: c * 8 + pair + 1]
                    sc = 1.0
                p.op("scalar", lambda e, o=outb[:, ts(g, 1024)],
                     i_=C.pb[bi][:, 0:1024], bb=bb, sc=sc:
                     e.activation(o, i_, AF.Identity, bias=bb, scale=sc),
                     reads=[C.pbk[bi], bqkk], writes=outk,
                     tag=f"aqkvev{pair}_{c}_{g}")
        for g in range(4):
            gsl = slice(g * 512, (g + 1) * 512)
            for src_, wc, we, dstk, dst in ((kT, wkc, wke, kTk, kT),
                                            (vraw, wvc, wve, None, None)):
                for h in range(2):
                    hs = slice(h * 64, (h + 1) * 64)
                    bi = C.bank()
                    p.op("tensor", lambda e, o=C.pb[bi][0:32, 0:512],
                         l=wc[hs, :], r=src_[hs, gsl]:
                         e.matmul(o, l, r, start=True, stop=True),
                         reads=(kTk if src_ is kT else vrawk) + [latk],
                         writes=[C.pbk[bi]], tag=f"lat{pair}_{g}_{h}")
                    p.op("vector", lambda e, o=klatb[0:32, ts(h, 512)],
                         i_=C.pb[bi][0:32, 0:512]:
                         e.tensor_copy(o, i_),
                         reads=[C.pbk[bi]], writes=[klatbk],
                         tag=f"latev{pair}_{g}_{h}")
                for h in range(2):
                    hs = slice(h * 64, (h + 1) * 64)
                    bi = C.bank()
                    p.op("tensor", lambda e, o=C.pb[bi][0:64, 0:512],
                         l=we, r=klatb[0:32, ts(h, 512)]:
                         e.matmul(o, l, r, start=True, stop=True),
                         reads=[klatbk, latk], writes=[C.pbk[bi]],
                         tag=f"exp{pair}_{g}_{h}")
                    if src_ is kT:
                        p.op("vector", lambda e, o=kT[hs, gsl],
                             i_=C.pb[bi][0:64, 0:512]:
                             e.tensor_copy(o, i_),
                             reads=[C.pbk[bi]], writes=kTk,
                             tag=f"expev{pair}_{g}_{h}")
                    else:
                        p.op("vector", lambda e, o=vf32[hs, gsl],
                             i_=C.pb[bi][0:64, 0:512]:
                             e.tensor_copy(o, i_),
                             reads=[C.pbk[bi]], writes=[vf32k],
                             tag=f"vexpev{pair}_{g}_{h}")
        for kk in range(16):
            bi = C.bank()
            p.op("tensor", lambda e, o=C.pb[bi][:, 0:128],
                 i_=vf32[:, ts(kk, 128)]:
                 e.transpose(o, i_, C.ident),
                 reads=[vf32k, ("ident",)], writes=[C.pbk[bi]],
                 tag=f"avtp{pair}_{kk}")
            emit_copy(C, "vector", vnat[:, ts(kk, 128)],
                      C.pb[bi][:, 0:128], [C.pbk[bi]], vnatk,
                      tag=f"avev{pair}_{kk}")
        for h in range(2):
            hs = slice(h * 64, (h + 1) * 64)
            for zi2 in (1, 2):
                p.op("vector", lambda e, o=C.pb[zi2][:, 0:1024]:
                     e.memset(o, 0.0),
                     writes=[C.pbk[zi2]], tag=f"azclr{pair}_{h}_{zi2}")
            tasks = []
            ti = 0
            for kb in range(16):
                c0 = kb // 8
                delta = kb * 128 - c0 * 1024
                for qc in range(c0, 2):
                    on_diag = (qc == c0)
                    hb = (delta // 512) * 512 if on_diag else 0
                    tasks.append(dict(kb=kb, qc=qc, od=on_diag, hb=hb,
                                      lo=(delta if on_diag else 0),
                                      delta=delta,
                                      si=(0 if ti % 2 == 0 else 3),
                                      ri=ti))
                    ti += 1

            def emit_S(t):
                sp, spk = C.pb[t["si"]], C.pbk[t["si"]]
                for th in range(t["hb"] // 512, 2):
                    p.op("tensor", lambda e, o=sp[:, ts(th, 512)],
                         l=kT[hs, ts(t["kb"], 128)],
                         r=qT[hs, t["qc"] * 1024 + th * 512:
                              t["qc"] * 1024 + (th + 1) * 512]:
                         e.matmul(o, l, r, start=True, stop=True),
                         reads=kTk + qTk, writes=[spk],
                         tag=f"aS{pair}_{h}_{t['kb']}_{t['qc']}_{th}")

            def emit_rest(t):
                kb, qc, delta, lo = t["kb"], t["qc"], t["delta"], t["lo"]
                sp, spk = C.pb[t["si"]], C.pbk[t["si"]]
                ri = t["ri"]
                mr, mrk = mrr[ri % 4], mrrk[ri % 4]
                va, vak = vaur[ri % 4], vaurk[ri % 4]
                pt, ptk = ptr[ri % 2], ptrk[ri % 2]
                if t["od"]:
                    p.op("vector", lambda e, mr=mr, o=sp[:, delta:delta + 128]:
                         e.reduce_max(mr[:, 0:1], o, axis=AX.X, negate=True),
                         reads=[spk], writes=[mrk],
                         tag=f"aSmx{pair}_{h}_{kb}_{qc}")
                    if delta + 128 < 1024:
                        p.op("vector", lambda e, mr=mr,
                             i_=sp[:, delta + 128:1024:4]:
                             e.reduce_max(mr[:, 1:2], i_, axis=AX.X,
                                          negate=True),
                             reads=[spk], writes=[mrk],
                             tag=f"aSmx2{pair}_{h}_{kb}_{qc}")
                        p.op("vector", lambda e, mr=mr: e.tensor_tensor(
                            mr[:, 2:3], mr[:, 0:1], mr[:, 1:2], ALU.min),
                            reads=[mrk], writes=[mrk],
                            tag=f"aSmc{pair}_{h}_{kb}_{qc}")
                        ng = 2
                    else:
                        ng = 0
                    p.op("vector", lambda e, o=sp[:, delta:delta + 128]:
                         e.tensor_add(o, o, dmask),
                         reads=[spk, dmaskk], writes=[spk],
                         tag=f"aSm{pair}_{h}_{kb}")
                else:
                    p.op("vector", lambda e, mr=mr, i_=sp[:, 0:1024:4]:
                         e.reduce_max(mr[:, 0:1], i_, axis=AX.X, negate=True),
                         reads=[spk], writes=[mrk],
                         tag=f"aSmx{pair}_{h}_{kb}_{qc}")
                    ng = 0
                negm = mr[:, ng:ng + 1]
                p.op("scalar", lambda e,
                     o=pt[:, lo:1024],
                     i_=sp[:, lo:1024], bb=negm:
                     e.activation(o, i_, AF.Exp, bias=bb),
                     reads=[spk, mrk], writes=[ptk],
                     tag=f"aSex{pair}_{h}_{kb}_{qc}")
                p.op("scalar", lambda e, mr=mr, i_=negm:
                     e.activation(mr[:, 4:5], i_, AF.Exp, scale=-1.0),
                     reads=[mrk], writes=[mrk],
                     tag=f"aSem{pair}_{h}_{kb}_{qc}")
                p.op("vector", lambda e, o=va[:, 64:65], mr=mr:
                     e.tensor_copy(o, mr[:, 4:5]),
                     reads=[mrk], writes=[vak],
                     tag=f"ave{pair}_{h}_{kb}_{qc}")
                p.op("vector", lambda e, o=va[:, 0:64],
                     i_=vnat[:, kb * 128 + h * 64:
                             kb * 128 + h * 64 + 64], mr=mr:
                     e.tensor_scalar_mul(o, i_, mr[:, 4:5]),
                     reads=vnatk + [mrk], writes=[vak],
                     tag=f"avs{pair}_{h}_{kb}_{qc}")
                for qt in range(max(kb, qc * 8), qc * 8 + 8):
                    zi = 1 if qt < 8 else 2
                    zcol = (qt % 4) * 128 + ((qt % 8) // 4) * 512
                    p.op("tensor", lambda e,
                         o=C.pb[zi][:, zcol:zcol + 65],
                         l=pt[:, (qt - qc * 8) * 128:
                               (qt - qc * 8 + 1) * 128],
                         r=va[:, 0:65]:
                         e.matmul(o, l, r, start=False, stop=False,
                                  skip_group_check=True),
                         reads=[ptk, vak], writes=[C.pbk[zi]],
                         tag=f"aPV{pair}_{h}_{kb}_{qt}")

            prev = None
            for t in tasks:
                emit_S(t)
                if prev is not None:
                    emit_rest(prev)
                prev = t
            emit_rest(prev)
            for qt in range(16):
                zi = 1 if qt < 8 else 2
                zcol = (qt % 4) * 128 + ((qt % 8) // 4) * 512
                p.op("vector", lambda e, i_=C.pb[zi][:, zcol + 64:zcol + 65]:
                     e.reciprocal(mred[:, 4:5], i_),
                     reads=[C.pbk[zi]], writes=[mredk],
                     tag=f"azr{pair}_{h}_{qt}")
                p.op("vector", lambda e, o=znat[:, ts(qt, 64)],
                     i_=C.pb[zi][:, zcol:zcol + 64]:
                     e.tensor_scalar_mul(o, i_, mred[:, 4:5]),
                     reads=[C.pbk[zi], mredk], writes=[znatk],
                     tag=f"azn{pair}_{h}_{qt}")
            for qt in range(16):
                si = 0 if (qt % 2 == 0) else 3
                p.op("tensor", lambda e, o=C.pb[si][0:64, 0:128],
                     i_=znat[:, ts(qt, 64)]:
                     e.transpose(o, i_, C.ident),
                     reads=[znatk, ("ident",)], writes=[C.pbk[si]],
                     tag=f"aztp{pair}_{h}_{qt}")
                p.op("vector", lambda e,
                     o=ztb[hs, pair * 2048 + qt * 128:
                           pair * 2048 + (qt + 1) * 128],
                     i_=C.pb[si][0:64, 0:128]:
                     e.tensor_copy(o, i_),
                     reads=[C.pbk[si]],
                     writes=[("wa", S_ZT + pair * 2 + qt // 8)],
                     tag=f"azev{pair}_{h}_{qt}")

    if b == DEBUG_B:
        C.debug_dump("ztb_l0", ztb, ztbk)
    for fc in range(8):
        po = [C.pb[2 * (fc % 2)], C.pb[2 * (fc % 2) + 1]]
        pok = [C.pbk[2 * (fc % 2)], C.pbk[2 * (fc % 2) + 1]]
        for zc in range(8):
            p.dma("gpsimd", out=cpr[zc % 2][:, 0:128],
                  in_=CP[ts(zc, 128), ts(fc, 128)],
                  writes=[cprk[zc % 2]], tag=f"acp{fc}_{zc}")
            for th in range(4):
                p.op("tensor", lambda e,
                     o=po[th // 2][:, (th % 2) * 512:(th % 2) * 512 + 512],
                     l=cpr[zc % 2][:, 0:128],
                     r=ztb[:, zc * 2048 + th * 512: zc * 2048 + (th + 1) * 512],
                     s=(zc == 0), st=(zc == 7):
                     e.matmul(o, l, r, start=s, stop=st),
                     reads=[cprk[zc % 2]] + ztbk, writes=[pok[th // 2]],
                     tag=f"aomm{fc}_{zc}_{th}")
        for half in range(2):
            p.op("scalar", lambda e, o=otmp[:, ts(half, 1024)],
                 i_=po[half][:, 0:1024], bb=bpr[:, fc:fc + 1]:
                 e.activation(o, i_, AF.Identity, bias=bb),
                 reads=[pok[half], bprk], writes=[otmpk],
                 tag=f"aoev{fc}_{half}")
        for tk in range(16):
            bi = 2 * ((fc + 1) % 2) + (tk % 2)
            p.op("tensor", lambda e, o=C.pb[bi][:, 0:128],
                 i_=otmp[:, ts(tk, 128)]:
                 e.transpose(o, i_, C.ident),
                 reads=[otmpk, ("ident",)], writes=[C.pbk[bi]],
                 tag=f"aotp{fc}_{tk}")
            p.op("vector", lambda e, o=C.x(tk)[:, ts(fc, 128)],
                 i_=C.pb[bi][:, 0:128]:
                 e.tensor_add(o, i_, o),
                 reads=[C.pbk[bi], C.xk(tk)], writes=[C.xk(tk)],
                 tag=f"aoadd{fc}_{tk}")


PHASE_EMIT["l0attn"] = emit_l0attn


def emit_l0ffn(C, b):
    p, scr = C.prog, C.scr
    scr.reset()

    S_NAT, S_SEL, S_SELT, S_EON, S_W12, S_W3 = 0, 16, 20, 24, 26, 30
    natk = [("wa", S_NAT + i) for i in range(16)]
    nat = C.wa(S_NAT, 16)
    selb = [C.wa(S_SEL, 2), C.wa(S_SEL + 2, 2)]
    selbk = [[("wa", S_SEL), ("wa", S_SEL + 1)],
             [("wa", S_SEL + 2), ("wa", S_SEL + 3)]]
    selT = C.wa(S_SELT, 4)
    selTk = [("wa", S_SELT + i) for i in range(4)]
    eon = [C.wa(S_EON, 1), C.wa(S_EON + 1, 1)]
    eonk = [("wa", S_EON), ("wa", S_EON + 1)]
    w12 = [C.wa(S_W12 + i, 1) for i in range(4)]
    w12k = [("wa", S_W12 + i) for i in range(4)]
    w3r = [C.wa(S_W3, 1), C.wa(S_W3 + 1, 1)]
    w3rk = [("wa", S_W3), ("wa", S_W3 + 1)]

    pk3, pk3k = scr.alloc(p, 2048, F32, "pack3")
    rbc, rbck = scr.alloc(p, 2048, F32, "rank_bcast")
    rkc, rkck = scr.alloc(p, 256, F32, "rank_cols")
    wcl, wclk = scr.alloc(p, 256, BF16, "w_cols")
    ewc, ewck = scr.alloc(p, 64, F32, "ew_top8")
    mark0 = scr.mark()
    pk1, pk1k = scr.alloc(p, 2048, F32, "pack1")
    pk2, pk2k = scr.alloc(p, 2048, F32, "pack2")

    logits = pk1[0:16, :]
    zap = pk2[0:16, :]
    cum3 = pk3[0:16, :]
    top8 = ewc[0:16, 0:8]
    ew2 = ewc[:, 16:18]

    ln = emit_ln(C, rms=False, nat=True, nat_ap=nat, nat_keys=natk)
    xnT, xnTk = ln["xnT"], ln["xnTk"]

    rw1, rw1k = scr.alloc(p, 1024, BF16, "rw1")
    rw2, rw2k = scr.alloc(p, 16, BF16, "rw2")
    r1T, r1Tk = scr.alloc(p, 512, BF16, "r1T")
    for dc in range(8):
        p.dma("gpsimd", out=rw1[:, ts(dc, 128)],
              in_=C.w["L0_ffn_router_w1"][ts(dc, 128), :],
              writes=[rw1k], tag=f"rw1{dc}")
    p.dma("gpsimd", out=rw2, in_=C.w["L0_ffn_router_w2"],
          writes=[rw2k], tag="rw2")
    for tc in range(4):
        bi = C.bank()
        for dc in range(8):
            p.op("tensor", lambda e, o=C.pb[bi][:, 0:512],
                 l=rw1[:, ts(dc, 128)],
                 r=xnT[:, dc * SEQ + tc * 512: dc * SEQ + (tc + 1) * 512],
                 s=(dc == 0), st=(dc == 7):
                 e.matmul(o, l, r, start=s, stop=st),
                 reads=[rw1k, xnTk], writes=[C.pbk[bi]], tag=f"r1{tc}")
        p.op("scalar", lambda e, o=r1T,
             i_=C.pb[bi][:, 0:512]:
             e.activation(o, i_, AF.Silu),
             reads=[C.pbk[bi]], writes=[r1Tk], tag=f"r1ev{tc}")
        bj = C.bank()
        p.op("tensor", lambda e, o=C.pb[bj][0:16, 0:512],
             l=rw2, r=r1T:
             e.matmul(o, l, r, start=True, stop=True),
             reads=[rw2k, r1Tk], writes=[C.pbk[bj]], tag=f"r2{tc}")
        p.op("vector", lambda e, o=logits[:, ts(tc, 512)],
             i_=C.pb[bj][0:16, 0:512]:
             e.tensor_copy(o, i_),
             reads=[C.pbk[bj]], writes=[pk1k], tag=f"r2ev{tc}")

    p.op("vector", lambda e: e.reduce_max(
        ewc[0:16, 8:9], logits, axis=AX.X, negate=True),
        reads=[pk1k], writes=[ewck], tag="smax")
    p.op("scalar", lambda e: e.activation(
        logits, logits, AF.Exp, bias=ewc[0:16, 8:9],
        accum_out=ewc[0:16, 9:10]),
        reads=[pk1k, ewck], writes=[pk1k, ewck], tag="sexp")
    p.op("vector", lambda e: e.reciprocal(ewc[0:16, 10:11],
                                          ewc[0:16, 9:10]),
         reads=[ewck], writes=[ewck], tag="srec")
    p.op("vector", lambda e: e.tensor_scalar_mul(
        logits, logits, ewc[0:16, 10:11]),
        reads=[pk1k, ewck], writes=[pk1k], tag="snorm")
    p.op("vector", lambda e: e.tensor_copy(zap, logits),
         reads=[pk1k], writes=[pk2k], tag="zap0")
    for r in range(32):
        p.op("vector", lambda e: e.max(out=ewc[0:16, 0:8], in_=zap),
             reads=[pk2k], writes=[ewck], tag=f"top8_{r}")
        p.op("vector", lambda e: e.match_replace(
            out=zap, in_to_replace=ewc[0:16, 0:8], in_values=zap,
            imm_value=0.0),
            reads=[pk2k, ewck], writes=[pk2k], tag=f"zap_{r}")
    p.op("vector", lambda e: e.tensor_sub(logits, logits, zap),
         reads=[pk1k, pk2k], writes=[pk1k], tag="psel")
    p.op("vector", lambda e: e.reduce_sum(ewc[0:16, 11:12], logits,
                                          axis=AX.X),
         reads=[pk1k], writes=[ewck], tag="ssum")
    p.op("vector", lambda e: e.tensor_scalar_add(
        ewc[0:16, 12:13], ewc[0:16, 11:12], 1e-9),
        reads=[ewck], writes=[ewck], tag="ssum2")
    p.op("vector", lambda e: e.reciprocal(ewc[0:16, 13:14],
                                          ewc[0:16, 12:13]),
         reads=[ewck], writes=[ewck], tag="srs")
    p.op("vector", lambda e: e.tensor_single_scalar(
        zap, logits, 0.0, ALU.is_gt),
        reads=[pk1k], writes=[pk2k], tag="mask0")
    pr = [(zap, pk2k), (cum3, pk3k)]
    src_i = 0
    for k in range(11):
        s = 1 << k
        aa, ak = pr[src_i]
        bb, bk = pr[1 - src_i]
        p.op("vector", lambda e, aa=aa, bb=bb, s=s: e.tensor_copy(
            bb[:, 0:s], aa[:, 0:s]),
            reads=[ak], writes=[bk], tag=f"cumc{k}")
        p.op("vector", lambda e, aa=aa, bb=bb, s=s: e.tensor_add(
            bb[:, s:2048], aa[:, s:2048], aa[:, 0:2048 - s]),
            reads=[ak], writes=[bk], tag=f"cuma{k}")
        src_i = 1 - src_i
    assert src_i == 1
    p.op("vector", lambda e: e.tensor_single_scalar(
        zap, logits, 0.0, ALU.is_gt),
        reads=[pk1k], writes=[pk2k], tag="mask2")
    p.op("vector", lambda e: e.tensor_mul(cum3, cum3, zap),
         reads=[pk3k, pk2k], writes=[pk3k], tag="rankm")
    p.op("vector", lambda e: e.tensor_scalar_sub(cum3, cum3, 1.0),
         reads=[pk3k], writes=[pk3k], tag="rank1")
    rankp = cum3
    p.op("vector", lambda e: e.tensor_scalar_mul(
        logits, logits, ewc[0:16, 13:14]),
        reads=[pk1k, ewck], writes=[pk1k], tag="pseln")
    pseln = logits
    for t in range(NT):
        bi = C.bank()
        p.op("tensor", lambda e, o=C.pb[bi][:, 0:16],
             i_=rankp[:, ts(t, 128)]:
             e.transpose(o, i_, C.ident[0:16, 0:16]),
             reads=[pk3k, ("ident",)], writes=[C.pbk[bi]], tag=f"rktp{t}")
        p.op("vector", lambda e, o=rkc[:, ts(t, 16)],
             i_=C.pb[bi][:, 0:16]:
             e.tensor_copy(o, i_),
             reads=[C.pbk[bi]], writes=[rkck], tag=f"rkev{t}")
        bj = C.bank()
        p.op("tensor", lambda e, o=C.pb[bj][:, 0:16],
             i_=pseln[:, ts(t, 128)]:
             e.transpose(o, i_, C.ident[0:16, 0:16]),
             reads=[pk1k, ("ident",)], writes=[C.pbk[bj]], tag=f"wctp{t}")
        p.op("vector", lambda e, o=wcl[:, ts(t, 16)],
             i_=C.pb[bj][:, 0:16]:
             e.tensor_copy(o, i_),
             reads=[C.pbk[bj]], writes=[wclk], tag=f"wcev{t}")

    scr.pop(mark0)
    einT, einTk = scr.alloc(p, 2048, BF16, "einT")
    hbuf, hk = scr.alloc(p, 16 * 256, BF16, "hbuf")
    eoT, eoTk = scr.alloc(p, 1024, F32, "eoT")
    sring, sringk = scr.alloc(p, 2 * 256, BF16, "silu_ring")
    ones_f, ones_fk = scr.alloc(p, 128, F32, "ones_f")
    p.op("vector", lambda e: e.memset(ones_f[0:1, :], 1.0),
         writes=[ones_fk], tag="onesf")

    w1h, w2h, w3h = C.w["L0_ffn_w1"], C.w["L0_ffn_w2"], C.w["L0_ffn_w3"]

    for e_ in range(16):
        for t in range(NT):
            p.op("vector", lambda e, o=selT[:, ts(t, 256)],
                 rc=rkc[:, t * 16 + e_: t * 16 + e_ + 1]:
                 e.tensor_single_scalar(o, C.iota_row, rc, ALU.is_equal),
                 reads=[rkck, ("iota",)], writes=[selTk[t // 4]],
                 tag=f"selT{e_}_{t}")
        p.dma("gpsimd", out=rbc[0:1, :], in_=cum3[e_:e_ + 1, :],
              reads=[pk3k], writes=[rbck], tag=f"rrow{e_}")
        for g in range(4):
            bi = C.bank()
            p.op("tensor", lambda e, o=C.pb[bi][:, 0:512],
                 l=ones_f[0:1, :], r=rbc[0:1, ts(g, 512)]:
                 e.matmul(o, l, r, start=True, stop=True),
                 reads=[ones_fk, rbck], writes=[C.pbk[bi]],
                 tag=f"rbc{e_}_{g}")
            p.op("vector", lambda e, o=rbc[:, ts(g, 512)],
                 i_=C.pb[bi][:, 0:512]:
                 e.tensor_copy(o, i_),
                 reads=[C.pbk[bi]], writes=[rbck], tag=f"rbcev{e_}_{g}")
        for cc in range(2):
            p.op("vector", lambda e, o=selb[cc][:, 0:2048],
                 ic=(C.iota_c0 if cc == 0 else C.iota_c1):
                 e.tensor_single_scalar(o, rbc, ic, ALU.is_equal),
                 reads=[rbck, ("iota",)], writes=selbk[cc],
                 tag=f"sel{e_}_{cc}")
        for cc in range(2):
            bi = C.bank()
            for t in range(NT):
                p.op("tensor", lambda e, o=C.pb[bi][:, 0:1],
                     l=selT[:, t * 256 + cc * 128: t * 256 + cc * 128 + 128],
                     r=wcl[:, t * 16 + e_: t * 16 + e_ + 1],
                     s=(t == 0), st=(t == NT - 1):
                     e.matmul(o, l, r, start=s, stop=st),
                     reads=[selTk[t // 4], wclk], writes=[C.pbk[bi]],
                     tag=f"ew{e_}_{cc}_{t}")
            p.op("vector", lambda e, o=ew2[:, cc:cc + 1],
                 i_=C.pb[bi][:, 0:1]:
                 e.tensor_copy(o, i_),
                 reads=[C.pbk[bi]], writes=[ewck], tag=f"ewev{e_}_{cc}")
        for dc in range(8):
            bi = C.bank()
            for t in range(NT):
                p.op("tensor", lambda e, o=C.pb[bi][:, 0:256],
                     l=nat[:, t * 1024 + dc * 128: t * 1024 + (dc + 1) * 128],
                     r=selT[:, ts(t, 256)],
                     s=(t == 0), st=(t == NT - 1):
                     e.matmul(o, l, r, start=s, stop=st),
                     reads=[natk[t], selTk[t // 4]], writes=[C.pbk[bi]],
                     tag=f"gth{e_}_{dc}_{t}")
            p.op("vector", lambda e, o=einT[:, ts(dc, 256)],
                 i_=C.pb[bi][:, 0:256]:
                 e.tensor_copy(o, i_),
                 reads=[C.pbk[bi]], writes=[einTk], tag=f"gthev{e_}_{dc}")
        for half in range(2):
            for dc in range(8):
                p.dma("gpsimd", out=w12[dc % 2],
                      in_=w1h[e_, ts(dc, 128), half * 1024:(half + 1) * 1024],
                      writes=[w12k[dc % 2]], tag=f"w1{e_}_{half}_{dc}")
                p.dma("gpsimd", out=w12[2 + dc % 2],
                      in_=w2h[e_, ts(dc, 128), half * 1024:(half + 1) * 1024],
                      writes=[w12k[2 + dc % 2]], tag=f"w2{e_}_{half}_{dc}")
                for df in range(8):
                    pg = C.pb[df // 4]
                    pv = C.pb[2 + df // 4]
                    po = (df % 4) * 256
                    p.op("tensor", lambda e, o=pg[:, po:po + 256],
                         l=w12[dc % 2][:, ts(df, 128)],
                         r=einT[:, ts(dc, 256)],
                         s=(dc == 0), st=(dc == 7):
                         e.matmul(o, l, r, start=s, stop=st),
                         reads=[w12k[dc % 2], einTk],
                         writes=[C.pbk[df // 4]], tag=f"g{e_}_{half}_{df}")
                    p.op("tensor", lambda e, o=pv[:, po:po + 256],
                         l=w12[2 + dc % 2][:, ts(df, 128)],
                         r=einT[:, ts(dc, 256)],
                         s=(dc == 0), st=(dc == 7):
                         e.matmul(o, l, r, start=s, stop=st),
                         reads=[w12k[2 + dc % 2], einTk],
                         writes=[C.pbk[2 + df // 4]],
                         tag=f"v{e_}_{half}_{df}")
            for df in range(8):
                pg = C.pb[df // 4]
                pv = C.pb[2 + df // 4]
                po = (df % 4) * 256
                p.op("scalar", lambda e, o=sring[:, ts(df % 2, 256)],
                     i_=pg[:, po:po + 256]:
                     e.activation(o, i_, AF.Silu),
                     reads=[C.pbk[df // 4]], writes=[sringk],
                     tag=f"silu{e_}_{half}_{df}")
                p.op("vector", lambda e,
                     o=hbuf[:, (half * 8 + df) * 256:
                            (half * 8 + df + 1) * 256],
                     i_=pv[:, po:po + 256], g_=sring[:, ts(df % 2, 256)]:
                     e.tensor_mul(o, i_, g_),
                     reads=[C.pbk[2 + df // 4], sringk], writes=[hk],
                     tag=f"h{e_}_{half}_{df}")
        for dh in range(2):
            for dffc in range(16):
                p.dma("gpsimd", out=w3r[dffc % 2][:, 0:512],
                      in_=w3h[e_, ts(dffc, 128),
                              dh * 512:(dh + 1) * 512],
                      writes=[w3rk[dffc % 2]], tag=f"w3{e_}_{dh}_{dffc}")
                for dq in range(4):
                    pq = C.pb[dq // 2]
                    po = (dq % 2) * 256
                    p.op("tensor", lambda e, o=pq[:, po:po + 256],
                         l=w3r[dffc % 2][:, ts(dq, 128)],
                         r=hbuf[:, ts(dffc, 256)],
                         s=(dffc == 0), st=(dffc == 15):
                         e.matmul(o, l, r, start=s, stop=st),
                         reads=[w3rk[dffc % 2], hk],
                         writes=[C.pbk[dq // 2]], tag=f"w3m{e_}_{dh}_{dq}")
            for dq in range(4):
                pq = C.pb[dq // 2]
                po = (dq % 2) * 256
                p.op("vector", lambda e, o=eoT[:, ts(dq, 256)],
                     i_=pq[:, po:po + 256]:
                     e.tensor_copy(o, i_),
                     reads=[C.pbk[dq // 2]], writes=[eoTk],
                     tag=f"w3ev{e_}_{dh}_{dq}")
            for dq in range(4):
                for cc in range(2):
                    bi = 2 + (dq % 2)
                    p.op("tensor", lambda e, o=C.pb[bi][:, 512:640],
                         i_=eoT[:, dq * 256 + cc * 128:
                                dq * 256 + (cc + 1) * 128]:
                         e.transpose(o, i_, C.ident),
                         reads=[eoTk, ("ident",)], writes=[C.pbk[bi]],
                         tag=f"eontp{e_}_{dh}_{dq}_{cc}")
                    dcol = dh * 512 + dq * 128
                    p.op("vector", lambda e,
                         o=eon[cc][:, dcol:dcol + 128],
                         i_=C.pb[bi][:, 512:640],
                         sc=ew2[:, cc:cc + 1]:
                         e.tensor_scalar_mul(o, i_, sc),
                         reads=[C.pbk[bi], ewck], writes=[eonk[cc]],
                         tag=f"eonev{e_}_{dh}_{dq}_{cc}")
        for t in range(NT):
            bi = 2 + (t % 2)
            for cc in range(2):
                for dh in range(2):
                    p.op("tensor", lambda e,
                         o=C.pb[bi][:, ts(dh, 512)],
                         l=selb[cc][:, ts(t, 128)],
                         r=eon[cc][:, ts(dh, 512)],
                         s=(cc == 0), st=(cc == 1):
                         e.matmul(o, l, r, start=s, stop=st),
                         reads=selbk[cc] + [eonk[cc]],
                         writes=[C.pbk[bi]], tag=f"sc{e_}_{t}_{cc}_{dh}")
            p.op("vector", lambda e, o=C.x(t), i_=C.pb[bi][:, 0:1024]:
                 e.tensor_add(o, i_, o),
                 reads=[C.pbk[bi], C.xk(t)], writes=[C.xk(t)],
                 tag=f"scadd{e_}_{t}")


PHASE_EMIT["l0ffn"] = emit_l0ffn


# version 08
# speedup vs baseline: 1.1596x; 1.1596x over previous
import numpy as np
import ml_dtypes
import concourse.bass as bass
import concourse.mybir as mybir
from concourse.bass_utils import run_bass_kernel_spmd

F32 = mybir.dt.float32
BF16 = mybir.dt.bfloat16
AF = mybir.ActivationFunctionType
ALU = mybir.AluOpType
AX = mybir.AxisListType

ENGINES = ("tensor", "vector", "scalar", "gpsimd", "sync")
QUEUES = ("sync", "gpsimd", "scalar")

DEBUG = False
DEBUG_B = 0
NT = 16
D = 1024
SEQ = 2048
NB = 2



class Op:
    __slots__ = ("idx", "engine", "fn", "deps", "is_dma", "queue", "inc_no",
                 "needed", "guar", "waits", "tag")

    def __init__(self, idx, engine, fn, is_dma=False, queue=None, tag=""):
        self.idx = idx
        self.engine = engine
        self.fn = fn
        self.deps = []
        self.is_dma = is_dma
        self.queue = queue
        self.inc_no = None
        self.needed = is_dma
        self.guar = None
        self.waits = None
        self.tag = tag


class Prog:

    def __init__(self, nc):
        self.nc = nc
        self.ops = []
        self.last_writer = {}
        self.readers = {}
        self.alias_pending = {}

    def alias(self, new_key, old_keys):
        self.alias_pending.setdefault(new_key, []).extend(old_keys)

    def _dep(self, op, d):
        if d is op:
            return
        if (d.engine == "tensor" and op.engine == "tensor"
                and not d.is_dma and not op.is_dma):
            return
        op.deps.append(d)

    def _absorb_alias(self, op, key):
        olds = self.alias_pending.pop(key, None)
        if olds:
            for ok in olds:
                lw = self.last_writer.get(ok)
                if lw is not None:
                    self._dep(op, lw)
                for r in self.readers.get(ok, ()):
                    self._dep(op, r)

    def op(self, engine, fn, reads=(), writes=(), tag=""):
        o = Op(len(self.ops), engine, fn, tag=tag)
        self._wire(o, reads, writes)
        self.ops.append(o)
        return o

    def dma(self, queue, out, in_, reads=(), writes=(), tag="", **kw):
        def fn(eng, out=out, in_=in_, kw=kw):
            return eng.dma_start(out=out, in_=in_, **kw)
        o = Op(len(self.ops), queue, fn, is_dma=True, queue=queue, tag=tag)
        self._wire(o, reads, writes)
        self.ops.append(o)
        return o

    def _wire(self, o, reads, writes):
        for r in reads:
            self._absorb_alias(o, r)
            lw = self.last_writer.get(r)
            if lw is not None:
                self._dep(o, lw)
        for w in writes:
            self._absorb_alias(o, w)
            lw = self.last_writer.get(w)
            if lw is not None:
                self._dep(o, lw)
            for rd in self.readers.get(w, ()):
                self._dep(o, rd)
        for w in writes:
            self.last_writer[w] = o
            self.readers[w] = []
        for r in reads:
            self.readers.setdefault(r, []).append(o)

    def finalize_and_emit(self, block):
        for o in self.ops:
            for d in o.deps:
                d.needed = True
        counts = {}
        for o in self.ops:
            if o.needed:
                grp = ("q", o.queue) if o.is_dma else ("e", o.engine)
                step = 16 if o.is_dma else 1
                counts[grp] = counts.get(grp, 0) + step
                o.inc_no = counts[grp]
        self.final_counts = counts
        for o in self.ops:
            need = {}
            for d in o.deps:
                grp = ("q", d.queue) if d.is_dma else ("e", d.engine)
                v = d.inc_no
                if v is not None and v > need.get(grp, 0):
                    need[grp] = v
                if d.guar:
                    for g, gv in d.guar.items():
                        if gv > need.get(g, 0):
                            need[g] = gv
            o.waits = need
            g = dict(need)
            if o.inc_no is not None:
                grp = ("q", o.queue) if o.is_dma else ("e", o.engine)
                if o.inc_no > g.get(grp, 0):
                    g[grp] = o.inc_no
            o.guar = g

        sems = {}
        self._sem_cms = []
        for e in ("tensor", "vector", "scalar", "gpsimd"):
            cm = self.nc.semaphore(f"sem_{e}")
            sems[("e", e)] = cm.__enter__()
            self._sem_cms.append(cm)
        for q in QUEUES:
            cm = self.nc.semaphore(f"q_{q}")
            sems[("q", q)] = cm.__enter__()
            self._sem_cms.append(cm)
        self.sems = sems

        per_engine = {e: [] for e in ENGINES}
        for o in self.ops:
            per_engine[o.engine].append(o)
        self.n_ops = {e: len(v) for e, v in per_engine.items()}

        def make_body(engine, ops):
            def body(eng):
                frontier = {}
                for o in ops:
                    for grp, v in sorted(o.waits.items()):
                        if frontier.get(grp, 0) >= v:
                            continue
                        eng.wait_ge(sems[grp], v)
                        frontier[grp] = v
                    inst = o.fn(eng)
                    if o.inc_no is not None:
                        step = 16 if o.is_dma else 1
                        grp = ("q", o.queue) if o.is_dma else ("e", o.engine)
                        inst.then_inc(sems[grp], step)
            return body

        final = dict(self.final_counts)

        def body_sync(eng):
            make_body("sync", per_engine["sync"])(eng)
            for grp, v in sorted(final.items()):
                if grp != ("q", "sync"):
                    eng.wait_ge(sems[grp], v)
            if ("q", "sync") in final:
                eng.wait_ge(sems[("q", "sync")], final[("q", "sync")])

        block.sync(body_sync)
        for e, mk in (("tensor", block.tensor), ("vector", block.vector),
                      ("scalar", block.scalar), ("gpsimd", block.gpsimd)):
            if per_engine[e]:
                mk(make_body(e, per_engine[e]))


class Arena:

    def __init__(self, nc, name, bytes_pp):
        self.t = nc.alloc_sbuf_tensor(name, [128, bytes_pp // 2], BF16)
        self.name = name
        self.bytes = bytes_pp
        self.pos = 0
        self.live = []
        self.dead = []
        self.n = 0

    def alloc(self, prog, nelem, dtype, tag=""):
        dsz = 4 if dtype == F32 else 2
        nbytes = (nelem * dsz + 63) // 64 * 64
        assert self.pos + nbytes <= self.bytes, (
            f"arena {self.name}: {self.pos}+{nbytes}>{self.bytes} ({tag})")
        start, end = self.pos, self.pos + nbytes
        self.pos = end
        key = (self.name, self.n, tag)
        self.n += 1
        olds = [k for (s, e, k) in self.dead if s < end and e > start]
        if olds:
            prog.alias(key, olds)
            self.dead = [(s, e, k) for (s, e, k) in self.dead
                         if not (start <= s and e <= end)]
        self.live.append((start, end, key))
        ap = self.t.ap()[:, start // 2: start // 2 + nelem * dsz // 2]
        if dtype == F32:
            ap = ap.bitcast(F32)
        return ap, key

    def reset(self):
        self.dead.extend(self.live)
        self.live = []
        self.pos = 0

    def mark(self):
        return self.pos

    def pop(self, mark):
        keep = [x for x in self.live if x[0] < mark]
        retire = [x for x in self.live if x[0] >= mark]
        self.dead.extend(retire)
        self.live = keep
        self.pos = mark


def ts(i, w):
    return slice(i * w, (i + 1) * w)



EPS_LN = 1e-5
EPS_RMS = 1.1920929e-07


class Ctx:
    def __init__(self, nc, prog):
        self.nc = nc
        self.prog = prog
        self.x_t = nc.alloc_sbuf_tensor("x_res", [128, NT * D], F32)
        self.wa_t = nc.alloc_sbuf_tensor("warena", [128, 32 * 1024], BF16)
        self.scr = Arena(nc, "scratch", 78 * 1024)
        self.pb_t = [nc.alloc_psum_tensor(f"pb{i}", [128, 1024], F32)
                     for i in range(4)]
        self.pb = [t.ap() for t in self.pb_t]
        self.pbk = [("pb", i) for i in range(4)]
        self.ident_t = nc.alloc_sbuf_tensor("ident", [128, 128], F32)
        self.ident = self.ident_t.ap()
        self.ones_t = nc.alloc_sbuf_tensor("ones_bf", [1, 128], BF16)
        self.ones = self.ones_t.ap()
        self.iota_t = nc.alloc_sbuf_tensor("iotas", [128, 260], F32)
        self.cpool_t = nc.alloc_sbuf_tensor("cpool", [128, 8], F32)
        self._consts = {}
        self._bank = 0
        self.w = {}
        self.wt = {}

    def x(self, t):
        return self.x_t.ap()[:, ts(t, D)]

    def xk(self, t):
        return ("x", t)

    def wa(self, slot, n=1):
        return self.wa_t.ap()[:, slot * 1024:(slot + n) * 1024]

    def wa_f32(self, slot, n=1):
        return self.wa_t.ap()[:, slot * 1024:(slot + n) * 1024].bitcast(F32)

    def wak(self, slot, n=1):
        return [("wa", s) for s in range(slot, slot + n)]

    def bank(self):
        i = self._bank % 4
        self._bank += 1
        return i

    def debug_dump(self, name, ap, reads, dtype=None):
        if not DEBUG:
            return
        nelem = 1
        shp = list(ap.shape)
        t = self.nc.dram_tensor("dbg_" + name, shp, ap.dtype,
                                kind="ExternalOutput")
        self.prog.dma("sync", out=t.ap(), in_=ap, reads=reads,
                      tag="dbg_" + name)

    def constf(self, v):
        if v not in self._consts:
            i = len(self._consts)
            assert i < 8
            ap = self.cpool_t.ap()[:, i:i + 1]
            self.prog.op("vector", lambda e, a=ap, vv=v: e.memset(a, vv),
                         writes=[("cpool",)], tag=f"const{i}")
            self._consts[v] = ap
        return self._consts[v]

    def emit_consts(self):
        p = self.prog
        p.dma("sync", out=self.ident, in_=self.w["ident_f32"],
              writes=[("ident",)], tag="ident")
        p.op("vector", lambda e: e.memset(self.ones, 1.0),
             writes=[("ones",)], tag="ones")
        it = self.iota_t.ap()
        p.op("gpsimd", lambda e: e.iota(
            it[:, 0:256], pattern=[[1, 256]], base=0, channel_multiplier=0,
            allow_small_or_imprecise_dtypes=True),
            writes=[("iota",)], tag="iota_row")
        p.op("gpsimd", lambda e: e.iota(
            it[:, 256:257], pattern=[[1, 1]], base=0, channel_multiplier=1,
            allow_small_or_imprecise_dtypes=True),
            writes=[("iota",)], tag="iota_c0")
        p.op("gpsimd", lambda e: e.iota(
            it[:, 257:258], pattern=[[1, 1]], base=128, channel_multiplier=1,
            allow_small_or_imprecise_dtypes=True),
            writes=[("iota",)], tag="iota_c1")

    @property
    def iota_row(self):
        return self.iota_t.ap()[:, 0:256]

    @property
    def iota_c0(self):
        return self.iota_t.ap()[:, 256:257]

    @property
    def iota_c1(self):
        return self.iota_t.ap()[:, 257:258]


def emit_ln(C, rms=False, nat=False, nat_ap=None, nat_keys=None):
    p, scr = C.prog, C.scr
    st, stk = scr.alloc(p, 128, F32, "ln_stats")
    s_sum, s_sq = st[:, 0:16], st[:, 16:32]
    mean, sval = st[:, 32:48], st[:, 48:64]
    t1, t2, t3 = st[:, 64:80], st[:, 80:96], st[:, 96:112]
    xnf = []
    for i in range(2):
        a, k = scr.alloc(p, D, F32, f"xnf{i}")
        xnf.append((a, k))
    xnT, xnTk = scr.alloc(p, 8 * SEQ, BF16, "xnT")
    out = {"xnT": xnT, "xnTk": xnTk}
    if nat:
        if nat_ap is not None:
            out["nat"] = nat_ap
            out["natks"] = nat_keys
        else:
            xnat, xnatk = scr.alloc(p, NT * D, BF16, "xn_nat")
            out["nat"] = xnat
            out["natks"] = [xnatk]

    for t in range(NT):
        xt = C.x(t)
        p.op("vector", lambda e, o=s_sum[:, t:t + 1], i_=xt:
             e.reduce_sum(o, i_, axis=AX.X),
             reads=[C.xk(t)], writes=[stk], tag=f"ln_sum{t}")
        dump = xnf[t % 2][0]
        p.op("scalar", lambda e, o=dump, i_=xt, a=s_sq[:, t:t + 1]:
             e.activation(o, i_, AF.Square, accum_out=a),
             reads=[C.xk(t)], writes=[stk, xnf[t % 2][1]], tag=f"ln_sq{t}")
    k = 1.0 / D
    p.op("vector", lambda e: e.tensor_scalar_mul(mean, s_sum, k),
         reads=[stk], writes=[stk], tag="ln_mean")
    p.op("vector", lambda e: e.tensor_scalar_mul(t1, s_sq, k),
         reads=[stk], writes=[stk], tag="ln_ex2")
    p.op("vector", lambda e: e.tensor_mul(t2, mean, mean),
         reads=[stk], writes=[stk], tag="ln_m2")
    p.op("vector", lambda e: e.tensor_sub(t3, t1, t2),
         reads=[stk], writes=[stk], tag="ln_var")
    p.op("scalar", lambda e, bb=C.constf(EPS_LN): e.activation(
        t1, t3, AF.Ln, bias=bb),
         reads=[stk, ("cpool",)], writes=[stk], tag="ln_lnv")
    p.op("scalar", lambda e: e.activation(sval, t1, AF.Exp, scale=-0.5),
         reads=[stk], writes=[stk], tag="ln_rstd")
    if rms:
        p.op("vector", lambda e: e.tensor_scalar_add(t2, t3, EPS_LN),
             reads=[stk], writes=[stk], tag="rms_vp")
        p.op("vector", lambda e: e.reciprocal(t1, t2),
             reads=[stk], writes=[stk], tag="rms_rv")
        p.op("vector", lambda e: e.tensor_mul(t2, t3, t1),
             reads=[stk], writes=[stk], tag="rms_q")
        p.op("scalar", lambda e, bb=C.constf(EPS_RMS): e.activation(
            t1, t2, AF.Ln, bias=bb),
             reads=[stk, ("cpool",)], writes=[stk], tag="rms_ln")
        p.op("scalar", lambda e: e.activation(t2, t1, AF.Exp, scale=-0.5),
             reads=[stk], writes=[stk], tag="rms_exp")
        p.op("vector", lambda e: e.tensor_mul(sval, sval, t2),
             reads=[stk], writes=[stk], tag="rms_sval")

    for t in range(NT):
        xf, xfk = xnf[t % 2]
        p.op("vector", lambda e, o=xf, i_=C.x(t), m=mean[:, t:t + 1],
             s=sval[:, t:t + 1]:
             e.tensor_scalar(o, i_, m, s, ALU.subtract, ALU.mult),
             reads=[C.xk(t), stk], writes=[xfk], tag=f"ln_xn{t}")
        if nat:
            natt = out["nat"][:, ts(t, D)]
            natw = (out["natks"] if nat_ap is None
                    else [out["natks"][t * D // 1024 + j]
                          for j in range((D + 1023) // 1024)])
            p.op("vector", lambda e, o=natt, i_=xf:
                 e.tensor_copy(o, i_),
                 reads=[xfk], writes=natw, tag=f"ln_nat{t}")
        for dc in range(8):
            bi = C.bank()
            p.op("tensor", lambda e, o=C.pb[bi][:, 0:128],
                 i_=xf[:, ts(dc, 128)]:
                 e.transpose(o, i_, C.ident),
                 reads=[xfk, ("ident",)], writes=[C.pbk[bi]],
                 tag=f"ln_tp{t}_{dc}")
            emit_copy(C, "scalar" if (dc % 2) else "vector",
                      xnT[:, dc * SEQ + t * 128: dc * SEQ + (t + 1) * 128],
                      C.pb[bi][:, 0:128], [C.pbk[bi]], [xnTk],
                      tag=f"ln_ev{t}_{dc}")
    return out


def emit_copy(C, engine, out, in_, reads, writes, tag=""):
    if engine == "scalar":
        C.prog.op("scalar", lambda e, o=out, i_=in_: e.copy(o, i_),
                  reads=reads, writes=writes, tag=tag)
    else:
        C.prog.op(engine, lambda e, o=out, i_=in_: e.tensor_copy(o, i_),
                  reads=reads, writes=writes, tag=tag)



def emit_l1ffn(C, b):
    p, scr, nc = C.prog, C.scr, C.nc
    scr.reset()
    ln = emit_ln(C, rms=False, nat=False)
    xnT, xnTk = ln["xnT"], ln["xnTk"]

    c_nat, c_natk = scr.alloc(p, NT * 16, F32, "c_nat")
    b1c, b1ck = scr.alloc(p, 256, F32, "b1cols")
    mark0 = scr.mark()

    rw, rwk = scr.alloc(p, 8 * 16, BF16, "rw")
    for dc in range(8):
        p.dma("gpsimd", out=rw[:, ts(dc, 16)],
              in_=C.w["L1_ffn_router_w"][ts(dc, 128), :],
              writes=[rwk], tag=f"rw{dc}")
    brow, browk = scr.alloc(p, 16, BF16, "brow")
    p.dma("gpsimd", out=brow[0:1, :], in_=C.w["rb_row"],
          writes=[browk], tag="brow")

    logits, logitsk = scr.alloc(p, NT * 16, F32, "logits")
    cT, cTk = scr.alloc(p, SEQ, BF16, "cT")
    tops, topsk = scr.alloc(p, 32, F32, "tops")
    top8, negm = tops[:, 0:8], tops[:, 8:9]
    s4c, rsc = tops[:, 9:10], tops[:, 10:11]
    prb, prbk = scr.alloc(p, 48, F32, "prb")
    probs_t, mask_t, pm_t = prb[:, 0:16], prb[:, 16:32], prb[:, 32:48]

    for t in range(NT):
        bi = C.bank()
        ps = C.pb[bi][:, 0:16]
        for dc in range(8):
            p.op("tensor", lambda e, o=ps,
                 l=xnT[:, dc * SEQ + t * 128: dc * SEQ + (t + 1) * 128],
                 r=rw[:, ts(dc, 16)], s=(dc == 0):
                 e.matmul(o, l, r, start=s, stop=False),
                 reads=[xnTk, rwk], writes=[C.pbk[bi]], tag=f"rlog{t}_{dc}")
        p.op("tensor", lambda e, o=ps: e.matmul(
            o, C.ones, brow[0:1, :], start=False, stop=True),
            reads=[("ones",), browk], writes=[C.pbk[bi]], tag=f"rbias{t}")
        p.op("vector", lambda e, o=logits[:, ts(t, 16)], i_=ps:
             e.tensor_copy(o, i_),
             reads=[C.pbk[bi]], writes=[logitsk], tag=f"rlogev{t}")
        lg = logits[:, ts(t, 16)]
        p.op("vector", lambda e, o=top8, i_=lg: e.max(out=o, in_=i_),
             reads=[logitsk], writes=[topsk], tag=f"rtop8{t}")
        p.op("vector", lambda e: e.tensor_scalar_mul(negm, top8[:, 0:1], -1.0),
             reads=[topsk], writes=[topsk], tag=f"rnegm{t}")
        p.op("scalar", lambda e, i_=lg: e.activation(
            probs_t, i_, AF.Exp, bias=negm, accum_out=s4c),
            reads=[logitsk, topsk], writes=[prbk, topsk], tag=f"rexp{t}")
        p.op("vector", lambda e, i_=lg: e.tensor_single_scalar(
            mask_t, i_, top8[:, 3:4], ALU.is_ge),
            reads=[logitsk, topsk], writes=[prbk], tag=f"rmask{t}")
        p.op("vector", lambda e: e.tensor_mul(pm_t, probs_t, mask_t),
             reads=[prbk], writes=[prbk], tag=f"rpm{t}")
        p.op("vector", lambda e: e.reduce_sum(s4c, pm_t, axis=AX.X),
             reads=[prbk], writes=[topsk], tag=f"rs4{t}")
        p.op("vector", lambda e: e.reciprocal(rsc, s4c),
             reads=[topsk], writes=[topsk], tag=f"rrs{t}")
        p.op("vector", lambda e, o=c_nat[:, ts(t, 16)]:
             e.tensor_scalar_mul(o, pm_t, rsc),
             reads=[prbk, topsk], writes=[c_natk], tag=f"rc{t}")
        bj = C.bank()
        p.op("tensor", lambda e, o=C.pb[bj][0:16, 0:128],
             i_=c_nat[:, ts(t, 16)]:
             e.transpose(o, i_, C.ident),
             reads=[c_natk, ("ident",)], writes=[C.pbk[bj]], tag=f"rcT{t}")
        emit_copy(C, "scalar", cT[0:16, t * 128:(t + 1) * 128],
                  C.pb[bj][0:16, 0:128], [C.pbk[bj]], [cTk], tag=f"rcTev{t}")

    if b == DEBUG_B:
        C.debug_dump("xnT", xnT, [xnTk])
        C.debug_dump("logits", logits, [logitsk])
        C.debug_dump("c_nat", c_nat, [c_natk])

    b2sb, b2k = scr.alloc(p, D, BF16, "b2")
    p.dma("gpsimd", out=b2sb[0:16, :], in_=C.w["L1_ffn_b2"],
          writes=[b2k], tag="b2")
    for t in range(NT):
        bi = C.bank()
        for dh in range(2):
            p.op("tensor", lambda e, o=C.pb[bi][:, ts(dh, 512)],
                 l=cT[0:16, ts(t, 128)], r=b2sb[0:16, ts(dh, 512)]:
                 e.matmul(o, l, r, start=True, stop=True),
                 reads=[cTk, b2k], writes=[C.pbk[bi]], tag=f"b2mm{t}_{dh}")
        for dh in range(2):
            p.op("vector", lambda e, o=C.x(t)[:, ts(dh, 512)],
                 i_=C.pb[bi][:, ts(dh, 512)]:
                 e.tensor_add(o, i_, o),
                 reads=[C.pbk[bi], C.xk(t)], writes=[C.xk(t)],
                 tag=f"b2add{t}_{dh}")

    if b == DEBUG_B:
        C.debug_dump("x_postb2", C.x_t.ap(), [C.xk(t) for t in range(NT)])

    t16a, t16ak = scr.alloc(p, 128, F32, "t16a")
    for e_ in range(16):
        p.dma("sync", out=t16a[0:16, :],
              in_=C.w["b1_resh"][ts(e_, 16), :],
              writes=[t16ak], tag=f"b1ld{e_}")
        bi = C.bank()
        p.op("tensor", lambda e, o=C.pb[bi][:, 0:16], i_=t16a[0:16, :]:
             e.transpose(o, i_, C.ident[0:16, 0:16]),
             reads=[t16ak, ("ident",)], writes=[C.pbk[bi]], tag=f"b1tp{e_}")
        p.op("vector", lambda e, o=b1c[:, ts(e_, 16)], i_=C.pb[bi][:, 0:16]:
             e.tensor_copy(o, i_),
             reads=[C.pbk[bi]], writes=[b1ck], tag=f"b1ev{e_}")

    scr.pop(mark0)
    hmid, _hk = scr.alloc(p, 16 * 1024, BF16, "hmid")
    hmidks = [_hk for _ in range(16)]
    w1h = C.w["L1_ffn_w1"]
    w2h = C.w["L1_ffn_w2"]
    for e_ in range(16):
        for dc in range(8):
            p.dma("gpsimd", out=C.wa(2 * dc, 2),
                  in_=w1h[e_, ts(dc, 128), :],
                  writes=C.wak(2 * dc, 2), tag=f"w1_{e_}_{dc}")
        for df in range(16):
            p.dma("gpsimd", out=C.wa(16 + df, 1),
                  in_=w2h[e_, ts(df, 128), :],
                  writes=C.wak(16 + df, 1), tag=f"w2_{e_}_{df}")
        for H in range(2):
            for df in range(16):
                bi = df % 2
                for dc in range(8):
                    for th in range(2):
                        p.op("tensor", lambda e, o=C.pb[bi][:, ts(th, 512)],
                             l=C.wa(2 * dc, 2)[:, ts(df, 128)],
                             r=xnT[:, dc * SEQ + H * 1024 + th * 512:
                                   dc * SEQ + H * 1024 + (th + 1) * 512],
                             s=(dc == 0), st=(dc == 7):
                             e.matmul(o, l, r, start=s, stop=st),
                             reads=C.wak(2 * dc, 2) + [xnTk],
                             writes=[C.pbk[bi]], tag=f"w1mm{e_}_{H}_{df}")
                p.op("scalar", lambda e, o=hmid[:, ts(df, 1024)],
                     i_=C.pb[bi][:, 0:1024], bb=b1c[:, e_ * 16 + df:
                                                    e_ * 16 + df + 1]:
                     e.activation(o, i_, AF.Gelu, bias=bb),
                     reads=[C.pbk[bi], b1ck], writes=[hmidks[df]],
                     tag=f"gelu{e_}_{H}_{df}")
            for t128 in range(8):
                tt = H * 8 + t128
                bj = 2 + (t128 % 2)
                for df in range(16):
                    for dh in range(2):
                        p.op("tensor", lambda e, o=C.pb[bj][:, ts(dh, 512)],
                             l=hmid[:, df * 1024 + t128 * 128:
                                    df * 1024 + (t128 + 1) * 128],
                             r=C.wa(16 + df, 1)[:, ts(dh, 512)],
                             s=(df == 0), st=(df == 15):
                             e.matmul(o, l, r, start=s, stop=st),
                             reads=[hmidks[df]] + C.wak(16 + df, 1),
                             writes=[C.pbk[bj]], tag=f"w2mm{e_}_{tt}_{df}")
                for dh in range(2):
                    p.op("vector", lambda e, o=C.x(tt)[:, ts(dh, 512)],
                         i_=C.pb[bj][:, ts(dh, 512)],
                         cc=c_nat[:, tt * 16 + e_: tt * 16 + e_ + 1]:
                         e.scalar_tensor_tensor(o, i_, cc, o,
                                                ALU.mult, ALU.add),
                         reads=[C.pbk[bj], C.xk(tt), c_natk],
                         writes=[C.xk(tt)], tag=f"stt{e_}_{tt}_{dh}")



PHASE_EMIT = {}
PHASE_EMIT["l1ffn"] = emit_l1ffn

PHASE_WEIGHTS = {
    "l0attn": ["L0_attn_c_attn_w", "cattn_b_resh", "L0_attn_c_proj_w",
               "cproj_b_resh", "L0_attn_wk_c", "L0_attn_wk_e",
               "L0_attn_wv_c", "L0_attn_wv_e"],
    "l0ffn": ["L0_ffn_router_w1", "L0_ffn_router_w2", "L0_ffn_w1",
              "L0_ffn_w2", "L0_ffn_w3"],
    "l1attn": ["Wqkv_perm", "bqkv_resh", "L1_attn_W_o", "bo_resh"],
    "l1ffn": ["L1_ffn_router_w", "rb_row", "L1_ffn_w1", "b1_resh",
              "L1_ffn_w2", "L1_ffn_b2"],
}

WEIGHT_SHAPES = {
    "L0_attn_c_attn_w": (1024, 3072), "cattn_b_resh": (24, 128),
    "L0_attn_c_proj_w": (1024, 1024), "cproj_b_resh": (8, 128),
    "L0_attn_wk_c": (64, 32), "L0_attn_wk_e": (32, 64),
    "L0_attn_wv_c": (64, 32), "L0_attn_wv_e": (32, 64),
    "L0_ffn_router_w1": (1024, 128), "L0_ffn_router_w2": (128, 16),
    "L0_ffn_w1": (16, 1024, 2048), "L0_ffn_w2": (16, 1024, 2048),
    "L0_ffn_w3": (16, 2048, 1024),
    "Wqkv_perm": (3072, 1024), "bqkv_resh": (24, 128),
    "L1_attn_W_o": (1024, 1024), "bo_resh": (8, 128),
    "L1_ffn_router_w": (1024, 16), "rb_row": (1, 16),
    "L1_ffn_w1": (16, 1024, 2048), "b1_resh": (256, 128),
    "L1_ffn_w2": (16, 2048, 1024), "L1_ffn_b2": (16, 1024),
}


def build_program(phases):
    nc = bass.Bass("TRN2", target_bir_lowering=False)
    x_in = nc.dram_tensor("x_in", [NB, SEQ, D], F32, kind="ExternalInput")
    y = nc.dram_tensor("y", [NB, SEQ, D], F32, kind="ExternalOutput")
    names = ["ident_f32"]
    for ph in phases:
        names += PHASE_WEIGHTS[ph]
    prog = Prog(nc)
    C = Ctx(nc, prog)
    C.w["ident_f32"] = nc.dram_tensor(
        "ident_f32", [128, 128], F32, kind="ExternalInput").ap()
    for n in names:
        if n == "ident_f32":
            continue
        C.w[n] = nc.dram_tensor(n, list(WEIGHT_SHAPES[n]), F32,
                                kind="ExternalInput").ap()
    C.emit_consts()
    p = prog
    for b in range(NB):
        src = x_in.ap()[b].rearrange("(c p) d -> p c d", p=128)
        dst_x = C.x_t.ap().rearrange("p (c d) -> p c d", d=D)
        p.dma("sync", out=dst_x, in_=src,
              writes=[C.xk(t) for t in range(NT)], tag=f"xload{b}")
        for ph in phases:
            PHASE_EMIT[ph](C, b)
        dst = y.ap()[b].rearrange("(c p) d -> p c d", p=128)
        src_x = C.x_t.ap().rearrange("p (c d) -> p c d", d=D)
        p.dma("sync", out=dst, in_=src_x,
              reads=[C.xk(t) for t in range(NT)], tag=f"xstore{b}")
        C.scr.reset()
    with nc.Block() as block:
        prog.finalize_and_emit(block)
    return nc, names, prog


def _host_prep(inputs):
    d = {k: np.asarray(v) for k, v in inputs.items()}
    arrs = {}
    arrs["ident_f32"] = np.eye(128, dtype=np.float32)
    f = np.float32
    H_, Dh = 16, 64
    old = np.arange(3 * H_ * Dh).reshape(H_, Dh, 3)
    perm = np.transpose(old, (2, 0, 1)).reshape(-1)
    arrs["Wqkv_perm"] = np.ascontiguousarray(
        d["L1_attn_W_qkv"].astype(f)[perm])
    arrs["bqkv_resh"] = np.ascontiguousarray(
        d["L1_attn_b_qkv"].astype(f)[perm].reshape(24, 128))
    arrs["cattn_b_resh"] = np.ascontiguousarray(
        d["L0_attn_c_attn_b"].astype(f).reshape(24, 128))
    arrs["cproj_b_resh"] = np.ascontiguousarray(
        d["L0_attn_c_proj_b"].astype(f).reshape(8, 128))
    arrs["bo_resh"] = np.ascontiguousarray(
        d["L1_attn_b_o"].astype(f).reshape(8, 128))
    arrs["b1_resh"] = np.ascontiguousarray(
        d["L1_ffn_b1"].astype(f).reshape(256, 128))
    arrs["rb_row"] = np.ascontiguousarray(
        d["L1_ffn_router_b"].astype(f).reshape(1, 16))
    for k, v in d.items():
        arrs.setdefault(k, np.ascontiguousarray(v.astype(f)))
    return arrs


PHASE_GROUPS = [["l0attn", "l0ffn", "l1attn", "l1ffn"]]

_prog_cache = {}


def _get_program(group):
    key = tuple(group)
    if key not in _prog_cache:
        _prog_cache[key] = build_program(list(group))
    return _prog_cache[key]


def kernel(**inputs):
    arrs = _host_prep(inputs)
    x = arrs["L0_attn_hidden_states"]
    shards = [np.ascontiguousarray(x[2 * c:2 * c + 2]) for c in range(8)]
    for group in PHASE_GROUPS:
        nc, names, _ = _get_program(group)
        in_maps = []
        for c in range(8):
            m = {"x_in": shards[c]}
            for n in names:
                m[n] = arrs[n]
            in_maps.append(m)
        res = run_bass_kernel_spmd(nc, in_maps, core_ids=list(range(8)))
        shards = [np.asarray(r["y"]) for r in res.results]
    return np.concatenate(shards, axis=0)



def emit_l1attn(C, b):
    p, scr = C.prog, C.scr
    scr.reset()
    ln = emit_ln(C, rms=True, nat=False)
    xnT, xnTk = ln["xnT"], ln["xnTk"]

    S_ZT, S_QT, S_KT, S_VN, S_PT = 0, 16, 18, 20, 22
    S_WQKV, S_WOT, S_WOSL = 26, 29, 30

    t24, t24k = scr.alloc(p, 128, F32, "t24")
    bqk, bqkk = scr.alloc(p, 32, F32, "bqkcols")
    bo, bok = scr.alloc(p, 8, F32, "bocols")
    p.dma("sync", out=t24[0:24, :], in_=C.w["bqkv_resh"], writes=[t24k],
          tag="bqkld")
    bi = C.bank()
    p.op("tensor", lambda e, o=C.pb[bi][:, 0:24]: e.transpose(
        o, t24[0:24, :], C.ident[0:24, 0:24]),
        reads=[t24k, ("ident",)], writes=[C.pbk[bi]], tag="bqktp")
    p.op("vector", lambda e, i_=C.pb[bi][:, 0:24]: e.tensor_copy(
        bqk[:, 0:24], i_), reads=[C.pbk[bi]], writes=[bqkk], tag="bqkev")
    p.dma("sync", out=t24[0:8, :], in_=C.w["bo_resh"], writes=[t24k],
          tag="bold")
    bi = C.bank()
    p.op("tensor", lambda e, o=C.pb[bi][:, 0:8]: e.transpose(
        o, t24[0:8, :], C.ident[0:8, 0:8]),
        reads=[t24k, ("ident",)], writes=[C.pbk[bi]], tag="botp")
    p.op("vector", lambda e, i_=C.pb[bi][:, 0:8]: e.tensor_copy(
        bo[:, 0:8], i_), reads=[C.pbk[bi]], writes=[bok], tag="boev")

    zero, zerok = scr.alloc(p, 512, F32, "maskzero")
    p.op("vector", lambda e: e.memset(zero, 0.0), writes=[zerok],
         tag="maskzero")
    masks, maskks = [], []
    for i in range(4):
        m, mk = scr.alloc(p, 512, F32, f"mask{i}")
        masks.append(m)
        maskks.append(mk)
        p.op("gpsimd", lambda e, o=m, dj=i * 128: e.affine_select(
            o, zero, pattern=[[1, 512]], compare_op=ALU.is_ge,
            fill=-1e30, base=-dj, channel_multiplier=-1),
            reads=[zerok], writes=[mk], tag=f"mask{i}")

    vf32, vf32k = scr.alloc(p, 2048, F32, "vf32")
    wqsl, wqslk = scr.alloc(p, 1024, F32, "wqslab")
    otmp, otmpk = scr.alloc(p, 2048, F32, "otmp")
    mredh, mredhk, vnsh, vnshk = [], [], [], []
    for h in range(4):
        mm, mk_ = scr.alloc(p, 16, F32, f"mred{h}")
        mredh.append(mm)
        mredhk.append(mk_)
        vv, vk_ = scr.alloc(p, 128, BF16, f"vns{h}")
        vnsh.append(vv)
        vnshk.append(vk_)

    qT, qTk = C.wa(S_QT, 2), [("wa", S_QT), ("wa", S_QT + 1)]
    kT, kTk = C.wa(S_KT, 2), [("wa", S_KT), ("wa", S_KT + 1)]
    vnat, vnatk = C.wa(S_VN, 2), [("wa", S_VN), ("wa", S_VN + 1)]
    pth = [C.wa(S_PT, 2), C.wa(S_PT + 2, 2)]
    pthk = [[("wa", S_PT), ("wa", S_PT + 1)],
            [("wa", S_PT + 2), ("wa", S_PT + 3)]]
    ztb = C.wa(S_ZT, 16)
    ztbk = [("wa", S_ZT + i) for i in range(16)]
    wqkvT = [C.wa(S_WQKV + i, 1) for i in range(3)]
    wqkvTk = [("wa", S_WQKV + i) for i in range(3)]
    wot, wotk = C.wa(S_WOT, 1), [("wa", S_WOT)]
    wosl, woslk = C.wa_f32(S_WOSL, 2), [("wa", S_WOSL), ("wa", S_WOSL + 1)]

    Wq = C.w["Wqkv_perm"]
    Wo = C.w["L1_attn_W_o"]

    for pair in range(8):
        for c in range(3):
            p.dma("sync", out=wqsl,
                  in_=Wq[c * 1024 + pair * 128:
                         c * 1024 + (pair + 1) * 128, :],
                  writes=[wqslk], tag=f"wq{pair}_{c}")
            for dc in range(8):
                bi = C.bank()
                p.op("tensor", lambda e, o=C.pb[bi][:, 0:128],
                     i_=wqsl[:, ts(dc, 128)]:
                     e.transpose(o, i_, C.ident),
                     reads=[wqslk, ("ident",)], writes=[C.pbk[bi]],
                     tag=f"wqtp{pair}_{c}_{dc}")
                emit_copy(C, "vector", wqkvT[c][:, ts(dc, 128)],
                          C.pb[bi][:, 0:128], [C.pbk[bi]], [wqkvTk[c]],
                          tag=f"wqev{pair}_{c}_{dc}")
        for c, outb, outk in ((0, qT, qTk), (1, kT, kTk),
                              (2, vf32, [vf32k])):
            for g in range(2):
                bi = C.bank()
                for dc in range(8):
                    for th in range(2):
                        cb = g * 1024 + th * 512
                        p.op("tensor",
                             lambda e, o=C.pb[bi][:, ts(th, 512)],
                             l=wqkvT[c][:, ts(dc, 128)],
                             r=xnT[:, dc * SEQ + cb: dc * SEQ + cb + 512],
                             s=(dc == 0), st=(dc == 7):
                             e.matmul(o, l, r, start=s, stop=st),
                             reads=[wqkvTk[c], xnTk], writes=[C.pbk[bi]],
                             tag=f"qkv{pair}_{c}_{g}")
                bb = bqk[:, c * 8 + pair: c * 8 + pair + 1]
                p.op("scalar", lambda e, o=outb[:, ts(g, 1024)],
                     i_=C.pb[bi][:, 0:1024], bb=bb:
                     e.activation(o, i_, AF.Identity, bias=bb),
                     reads=[C.pbk[bi], bqkk], writes=outk,
                     tag=f"qkvev{pair}_{c}_{g}")
        for kk in range(16):
            bi = C.bank()
            p.op("tensor", lambda e, o=C.pb[bi][:, 0:128],
                 i_=vf32[:, ts(kk, 128)]:
                 e.transpose(o, i_, C.ident),
                 reads=[vf32k, ("ident",)], writes=[C.pbk[bi]],
                 tag=f"vtp{pair}_{kk}")
            emit_copy(C, "vector", vnat[:, ts(kk, 128)],
                      C.pb[bi][:, 0:128], [C.pbk[bi]], vnatk,
                      tag=f"vev{pair}_{kk}")
        def l1_S_first(kb, h):
            hs = slice(h * 64, (h + 1) * 64)
            sp = C.pb[0] if h == 0 else C.pb[3]
            spk = C.pbk[0] if h == 0 else C.pbk[3]
            c0 = kb // 8
            delta = kb * 128 - c0 * 1024
            hb = (delta // 512) * 512
            for th in range(hb // 512, 2):
                p.op("tensor", lambda e, o=sp[:, ts(th, 512)],
                     l=kT[hs, ts(kb, 128)],
                     r=qT[hs, c0 * 1024 + th * 512:
                          c0 * 1024 + (th + 1) * 512]:
                     e.matmul(o, l, r, start=True, stop=True),
                     reads=kTk + qTk, writes=[spk],
                     tag=f"S{pair}_{h}_{kb}_{c0}_{th}")

        def l1_rest_stages(kb, h):
            hs = slice(h * 64, (h + 1) * 64)
            sp = C.pb[0] if h == 0 else C.pb[3]
            spk = C.pbk[0] if h == 0 else C.pbk[3]
            hr = h * 2 + kb % 2
            mr = mredh[hr]
            mrk = mredhk[hr]
            pt = pth[h]
            ptk = pthk[h]
            c0 = kb // 8
            delta = kb * 128 - c0 * 1024
            nch = 2 - c0
            for ci, qc in enumerate(range(c0, 2)):
                hb = (delta // 512) * 512 if qc == c0 else 0
                lo = hb
                if qc != c0:
                    for th in range(2):
                        p.op("tensor", lambda e, o=sp[:, ts(th, 512)],
                             l=kT[hs, ts(kb, 128)],
                             r=qT[hs, qc * 1024 + th * 512:
                                  qc * 1024 + (th + 1) * 512]:
                             e.matmul(o, l, r, start=True, stop=True),
                             reads=kTk + qTk, writes=[spk],
                             tag=f"S{pair}_{h}_{kb}_{qc}_{th}")
                if qc == c0:
                    mi = (delta % 512) // 128
                    p.op("vector", lambda e,
                         o=sp[:, (delta // 512) * 512:
                              (delta // 512) * 512 + 512],
                         m=masks[mi]:
                         e.tensor_add(o, o, m),
                         reads=[spk, maskks[mi]], writes=[spk],
                         tag=f"Sm{pair}_{h}_{kb}")
                    p.op("vector", lambda e,
                         o=mr[:, ci:ci + 1],
                         i_=sp[:, (delta // 512) * 512:
                               (delta // 512) * 512 + 512]:
                         e.reduce_max(o, i_, axis=AX.X),
                         reads=[spk], writes=[mrk],
                         tag=f"Smx{pair}_{h}_{kb}_{qc}")
                    if hb == 0:
                        p.op("vector", lambda e,
                             o=mr[:, 4:5], i_=sp[:, 512:1024:4]:
                             e.reduce_max(o, i_, axis=AX.X),
                             reads=[spk], writes=[mrk],
                             tag=f"Smx2{pair}_{h}_{kb}_{qc}")
                        p.op("vector", lambda e, mr=mr, ci=ci:
                             e.tensor_max(mr[:, ci:ci + 1],
                                          mr[:, ci:ci + 1], mr[:, 4:5]),
                             reads=[mrk], writes=[mrk],
                             tag=f"Smxc{pair}_{h}_{kb}_{qc}")
                else:
                    p.op("vector", lambda e, o=mr[:, ci:ci + 1],
                         i_=sp[:, 0:1024:4]:
                         e.reduce_max(o, i_, axis=AX.X),
                         reads=[spk], writes=[mrk],
                         tag=f"Smx{pair}_{h}_{kb}_{qc}")
                p.op("vector", lambda e, mr=mr, ci=ci: e.tensor_scalar_mul(
                    mr[:, 5:6], mr[:, ci:ci + 1], -1.0),
                    reads=[mrk], writes=[mrk],
                    tag=f"Sng{pair}_{h}_{kb}_{qc}")
                yield
                p.op("scalar", lambda e,
                     o=pt[:, qc * 1024 + lo: (qc + 1) * 1024],
                     i_=sp[:, lo:1024], bb=mr[:, 5:6],
                     a=mr[:, 8 + ci:9 + ci]:
                     e.activation(o, i_, AF.Exp, bias=bb, accum_out=a),
                     reads=[spk, mrk], writes=ptk + [mrk],
                     tag=f"Sex{pair}_{h}_{kb}_{qc}")
                yield
            if nch == 2:
                p.op("vector", lambda e, mr=mr: e.tensor_max(
                    mr[:, 6:7], mr[:, 0:1], mr[:, 1:2]),
                    reads=[mrk], writes=[mrk], tag=f"Scm{pair}_{h}_{kb}")
                p.op("vector", lambda e, mr=mr: e.tensor_scalar_mul(
                    mr[:, 7:8], mr[:, 6:7], -1.0),
                    reads=[mrk], writes=[mrk], tag=f"Scn{pair}_{h}_{kb}")
                p.op("scalar", lambda e, mr=mr: e.activation(
                    mr[:, 10:12], mr[:, 0:2], AF.Exp, bias=mr[:, 7:8]),
                    reads=[mrk], writes=[mrk], tag=f"Sfc{pair}_{h}_{kb}")
                p.op("vector", lambda e, mr=mr: e.tensor_mul(
                    mr[:, 12:14], mr[:, 8:10], mr[:, 10:12]),
                    reads=[mrk], writes=[mrk], tag=f"Sds{pair}_{h}_{kb}")
                p.op("vector", lambda e, mr=mr: e.tensor_add(
                    mr[:, 14:15], mr[:, 12:13], mr[:, 13:14]),
                    reads=[mrk], writes=[mrk], tag=f"Sda{pair}_{h}_{kb}")
                p.op("vector", lambda e, mr=mr: e.reciprocal(
                    mr[:, 15:16], mr[:, 14:15]),
                    reads=[mrk], writes=[mrk], tag=f"Sdr{pair}_{h}_{kb}")
                p.op("vector", lambda e, mr=mr: e.tensor_scalar_mul(
                    mr[:, 10:12], mr[:, 10:12], mr[:, 15:16]),
                    reads=[mrk], writes=[mrk], tag=f"Ssc{pair}_{h}_{kb}")
            else:
                p.op("vector", lambda e, mr=mr: e.reciprocal(
                    mr[:, 11:12], mr[:, 8:9]),
                    reads=[mrk], writes=[mrk], tag=f"Sdr{pair}_{h}_{kb}")
            yield
            for ci, qc in enumerate(range(c0, 2)):
                sccol = mr[:, 10 + ci:11 + ci] if nch == 2 \
                    else mr[:, 11:12]
                p.op("vector", lambda e, o=vnsh[hr][:, ts(ci, 64)],
                     i_=vnat[:, kb * 128 + h * 64:
                             kb * 128 + h * 64 + 64], sc=sccol:
                     e.tensor_scalar_mul(o, i_, sc),
                     reads=vnatk + [mrk], writes=[vnshk[hr]],
                     tag=f"vs{pair}_{h}_{kb}_{qc}")
                zp = C.pb[1 + qc]
                zpk = C.pbk[1 + qc]
                hb_c = (delta // 512) * 512 if qc == c0 else 0
                for th in range(hb_c // 512, 2):
                    tpos = (0, 64) if h == 1 else None
                    p.op("tensor", lambda e,
                         o=zp[hs, ts(th, 512)],
                         l=vnsh[hr][:, ts(ci, 64)],
                         r=pt[:, qc * 1024 + th * 512:
                              qc * 1024 + (th + 1) * 512],
                         s=(kb == 0), stt=(kb == 8 * qc + 4 * th + 3),
                         tp=tpos:
                         e.matmul(o, l, r, start=s, stop=stt,
                                  tile_position=tp),
                         reads=[vnshk[hr]] + ptk, writes=[zpk],
                         tag=f"PV{pair}_{h}_{kb}_{qc}_{th}")

        def l1_task(kb, h):
            l1_S_first(kb, h)
            yield
            for _ in l1_rest_stages(kb, h):
                yield

        gprev = None
        for kb in range(16):
            gA = l1_task(kb, 0)
            gB = l1_task(kb, 1)
            alive = [gA, gB]
            while alive:
                for g in list(alive):
                    try:
                        next(g)
                    except StopIteration:
                        alive.remove(g)
        for h in range(2):
            hs = slice(h * 64, (h + 1) * 64)
            for half in range(2):
                p.op("vector", lambda e,
                     o=ztb[hs, pair * 2048 + half * 1024:
                           pair * 2048 + half * 1024 + 1024],
                     i_=C.pb[1 + half][hs, 0:1024]:
                     e.tensor_copy(o, i_),
                     reads=[C.pbk[1 + half]], writes=ztbk,
                     tag=f"zev{pair}_{h}_{half}")
    if b == DEBUG_B:
        C.debug_dump("ztb_l1", ztb, ztbk)

    for fc in range(8):
        p.dma("sync", out=wosl, in_=Wo[ts(fc, 128), :],
              writes=woslk, tag=f"wo{fc}")
        for zc in range(8):
            bi = C.bank()
            p.op("tensor", lambda e, o=C.pb[bi][:, 0:128],
                 i_=wosl[:, ts(zc, 128)]:
                 e.transpose(o, i_, C.ident),
                 reads=woslk + [("ident",)], writes=[C.pbk[bi]],
                 tag=f"wotp{fc}_{zc}")
            emit_copy(C, "vector", wot[:, ts(zc, 128)],
                      C.pb[bi][:, 0:128], [C.pbk[bi]], wotk,
                      tag=f"wotev{fc}_{zc}")
        po = [C.pb[2 * (fc % 2)], C.pb[2 * (fc % 2) + 1]]
        pok = [C.pbk[2 * (fc % 2)], C.pbk[2 * (fc % 2) + 1]]
        for zc in range(8):
            for th in range(4):
                p.op("tensor", lambda e,
                     o=po[th // 2][:, (th % 2) * 512:(th % 2) * 512 + 512],
                     l=wot[:, ts(zc, 128)],
                     r=ztb[:, zc * 2048 + th * 512: zc * 2048 + (th + 1) * 512],
                     s=(zc == 0), st=(zc == 7):
                     e.matmul(o, l, r, start=s, stop=st),
                     reads=wotk + ztbk, writes=[pok[th // 2]],
                     tag=f"omm{fc}_{zc}_{th}")
        for half in range(2):
            p.op("scalar", lambda e, o=otmp[:, ts(half, 1024)],
                 i_=po[half][:, 0:1024], bb=bo[:, fc:fc + 1]:
                 e.activation(o, i_, AF.Identity, bias=bb),
                 reads=[pok[half], bok], writes=[otmpk],
                 tag=f"oev{fc}_{half}")
        for tk in range(16):
            bi = 2 * ((fc + 1) % 2) + (tk % 2)
            p.op("tensor", lambda e, o=C.pb[bi][:, 0:128],
                 i_=otmp[:, ts(tk, 128)]:
                 e.transpose(o, i_, C.ident),
                 reads=[otmpk, ("ident",)], writes=[C.pbk[bi]],
                 tag=f"otp{fc}_{tk}")
            p.op("vector", lambda e, o=C.x(tk)[:, ts(fc, 128)],
                 i_=C.pb[bi][:, 0:128]:
                 e.tensor_add(o, i_, o),
                 reads=[C.pbk[bi], C.xk(tk)], writes=[C.xk(tk)],
                 tag=f"oadd{fc}_{tk}")


PHASE_EMIT["l1attn"] = emit_l1attn


def emit_l0attn(C, b):
    p, scr = C.prog, C.scr
    scr.reset()
    ln = emit_ln(C, rms=False, nat=False)
    xnT, xnTk = ln["xnT"], ln["xnTk"]

    S_ZT, S_QT, S_KT, S_VR, S_VN, S_PT, S_WQKV, S_CP = \
        0, 16, 18, 20, 22, 24, 26, 29

    t24, t24k = scr.alloc(p, 128, F32, "t24")
    bqk, bqkk = scr.alloc(p, 40, F32, "bqkcols")
    bpr, bprk = scr.alloc(p, 8, F32, "bprcols")
    p.dma("sync", out=t24[0:24, :], in_=C.w["cattn_b_resh"], writes=[t24k],
          tag="battld")
    bi = C.bank()
    p.op("tensor", lambda e, o=C.pb[bi][:, 0:24]: e.transpose(
        o, t24[0:24, :], C.ident[0:24, 0:24]),
        reads=[t24k, ("ident",)], writes=[C.pbk[bi]], tag="battp")
    p.op("vector", lambda e, i_=C.pb[bi][:, 0:24]: e.tensor_copy(
        bqk[:, 0:24], i_), reads=[C.pbk[bi]], writes=[bqkk], tag="battev")
    p.op("vector", lambda e: e.tensor_scalar_mul(
        bqk[:, 24:32], bqk[:, 0:8], 0.125),
        reads=[bqkk], writes=[bqkk], tag="bqscale")
    p.dma("sync", out=t24[0:8, :], in_=C.w["cproj_b_resh"], writes=[t24k],
          tag="bprld")
    bi = C.bank()
    p.op("tensor", lambda e, o=C.pb[bi][:, 0:8]: e.transpose(
        o, t24[0:8, :], C.ident[0:8, 0:8]),
        reads=[t24k, ("ident",)], writes=[C.pbk[bi]], tag="bprtp")
    p.op("vector", lambda e, i_=C.pb[bi][:, 0:8]: e.tensor_copy(
        bpr[:, 0:8], i_), reads=[C.pbk[bi]], writes=[bprk], tag="bprev")

    lat, latk = scr.alloc(p, 4 * 64, BF16, "latw")
    wkc, wke = lat[:, 0:32], lat[0:32, 32:96]
    wvc, wve = lat[:, 96:128], lat[0:32, 128:192]
    p.dma("gpsimd", out=lat[0:64, 0:32], in_=C.w["L0_attn_wk_c"],
          writes=[latk], tag="wkc")
    p.dma("gpsimd", out=lat[64:128, 0:32], in_=C.w["L0_attn_wk_c"],
          writes=[latk], tag="wkc2")
    p.dma("gpsimd", out=lat[0:32, 32:96], in_=C.w["L0_attn_wk_e"],
          writes=[latk], tag="wke")
    p.dma("gpsimd", out=lat[0:64, 96:128], in_=C.w["L0_attn_wv_c"],
          writes=[latk], tag="wvc")
    p.dma("gpsimd", out=lat[64:128, 96:128], in_=C.w["L0_attn_wv_c"],
          writes=[latk], tag="wvc2")
    p.dma("gpsimd", out=lat[0:32, 128:192], in_=C.w["L0_attn_wv_e"],
          writes=[latk], tag="wve")

    zero, zerok = scr.alloc(p, 128, F32, "maskzero")
    p.op("vector", lambda e: e.memset(zero, 0.0), writes=[zerok],
         tag="l0maskzero")
    dmask, dmaskk = scr.alloc(p, 128, F32, "l0diagmask")
    p.op("gpsimd", lambda e: e.affine_select(
        dmask, zero, pattern=[[1, 128]], compare_op=ALU.is_ge,
        fill=-1e30, base=0, channel_multiplier=-1),
        reads=[zerok], writes=[dmaskk], tag="l0dmask")

    vf32, vf32k = scr.alloc(p, 2048, F32, "vf32")
    klatb, klatbk = scr.alloc(p, 2 * 512, BF16, "klat")
    otmp, otmpk = scr.alloc(p, 2048, F32, "otmp")
    mred, mredk = scr.alloc(p, 16, F32, "mred")
    znat, znatk = scr.alloc(p, 1024, F32, "znat")
    mrr, mrrk, vaur, vaurk = [], [], [], []
    for i in range(4):
        a_, k_ = scr.alloc(p, 8, F32, f"mrr{i}")
        mrr.append(a_)
        mrrk.append(k_)
        a_, k_ = scr.alloc(p, 80, BF16, f"vaur{i}")
        vaur.append(a_)
        vaurk.append(k_)

    qT, qTk = C.wa(S_QT, 2), [("wa", S_QT), ("wa", S_QT + 1)]
    kT, kTk = C.wa(S_KT, 2), [("wa", S_KT), ("wa", S_KT + 1)]
    vraw, vrawk = C.wa(S_VR, 2), [("wa", S_VR), ("wa", S_VR + 1)]
    vnat, vnatk = C.wa(S_VN, 2), [("wa", S_VN), ("wa", S_VN + 1)]
    ptr = [C.wa(S_PT, 1), C.wa(S_PT + 1, 1)]
    ptrk = [("wa", S_PT), ("wa", S_PT + 1)]
    wqkvT = [C.wa(S_WQKV + i, 1) for i in range(3)]
    wqkvTk = [("wa", S_WQKV + i) for i in range(3)]
    cpr, cprk = [C.wa(S_CP, 1), C.wa(S_CP + 1, 1)], \
        [("wa", S_CP), ("wa", S_CP + 1)]

    CW = C.w["L0_attn_c_attn_w"]
    CP = C.w["L0_attn_c_proj_w"]
    ztb = C.wa(S_ZT, 16)
    ztbk = [("wa", S_ZT + i) for i in range(16)]

    for pair in range(8):
        for c in range(3):
            for dc in range(8):
                p.dma("gpsimd", out=wqkvT[c][:, ts(dc, 128)],
                      in_=CW[ts(dc, 128), c * 1024 + pair * 128:
                             c * 1024 + (pair + 1) * 128],
                      writes=[wqkvTk[c]], tag=f"cw{pair}_{c}_{dc}")
        for c, outb, outk in ((0, qT, qTk), (1, kT, kTk),
                              (2, vraw, vrawk)):
            for g in range(2):
                bi = C.bank()
                for dc in range(8):
                    for th in range(2):
                        cb = g * 1024 + th * 512
                        p.op("tensor",
                             lambda e, o=C.pb[bi][:, ts(th, 512)],
                             l=wqkvT[c][:, ts(dc, 128)],
                             r=xnT[:, dc * SEQ + cb: dc * SEQ + cb + 512],
                             s=(dc == 0), st=(dc == 7):
                             e.matmul(o, l, r, start=s, stop=st),
                             reads=[wqkvTk[c], xnTk], writes=[C.pbk[bi]],
                             tag=f"aqkv{pair}_{c}_{g}")
                if c == 0:
                    bb = bqk[:, 24 + pair: 25 + pair]
                    sc = 0.125
                else:
                    bb = bqk[:, c * 8 + pair: c * 8 + pair + 1]
                    sc = 1.0
                p.op("scalar", lambda e, o=outb[:, ts(g, 1024)],
                     i_=C.pb[bi][:, 0:1024], bb=bb, sc=sc:
                     e.activation(o, i_, AF.Identity, bias=bb, scale=sc),
                     reads=[C.pbk[bi], bqkk], writes=outk,
                     tag=f"aqkvev{pair}_{c}_{g}")
        for g in range(4):
            gsl = slice(g * 512, (g + 1) * 512)
            for src_, wc, we, dstk, dst in ((kT, wkc, wke, kTk, kT),
                                            (vraw, wvc, wve, None, None)):
                for h in range(2):
                    hs = slice(h * 64, (h + 1) * 64)
                    bi = C.bank()
                    p.op("tensor", lambda e, o=C.pb[bi][0:32, 0:512],
                         l=wc[hs, :], r=src_[hs, gsl]:
                         e.matmul(o, l, r, start=True, stop=True),
                         reads=(kTk if src_ is kT else vrawk) + [latk],
                         writes=[C.pbk[bi]], tag=f"lat{pair}_{g}_{h}")
                    p.op("vector", lambda e, o=klatb[0:32, ts(h, 512)],
                         i_=C.pb[bi][0:32, 0:512]:
                         e.tensor_copy(o, i_),
                         reads=[C.pbk[bi]], writes=[klatbk],
                         tag=f"latev{pair}_{g}_{h}")
                for h in range(2):
                    hs = slice(h * 64, (h + 1) * 64)
                    bi = C.bank()
                    p.op("tensor", lambda e, o=C.pb[bi][0:64, 0:512],
                         l=we, r=klatb[0:32, ts(h, 512)]:
                         e.matmul(o, l, r, start=True, stop=True),
                         reads=[klatbk, latk], writes=[C.pbk[bi]],
                         tag=f"exp{pair}_{g}_{h}")
                    if src_ is kT:
                        p.op("vector", lambda e, o=kT[hs, gsl],
                             i_=C.pb[bi][0:64, 0:512]:
                             e.tensor_copy(o, i_),
                             reads=[C.pbk[bi]], writes=kTk,
                             tag=f"expev{pair}_{g}_{h}")
                    else:
                        p.op("vector", lambda e, o=vf32[hs, gsl],
                             i_=C.pb[bi][0:64, 0:512]:
                             e.tensor_copy(o, i_),
                             reads=[C.pbk[bi]], writes=[vf32k],
                             tag=f"vexpev{pair}_{g}_{h}")
        for kk in range(16):
            bi = C.bank()
            p.op("tensor", lambda e, o=C.pb[bi][:, 0:128],
                 i_=vf32[:, ts(kk, 128)]:
                 e.transpose(o, i_, C.ident),
                 reads=[vf32k, ("ident",)], writes=[C.pbk[bi]],
                 tag=f"avtp{pair}_{kk}")
            emit_copy(C, "vector", vnat[:, ts(kk, 128)],
                      C.pb[bi][:, 0:128], [C.pbk[bi]], vnatk,
                      tag=f"avev{pair}_{kk}")
        for h in range(2):
            hs = slice(h * 64, (h + 1) * 64)
            for zi2 in (1, 2):
                p.op("vector", lambda e, o=C.pb[zi2][:, 0:1024]:
                     e.memset(o, 0.0),
                     writes=[C.pbk[zi2]], tag=f"azclr{pair}_{h}_{zi2}")
            tasks = []
            ti = 0
            for kb in range(16):
                c0 = kb // 8
                delta = kb * 128 - c0 * 1024
                for qc in range(c0, 2):
                    on_diag = (qc == c0)
                    hb = (delta // 512) * 512 if on_diag else 0
                    tasks.append(dict(kb=kb, qc=qc, od=on_diag, hb=hb,
                                      lo=(delta if on_diag else 0),
                                      delta=delta,
                                      si=(0 if ti % 2 == 0 else 3),
                                      ri=ti))
                    ti += 1

            def emit_S(t):
                sp, spk = C.pb[t["si"]], C.pbk[t["si"]]
                for th in range(t["hb"] // 512, 2):
                    p.op("tensor", lambda e, o=sp[:, ts(th, 512)],
                         l=kT[hs, ts(t["kb"], 128)],
                         r=qT[hs, t["qc"] * 1024 + th * 512:
                              t["qc"] * 1024 + (th + 1) * 512]:
                         e.matmul(o, l, r, start=True, stop=True),
                         reads=kTk + qTk, writes=[spk],
                         tag=f"aS{pair}_{h}_{t['kb']}_{t['qc']}_{th}")

            def emit_pre(t):
                kb, qc, delta = t["kb"], t["qc"], t["delta"]
                sp, spk = C.pb[t["si"]], C.pbk[t["si"]]
                ri = t["ri"]
                mr, mrk = mrr[ri % 4], mrrk[ri % 4]
                if t["od"]:
                    p.op("vector", lambda e, mr=mr, o=sp[:, delta:delta + 128]:
                         e.reduce_max(mr[:, 0:1], o, axis=AX.X, negate=True),
                         reads=[spk], writes=[mrk],
                         tag=f"aSmx{pair}_{h}_{kb}_{qc}")
                    if delta + 128 < 1024:
                        p.op("vector", lambda e, mr=mr,
                             i_=sp[:, delta + 128:1024:4]:
                             e.reduce_max(mr[:, 1:2], i_, axis=AX.X,
                                          negate=True),
                             reads=[spk], writes=[mrk],
                             tag=f"aSmx2{pair}_{h}_{kb}_{qc}")
                        p.op("vector", lambda e, mr=mr: e.tensor_tensor(
                            mr[:, 2:3], mr[:, 0:1], mr[:, 1:2], ALU.min),
                            reads=[mrk], writes=[mrk],
                            tag=f"aSmc{pair}_{h}_{kb}_{qc}")
                        t["ng"] = 2
                    else:
                        t["ng"] = 0
                    p.op("vector", lambda e, o=sp[:, delta:delta + 128]:
                         e.tensor_add(o, o, dmask),
                         reads=[spk, dmaskk], writes=[spk],
                         tag=f"aSm{pair}_{h}_{kb}")
                else:
                    p.op("vector", lambda e, mr=mr, i_=sp[:, 0:1024:4]:
                         e.reduce_max(mr[:, 0:1], i_, axis=AX.X, negate=True),
                         reads=[spk], writes=[mrk],
                         tag=f"aSmx{pair}_{h}_{kb}_{qc}")
                    t["ng"] = 0

            def emit_post(t):
                kb, qc, delta, lo = t["kb"], t["qc"], t["delta"], t["lo"]
                sp, spk = C.pb[t["si"]], C.pbk[t["si"]]
                ri = t["ri"]
                mr, mrk = mrr[ri % 4], mrrk[ri % 4]
                va, vak = vaur[ri % 4], vaurk[ri % 4]
                pt, ptk = ptr[ri % 2], ptrk[ri % 2]
                negm = mr[:, t["ng"]:t["ng"] + 1]
                p.op("scalar", lambda e,
                     o=pt[:, lo:1024],
                     i_=sp[:, lo:1024], bb=negm:
                     e.activation(o, i_, AF.Exp, bias=bb),
                     reads=[spk, mrk], writes=[ptk],
                     tag=f"aSex{pair}_{h}_{kb}_{qc}")
                p.op("scalar", lambda e, mr=mr, i_=negm:
                     e.activation(mr[:, 4:5], i_, AF.Exp, scale=-1.0),
                     reads=[mrk], writes=[mrk],
                     tag=f"aSem{pair}_{h}_{kb}_{qc}")
                p.op("vector", lambda e, o=va[:, 64:65], mr=mr:
                     e.tensor_copy(o, mr[:, 4:5]),
                     reads=[mrk], writes=[vak],
                     tag=f"ave{pair}_{h}_{kb}_{qc}")
                p.op("vector", lambda e, o=va[:, 0:64],
                     i_=vnat[:, kb * 128 + h * 64:
                             kb * 128 + h * 64 + 64], mr=mr:
                     e.tensor_scalar_mul(o, i_, mr[:, 4:5]),
                     reads=vnatk + [mrk], writes=[vak],
                     tag=f"avs{pair}_{h}_{kb}_{qc}")
                for qt in range(max(kb, qc * 8), qc * 8 + 8):
                    zi = 1 if qt < 8 else 2
                    zcol = (qt % 4) * 128 + ((qt % 8) // 4) * 512
                    p.op("tensor", lambda e,
                         o=C.pb[zi][:, zcol:zcol + 65],
                         l=pt[:, (qt - qc * 8) * 128:
                               (qt - qc * 8 + 1) * 128],
                         r=va[:, 0:65]:
                         e.matmul(o, l, r, start=False, stop=False,
                                  skip_group_check=True),
                         reads=[ptk, vak], writes=[C.pbk[zi]],
                         tag=f"aPV{pair}_{h}_{kb}_{qt}")

            prev = None
            for t in tasks:
                emit_S(t)
                emit_pre(t)
                if prev is not None:
                    emit_post(prev)
                prev = t
            emit_post(prev)
            for qt in range(16):
                zi = 1 if qt < 8 else 2
                zcol = (qt % 4) * 128 + ((qt % 8) // 4) * 512
                p.op("vector", lambda e, i_=C.pb[zi][:, zcol + 64:zcol + 65]:
                     e.reciprocal(mred[:, 4:5], i_),
                     reads=[C.pbk[zi]], writes=[mredk],
                     tag=f"azr{pair}_{h}_{qt}")
                p.op("vector", lambda e, o=znat[:, ts(qt, 64)],
                     i_=C.pb[zi][:, zcol:zcol + 64]:
                     e.tensor_scalar_mul(o, i_, mred[:, 4:5]),
                     reads=[C.pbk[zi], mredk], writes=[znatk],
                     tag=f"azn{pair}_{h}_{qt}")
            for qt in range(16):
                si = 0 if (qt % 2 == 0) else 3
                p.op("tensor", lambda e, o=C.pb[si][0:64, 0:128],
                     i_=znat[:, ts(qt, 64)]:
                     e.transpose(o, i_, C.ident),
                     reads=[znatk, ("ident",)], writes=[C.pbk[si]],
                     tag=f"aztp{pair}_{h}_{qt}")
                p.op("vector", lambda e,
                     o=ztb[hs, pair * 2048 + qt * 128:
                           pair * 2048 + (qt + 1) * 128],
                     i_=C.pb[si][0:64, 0:128]:
                     e.tensor_copy(o, i_),
                     reads=[C.pbk[si]],
                     writes=[("wa", S_ZT + pair * 2 + qt // 8)],
                     tag=f"azev{pair}_{h}_{qt}")

    if b == DEBUG_B:
        C.debug_dump("ztb_l0", ztb, ztbk)
    for fc in range(8):
        po = [C.pb[2 * (fc % 2)], C.pb[2 * (fc % 2) + 1]]
        pok = [C.pbk[2 * (fc % 2)], C.pbk[2 * (fc % 2) + 1]]
        for zc in range(8):
            p.dma("gpsimd", out=cpr[zc % 2][:, 0:128],
                  in_=CP[ts(zc, 128), ts(fc, 128)],
                  writes=[cprk[zc % 2]], tag=f"acp{fc}_{zc}")
            for th in range(4):
                p.op("tensor", lambda e,
                     o=po[th // 2][:, (th % 2) * 512:(th % 2) * 512 + 512],
                     l=cpr[zc % 2][:, 0:128],
                     r=ztb[:, zc * 2048 + th * 512: zc * 2048 + (th + 1) * 512],
                     s=(zc == 0), st=(zc == 7):
                     e.matmul(o, l, r, start=s, stop=st),
                     reads=[cprk[zc % 2]] + ztbk, writes=[pok[th // 2]],
                     tag=f"aomm{fc}_{zc}_{th}")
        for half in range(2):
            p.op("scalar", lambda e, o=otmp[:, ts(half, 1024)],
                 i_=po[half][:, 0:1024], bb=bpr[:, fc:fc + 1]:
                 e.activation(o, i_, AF.Identity, bias=bb),
                 reads=[pok[half], bprk], writes=[otmpk],
                 tag=f"aoev{fc}_{half}")
        for tk in range(16):
            bi = 2 * ((fc + 1) % 2) + (tk % 2)
            p.op("tensor", lambda e, o=C.pb[bi][:, 0:128],
                 i_=otmp[:, ts(tk, 128)]:
                 e.transpose(o, i_, C.ident),
                 reads=[otmpk, ("ident",)], writes=[C.pbk[bi]],
                 tag=f"aotp{fc}_{tk}")
            p.op("vector", lambda e, o=C.x(tk)[:, ts(fc, 128)],
                 i_=C.pb[bi][:, 0:128]:
                 e.tensor_add(o, i_, o),
                 reads=[C.pbk[bi], C.xk(tk)], writes=[C.xk(tk)],
                 tag=f"aoadd{fc}_{tk}")


PHASE_EMIT["l0attn"] = emit_l0attn


def emit_l0ffn(C, b):
    p, scr = C.prog, C.scr
    scr.reset()

    S_NAT, S_W1, S_W2 = 0, 16, 24
    natk = [("wa", S_NAT + i) for i in range(16)]
    nat = C.wa(S_NAT, 16)
    w1b = C.wa(S_W1, 8)
    w2b = C.wa(S_W2, 8)
    w1qk = [[("wa", S_W1 + 2 * q), ("wa", S_W1 + 2 * q + 1)]
            for q in range(4)]
    w2qk = [[("wa", S_W2 + 2 * q), ("wa", S_W2 + 2 * q + 1)]
            for q in range(4)]

    pk3, pk3k = scr.alloc(p, 2048, F32, "pack3")
    rbc, rbck = scr.alloc(p, 2048, F32, "rank_bcast")
    rkc, rkck = scr.alloc(p, 256, F32, "rank_cols")
    wcl, wclk = scr.alloc(p, 256, BF16, "w_cols")
    ewc, ewck = scr.alloc(p, 64, F32, "ew_top8")
    mark0 = scr.mark()
    pk1, pk1k = scr.alloc(p, 2048, F32, "pack1")
    pk2, pk2k = scr.alloc(p, 2048, F32, "pack2")

    logits = pk1[0:16, :]
    zap = pk2[0:16, :]
    cum3 = pk3[0:16, :]
    top8 = ewc[0:16, 0:8]
    ew2 = ewc[:, 16:18]

    ln = emit_ln(C, rms=False, nat=True, nat_ap=nat, nat_keys=natk)
    xnT, xnTk = ln["xnT"], ln["xnTk"]

    rw1, rw1k = scr.alloc(p, 1024, BF16, "rw1")
    rw2, rw2k = scr.alloc(p, 16, BF16, "rw2")
    r1T, r1Tk = scr.alloc(p, 512, BF16, "r1T")
    for dc in range(8):
        p.dma("gpsimd", out=rw1[:, ts(dc, 128)],
              in_=C.w["L0_ffn_router_w1"][ts(dc, 128), :],
              writes=[rw1k], tag=f"rw1{dc}")
    p.dma("gpsimd", out=rw2, in_=C.w["L0_ffn_router_w2"],
          writes=[rw2k], tag="rw2")
    for tc in range(4):
        bi = C.bank()
        for dc in range(8):
            p.op("tensor", lambda e, o=C.pb[bi][:, 0:512],
                 l=rw1[:, ts(dc, 128)],
                 r=xnT[:, dc * SEQ + tc * 512: dc * SEQ + (tc + 1) * 512],
                 s=(dc == 0), st=(dc == 7):
                 e.matmul(o, l, r, start=s, stop=st),
                 reads=[rw1k, xnTk], writes=[C.pbk[bi]], tag=f"r1{tc}")
        p.op("scalar", lambda e, o=r1T,
             i_=C.pb[bi][:, 0:512]:
             e.activation(o, i_, AF.Silu),
             reads=[C.pbk[bi]], writes=[r1Tk], tag=f"r1ev{tc}")
        bj = C.bank()
        p.op("tensor", lambda e, o=C.pb[bj][0:16, 0:512],
             l=rw2, r=r1T:
             e.matmul(o, l, r, start=True, stop=True),
             reads=[rw2k, r1Tk], writes=[C.pbk[bj]], tag=f"r2{tc}")
        p.op("vector", lambda e, o=logits[:, ts(tc, 512)],
             i_=C.pb[bj][0:16, 0:512]:
             e.tensor_copy(o, i_),
             reads=[C.pbk[bj]], writes=[pk1k], tag=f"r2ev{tc}")

    p.op("vector", lambda e: e.reduce_max(
        ewc[0:16, 8:9], logits, axis=AX.X, negate=True),
        reads=[pk1k], writes=[ewck], tag="smax")
    p.op("scalar", lambda e: e.activation(
        logits, logits, AF.Exp, bias=ewc[0:16, 8:9],
        accum_out=ewc[0:16, 9:10]),
        reads=[pk1k, ewck], writes=[pk1k, ewck], tag="sexp")
    p.op("vector", lambda e: e.reciprocal(ewc[0:16, 10:11],
                                          ewc[0:16, 9:10]),
         reads=[ewck], writes=[ewck], tag="srec")
    p.op("vector", lambda e: e.tensor_scalar_mul(
        logits, logits, ewc[0:16, 10:11]),
        reads=[pk1k, ewck], writes=[pk1k], tag="snorm")
    p.op("vector", lambda e: e.tensor_copy(zap, logits),
         reads=[pk1k], writes=[pk2k], tag="zap0")
    for r in range(32):
        p.op("vector", lambda e: e.max(out=ewc[0:16, 0:8], in_=zap),
             reads=[pk2k], writes=[ewck], tag=f"top8_{r}")
        p.op("vector", lambda e: e.match_replace(
            out=zap, in_to_replace=ewc[0:16, 0:8], in_values=zap,
            imm_value=0.0),
            reads=[pk2k, ewck], writes=[pk2k], tag=f"zap_{r}")
    p.op("vector", lambda e: e.tensor_sub(logits, logits, zap),
         reads=[pk1k, pk2k], writes=[pk1k], tag="psel")
    p.op("vector", lambda e: e.reduce_sum(ewc[0:16, 11:12], logits,
                                          axis=AX.X),
         reads=[pk1k], writes=[ewck], tag="ssum")
    p.op("vector", lambda e: e.tensor_scalar_add(
        ewc[0:16, 12:13], ewc[0:16, 11:12], 1e-9),
        reads=[ewck], writes=[ewck], tag="ssum2")
    p.op("vector", lambda e: e.reciprocal(ewc[0:16, 13:14],
                                          ewc[0:16, 12:13]),
         reads=[ewck], writes=[ewck], tag="srs")
    p.op("vector", lambda e: e.tensor_single_scalar(
        zap, logits, 0.0, ALU.is_gt),
        reads=[pk1k], writes=[pk2k], tag="mask0")
    pr = [(zap, pk2k), (cum3, pk3k)]
    src_i = 0
    for k in range(11):
        s = 1 << k
        aa, ak = pr[src_i]
        bb, bk = pr[1 - src_i]
        p.op("vector", lambda e, aa=aa, bb=bb, s=s: e.tensor_copy(
            bb[:, 0:s], aa[:, 0:s]),
            reads=[ak], writes=[bk], tag=f"cumc{k}")
        p.op("vector", lambda e, aa=aa, bb=bb, s=s: e.tensor_add(
            bb[:, s:2048], aa[:, s:2048], aa[:, 0:2048 - s]),
            reads=[ak], writes=[bk], tag=f"cuma{k}")
        src_i = 1 - src_i
    assert src_i == 1
    p.op("vector", lambda e: e.tensor_single_scalar(
        zap, logits, 0.0, ALU.is_gt),
        reads=[pk1k], writes=[pk2k], tag="mask2")
    p.op("vector", lambda e: e.tensor_mul(cum3, cum3, zap),
         reads=[pk3k, pk2k], writes=[pk3k], tag="rankm")
    p.op("vector", lambda e: e.tensor_scalar_sub(cum3, cum3, 1.0),
         reads=[pk3k], writes=[pk3k], tag="rank1")
    rankp = cum3
    p.op("vector", lambda e: e.tensor_scalar_mul(
        logits, logits, ewc[0:16, 13:14]),
        reads=[pk1k, ewck], writes=[pk1k], tag="pseln")
    pseln = logits
    for t in range(NT):
        bi = C.bank()
        p.op("tensor", lambda e, o=C.pb[bi][:, 0:16],
             i_=rankp[:, ts(t, 128)]:
             e.transpose(o, i_, C.ident[0:16, 0:16]),
             reads=[pk3k, ("ident",)], writes=[C.pbk[bi]], tag=f"rktp{t}")
        p.op("vector", lambda e, o=rkc[:, ts(t, 16)],
             i_=C.pb[bi][:, 0:16]:
             e.tensor_copy(o, i_),
             reads=[C.pbk[bi]], writes=[rkck], tag=f"rkev{t}")
        bj = C.bank()
        p.op("tensor", lambda e, o=C.pb[bj][:, 0:16],
             i_=pseln[:, ts(t, 128)]:
             e.transpose(o, i_, C.ident[0:16, 0:16]),
             reads=[pk1k, ("ident",)], writes=[C.pbk[bj]], tag=f"wctp{t}")
        p.op("vector", lambda e, o=wcl[:, ts(t, 16)],
             i_=C.pb[bj][:, 0:16]:
             e.tensor_copy(o, i_),
             reads=[C.pbk[bj]], writes=[wclk], tag=f"wcev{t}")

    scr.pop(mark0)
    einT, einTk = scr.alloc(p, 2048, BF16, "einT")
    hbuf, hk = scr.alloc(p, 16 * 256, BF16, "hbuf")
    eoT, eoTk = scr.alloc(p, 1024, F32, "eoT")
    sring, sringk = scr.alloc(p, 2 * 256, BF16, "silu_ring")
    ones_f, ones_fk = scr.alloc(p, 128, F32, "ones_f")
    selb, selbk = [], []
    for cc in range(2):
        a_, k_ = scr.alloc(p, 2048, BF16, f"sel{cc}")
        selb.append(a_)
        selbk.append([k_])
    selT, selTk0 = scr.alloc(p, 4096, BF16, "selT")
    selTk = [selTk0] * 4
    eon, eonk = [], []
    for cc in range(2):
        a_, k_ = scr.alloc(p, 1024, BF16, f"eon{cc}")
        eon.append(a_)
        eonk.append(k_)
    w3r, w3rk = [], []
    for i in range(2):
        a_, k_ = scr.alloc(p, 4096, BF16, f"w3r{i}")
        w3r.append(a_)
        w3rk.append(k_)
    p.op("vector", lambda e: e.memset(ones_f[0:1, :], 1.0),
         writes=[ones_fk], tag="onesf")

    w1h, w2h, w3h = C.w["L0_ffn_w1"], C.w["L0_ffn_w2"], C.w["L0_ffn_w3"]

    for e_ in range(16):
        for t in range(NT):
            p.op("vector", lambda e, o=selT[:, ts(t, 256)],
                 rc=rkc[:, t * 16 + e_: t * 16 + e_ + 1]:
                 e.tensor_single_scalar(o, C.iota_row, rc, ALU.is_equal),
                 reads=[rkck, ("iota",)], writes=[selTk[t // 4]],
                 tag=f"selT{e_}_{t}")
        p.dma("gpsimd", out=rbc[0:1, :], in_=cum3[e_:e_ + 1, :],
              reads=[pk3k], writes=[rbck], tag=f"rrow{e_}")
        for g in range(4):
            bi = C.bank()
            p.op("tensor", lambda e, o=C.pb[bi][:, 0:512],
                 l=ones_f[0:1, :], r=rbc[0:1, ts(g, 512)]:
                 e.matmul(o, l, r, start=True, stop=True),
                 reads=[ones_fk, rbck], writes=[C.pbk[bi]],
                 tag=f"rbc{e_}_{g}")
            p.op("vector", lambda e, o=rbc[:, ts(g, 512)],
                 i_=C.pb[bi][:, 0:512]:
                 e.tensor_copy(o, i_),
                 reads=[C.pbk[bi]], writes=[rbck], tag=f"rbcev{e_}_{g}")
        for cc in range(2):
            p.op("vector", lambda e, o=selb[cc][:, 0:2048],
                 ic=(C.iota_c0 if cc == 0 else C.iota_c1):
                 e.tensor_single_scalar(o, rbc, ic, ALU.is_equal),
                 reads=[rbck, ("iota",)], writes=selbk[cc],
                 tag=f"sel{e_}_{cc}")
        for cc in range(2):
            bi = C.bank()
            for t in range(NT):
                p.op("tensor", lambda e, o=C.pb[bi][:, 0:1],
                     l=selT[:, t * 256 + cc * 128: t * 256 + cc * 128 + 128],
                     r=wcl[:, t * 16 + e_: t * 16 + e_ + 1],
                     s=(t == 0), st=(t == NT - 1):
                     e.matmul(o, l, r, start=s, stop=st),
                     reads=[selTk[t // 4], wclk], writes=[C.pbk[bi]],
                     tag=f"ew{e_}_{cc}_{t}")
            p.op("vector", lambda e, o=ew2[:, cc:cc + 1],
                 i_=C.pb[bi][:, 0:1]:
                 e.tensor_copy(o, i_),
                 reads=[C.pbk[bi]], writes=[ewck], tag=f"ewev{e_}_{cc}")
        for dc in range(8):
            bi = C.bank()
            for t in range(NT):
                p.op("tensor", lambda e, o=C.pb[bi][:, 0:256],
                     l=nat[:, t * 1024 + dc * 128: t * 1024 + (dc + 1) * 128],
                     r=selT[:, ts(t, 256)],
                     s=(t == 0), st=(t == NT - 1):
                     e.matmul(o, l, r, start=s, stop=st),
                     reads=[natk[t], selTk[t // 4]], writes=[C.pbk[bi]],
                     tag=f"gth{e_}_{dc}_{t}")
            p.op("vector", lambda e, o=einT[:, ts(dc, 256)],
                 i_=C.pb[bi][:, 0:256]:
                 e.tensor_copy(o, i_),
                 reads=[C.pbk[bi]], writes=[einTk], tag=f"gthev{e_}_{dc}")
        for half in range(2):
            for q in range(4):
                p.dma("gpsimd",
                      out=w1b[:, q * 2048:(q + 1) * 2048].rearrange(
                          "p (c d) -> p c d", d=1024),
                      in_=w1h[e_, q * 256:(q + 1) * 256,
                              half * 1024:(half + 1) * 1024].rearrange(
                          "(c p) d -> p c d", p=128),
                      writes=w1qk[q], tag=f"w1{e_}_{half}_{q}")
                p.dma("gpsimd",
                      out=w2b[:, q * 2048:(q + 1) * 2048].rearrange(
                          "p (c d) -> p c d", d=1024),
                      in_=w2h[e_, q * 256:(q + 1) * 256,
                              half * 1024:(half + 1) * 1024].rearrange(
                          "(c p) d -> p c d", p=128),
                      writes=w2qk[q], tag=f"w2{e_}_{half}_{q}")
            for dc in range(8):
                for df in range(8):
                    pg = C.pb[df // 4]
                    pv = C.pb[2 + df // 4]
                    po = (df % 4) * 256
                    p.op("tensor", lambda e, o=pg[:, po:po + 256],
                         l=w1b[:, dc * 1024 + df * 128:
                               dc * 1024 + (df + 1) * 128],
                         r=einT[:, ts(dc, 256)],
                         s=(dc == 0), st=(dc == 7):
                         e.matmul(o, l, r, start=s, stop=st),
                         reads=w1qk[dc // 2] + [einTk],
                         writes=[C.pbk[df // 4]], tag=f"g{e_}_{half}_{df}")
                    p.op("tensor", lambda e, o=pv[:, po:po + 256],
                         l=w2b[:, dc * 1024 + df * 128:
                               dc * 1024 + (df + 1) * 128],
                         r=einT[:, ts(dc, 256)],
                         s=(dc == 0), st=(dc == 7):
                         e.matmul(o, l, r, start=s, stop=st),
                         reads=w2qk[dc // 2] + [einTk],
                         writes=[C.pbk[2 + df // 4]],
                         tag=f"v{e_}_{half}_{df}")
            for df in range(8):
                pg = C.pb[df // 4]
                pv = C.pb[2 + df // 4]
                po = (df % 4) * 256
                p.op("scalar", lambda e, o=sring[:, ts(df % 2, 256)],
                     i_=pg[:, po:po + 256]:
                     e.activation(o, i_, AF.Silu),
                     reads=[C.pbk[df // 4]], writes=[sringk],
                     tag=f"silu{e_}_{half}_{df}")
                p.op("vector", lambda e,
                     o=hbuf[:, (half * 8 + df) * 256:
                            (half * 8 + df + 1) * 256],
                     i_=pv[:, po:po + 256], g_=sring[:, ts(df % 2, 256)]:
                     e.tensor_mul(o, i_, g_),
                     reads=[C.pbk[2 + df // 4], sringk], writes=[hk],
                     tag=f"h{e_}_{half}_{df}")
        for dh in range(2):
            for P in range(2):
                p.dma("gpsimd",
                      out=w3r[P].rearrange("p (c d) -> p c d", d=512),
                      in_=w3h[e_, P * 1024:(P + 1) * 1024,
                              dh * 512:(dh + 1) * 512].rearrange(
                          "(c p) d -> p c d", p=128),
                      writes=[w3rk[P]], tag=f"w3{e_}_{dh}_{P}")
            for dffc in range(16):
                for dq in range(4):
                    pq = C.pb[dq // 2]
                    po = (dq % 2) * 256
                    p.op("tensor", lambda e, o=pq[:, po:po + 256],
                         l=w3r[dffc // 8][:, (dffc % 8) * 512 + dq * 128:
                                          (dffc % 8) * 512 + (dq + 1) * 128],
                         r=hbuf[:, ts(dffc, 256)],
                         s=(dffc == 0), st=(dffc == 15):
                         e.matmul(o, l, r, start=s, stop=st),
                         reads=[w3rk[dffc // 8], hk],
                         writes=[C.pbk[dq // 2]], tag=f"w3m{e_}_{dh}_{dq}")
            for dq in range(4):
                pq = C.pb[dq // 2]
                po = (dq % 2) * 256
                p.op("vector", lambda e, o=eoT[:, ts(dq, 256)],
                     i_=pq[:, po:po + 256]:
                     e.tensor_copy(o, i_),
                     reads=[C.pbk[dq // 2]], writes=[eoTk],
                     tag=f"w3ev{e_}_{dh}_{dq}")
            for dq in range(4):
                for cc in range(2):
                    bi = 2 + (dq % 2)
                    p.op("tensor", lambda e, o=C.pb[bi][:, 512:640],
                         i_=eoT[:, dq * 256 + cc * 128:
                                dq * 256 + (cc + 1) * 128]:
                         e.transpose(o, i_, C.ident),
                         reads=[eoTk, ("ident",)], writes=[C.pbk[bi]],
                         tag=f"eontp{e_}_{dh}_{dq}_{cc}")
                    dcol = dh * 512 + dq * 128
                    p.op("vector", lambda e,
                         o=eon[cc][:, dcol:dcol + 128],
                         i_=C.pb[bi][:, 512:640],
                         sc=ew2[:, cc:cc + 1]:
                         e.tensor_scalar_mul(o, i_, sc),
                         reads=[C.pbk[bi], ewck], writes=[eonk[cc]],
                         tag=f"eonev{e_}_{dh}_{dq}_{cc}")
        for t in range(NT):
            bi = 2 + (t % 2)
            for cc in range(2):
                for dh in range(2):
                    p.op("tensor", lambda e,
                         o=C.pb[bi][:, ts(dh, 512)],
                         l=selb[cc][:, ts(t, 128)],
                         r=eon[cc][:, ts(dh, 512)],
                         s=(cc == 0), st=(cc == 1):
                         e.matmul(o, l, r, start=s, stop=st),
                         reads=selbk[cc] + [eonk[cc]],
                         writes=[C.pbk[bi]], tag=f"sc{e_}_{t}_{cc}_{dh}")
            p.op("vector", lambda e, o=C.x(t), i_=C.pb[bi][:, 0:1024]:
                 e.tensor_add(o, i_, o),
                 reads=[C.pbk[bi], C.xk(t)], writes=[C.xk(t)],
                 tag=f"scadd{e_}_{t}")


PHASE_EMIT["l0ffn"] = emit_l0ffn


# version 09
# speedup vs baseline: 1.1979x; 1.0330x over previous
import numpy as np
import ml_dtypes
import concourse.bass as bass
import concourse.mybir as mybir
from concourse.bass_utils import run_bass_kernel_spmd

F32 = mybir.dt.float32
BF16 = mybir.dt.bfloat16
AF = mybir.ActivationFunctionType
ALU = mybir.AluOpType
AX = mybir.AxisListType

ENGINES = ("tensor", "vector", "scalar", "gpsimd", "sync")
QUEUES = ("sync", "gpsimd", "scalar")

DEBUG = False
DEBUG_B = 0
NT = 16
D = 1024
SEQ = 2048
NB = 2



class Op:
    __slots__ = ("idx", "engine", "fn", "deps", "is_dma", "queue", "inc_no",
                 "needed", "guar", "waits", "tag")

    def __init__(self, idx, engine, fn, is_dma=False, queue=None, tag=""):
        self.idx = idx
        self.engine = engine
        self.fn = fn
        self.deps = []
        self.is_dma = is_dma
        self.queue = queue
        self.inc_no = None
        self.needed = is_dma
        self.guar = None
        self.waits = None
        self.tag = tag


class Prog:

    def __init__(self, nc):
        self.nc = nc
        self.ops = []
        self.last_writer = {}
        self.readers = {}
        self.alias_pending = {}

    def alias(self, new_key, old_keys):
        self.alias_pending.setdefault(new_key, []).extend(old_keys)

    def _dep(self, op, d):
        if d is op:
            return
        if (d.engine == "tensor" and op.engine == "tensor"
                and not d.is_dma and not op.is_dma):
            return
        op.deps.append(d)

    def _absorb_alias(self, op, key):
        olds = self.alias_pending.pop(key, None)
        if olds:
            for ok in olds:
                lw = self.last_writer.get(ok)
                if lw is not None:
                    self._dep(op, lw)
                for r in self.readers.get(ok, ()):
                    self._dep(op, r)

    def op(self, engine, fn, reads=(), writes=(), tag=""):
        o = Op(len(self.ops), engine, fn, tag=tag)
        self._wire(o, reads, writes)
        self.ops.append(o)
        return o

    def dma(self, queue, out, in_, reads=(), writes=(), tag="", **kw):
        def fn(eng, out=out, in_=in_, kw=kw):
            return eng.dma_start(out=out, in_=in_, **kw)
        o = Op(len(self.ops), queue, fn, is_dma=True, queue=queue, tag=tag)
        self._wire(o, reads, writes)
        self.ops.append(o)
        return o

    def _wire(self, o, reads, writes):
        for r in reads:
            self._absorb_alias(o, r)
            lw = self.last_writer.get(r)
            if lw is not None:
                self._dep(o, lw)
        for w in writes:
            self._absorb_alias(o, w)
            lw = self.last_writer.get(w)
            if lw is not None:
                self._dep(o, lw)
            for rd in self.readers.get(w, ()):
                self._dep(o, rd)
        for w in writes:
            self.last_writer[w] = o
            self.readers[w] = []
        for r in reads:
            self.readers.setdefault(r, []).append(o)

    def finalize_and_emit(self, block):
        for o in self.ops:
            for d in o.deps:
                d.needed = True
        counts = {}
        for o in self.ops:
            if o.needed:
                grp = ("q", o.queue) if o.is_dma else ("e", o.engine)
                step = 16 if o.is_dma else 1
                counts[grp] = counts.get(grp, 0) + step
                o.inc_no = counts[grp]
        self.final_counts = counts
        for o in self.ops:
            need = {}
            for d in o.deps:
                grp = ("q", d.queue) if d.is_dma else ("e", d.engine)
                v = d.inc_no
                if v is not None and v > need.get(grp, 0):
                    need[grp] = v
                if d.guar:
                    for g, gv in d.guar.items():
                        if gv > need.get(g, 0):
                            need[g] = gv
            o.waits = need
            g = dict(need)
            if o.inc_no is not None:
                grp = ("q", o.queue) if o.is_dma else ("e", o.engine)
                if o.inc_no > g.get(grp, 0):
                    g[grp] = o.inc_no
            o.guar = g

        sems = {}
        self._sem_cms = []
        for e in ("tensor", "vector", "scalar", "gpsimd"):
            cm = self.nc.semaphore(f"sem_{e}")
            sems[("e", e)] = cm.__enter__()
            self._sem_cms.append(cm)
        for q in QUEUES:
            cm = self.nc.semaphore(f"q_{q}")
            sems[("q", q)] = cm.__enter__()
            self._sem_cms.append(cm)
        self.sems = sems

        per_engine = {e: [] for e in ENGINES}
        for o in self.ops:
            per_engine[o.engine].append(o)
        self.n_ops = {e: len(v) for e, v in per_engine.items()}

        def make_body(engine, ops):
            def body(eng):
                frontier = {}
                for o in ops:
                    for grp, v in sorted(o.waits.items()):
                        if frontier.get(grp, 0) >= v:
                            continue
                        eng.wait_ge(sems[grp], v)
                        frontier[grp] = v
                    inst = o.fn(eng)
                    if o.inc_no is not None:
                        step = 16 if o.is_dma else 1
                        grp = ("q", o.queue) if o.is_dma else ("e", o.engine)
                        inst.then_inc(sems[grp], step)
            return body

        final = dict(self.final_counts)

        def body_sync(eng):
            make_body("sync", per_engine["sync"])(eng)
            for grp, v in sorted(final.items()):
                if grp != ("q", "sync"):
                    eng.wait_ge(sems[grp], v)
            if ("q", "sync") in final:
                eng.wait_ge(sems[("q", "sync")], final[("q", "sync")])

        block.sync(body_sync)
        for e, mk in (("tensor", block.tensor), ("vector", block.vector),
                      ("scalar", block.scalar), ("gpsimd", block.gpsimd)):
            if per_engine[e]:
                mk(make_body(e, per_engine[e]))


class Arena:

    def __init__(self, nc, name, bytes_pp):
        self.t = nc.alloc_sbuf_tensor(name, [128, bytes_pp // 2], BF16)
        self.name = name
        self.bytes = bytes_pp
        self.pos = 0
        self.live = []
        self.dead = []
        self.n = 0

    def alloc(self, prog, nelem, dtype, tag=""):
        dsz = 4 if dtype == F32 else 2
        nbytes = (nelem * dsz + 63) // 64 * 64
        assert self.pos + nbytes <= self.bytes, (
            f"arena {self.name}: {self.pos}+{nbytes}>{self.bytes} ({tag})")
        start, end = self.pos, self.pos + nbytes
        self.pos = end
        key = (self.name, self.n, tag)
        self.n += 1
        olds = [k for (s, e, k) in self.dead if s < end and e > start]
        if olds:
            prog.alias(key, olds)
            self.dead = [(s, e, k) for (s, e, k) in self.dead
                         if not (start <= s and e <= end)]
        self.live.append((start, end, key))
        ap = self.t.ap()[:, start // 2: start // 2 + nelem * dsz // 2]
        if dtype == F32:
            ap = ap.bitcast(F32)
        return ap, key

    def reset(self):
        self.dead.extend(self.live)
        self.live = []
        self.pos = 0

    def mark(self):
        return self.pos

    def pop(self, mark):
        keep = [x for x in self.live if x[0] < mark]
        retire = [x for x in self.live if x[0] >= mark]
        self.dead.extend(retire)
        self.live = keep
        self.pos = mark


def ts(i, w):
    return slice(i * w, (i + 1) * w)



EPS_LN = 1e-5
EPS_RMS = 1.1920929e-07


class Ctx:
    def __init__(self, nc, prog):
        self.nc = nc
        self.prog = prog
        self.x_t = nc.alloc_sbuf_tensor("x_res", [128, NT * D], F32)
        self.wa_t = nc.alloc_sbuf_tensor("warena", [128, 32 * 1024], BF16)
        self.scr = Arena(nc, "scratch", 78 * 1024)
        self.pb_t = [nc.alloc_psum_tensor(f"pb{i}", [128, 1024], F32)
                     for i in range(4)]
        self.pb = [t.ap() for t in self.pb_t]
        self.pbk = [("pb", i) for i in range(4)]
        self.ident_t = nc.alloc_sbuf_tensor("ident", [128, 128], F32)
        self.ident = self.ident_t.ap()
        self.ones_t = nc.alloc_sbuf_tensor("ones_bf", [1, 128], BF16)
        self.ones = self.ones_t.ap()
        self.iota_t = nc.alloc_sbuf_tensor("iotas", [128, 260], F32)
        self.cpool_t = nc.alloc_sbuf_tensor("cpool", [128, 8], F32)
        self._consts = {}
        self._bank = 0
        self.w = {}
        self.wt = {}

    def x(self, t):
        return self.x_t.ap()[:, ts(t, D)]

    def xk(self, t):
        return ("x", t)

    def wa(self, slot, n=1):
        return self.wa_t.ap()[:, slot * 1024:(slot + n) * 1024]

    def wa_f32(self, slot, n=1):
        return self.wa_t.ap()[:, slot * 1024:(slot + n) * 1024].bitcast(F32)

    def wak(self, slot, n=1):
        return [("wa", s) for s in range(slot, slot + n)]

    def bank(self):
        i = self._bank % 4
        self._bank += 1
        return i

    def debug_dump(self, name, ap, reads, dtype=None):
        if not DEBUG:
            return
        nelem = 1
        shp = list(ap.shape)
        t = self.nc.dram_tensor("dbg_" + name, shp, ap.dtype,
                                kind="ExternalOutput")
        self.prog.dma("sync", out=t.ap(), in_=ap, reads=reads,
                      tag="dbg_" + name)

    def constf(self, v):
        if v not in self._consts:
            i = len(self._consts)
            assert i < 8
            ap = self.cpool_t.ap()[:, i:i + 1]
            self.prog.op("vector", lambda e, a=ap, vv=v: e.memset(a, vv),
                         writes=[("cpool",)], tag=f"const{i}")
            self._consts[v] = ap
        return self._consts[v]

    def emit_consts(self):
        p = self.prog
        p.dma("sync", out=self.ident, in_=self.w["ident_f32"],
              writes=[("ident",)], tag="ident")
        p.op("vector", lambda e: e.memset(self.ones, 1.0),
             writes=[("ones",)], tag="ones")
        it = self.iota_t.ap()
        p.op("gpsimd", lambda e: e.iota(
            it[:, 0:256], pattern=[[1, 256]], base=0, channel_multiplier=0,
            allow_small_or_imprecise_dtypes=True),
            writes=[("iota",)], tag="iota_row")
        p.op("gpsimd", lambda e: e.iota(
            it[:, 256:257], pattern=[[1, 1]], base=0, channel_multiplier=1,
            allow_small_or_imprecise_dtypes=True),
            writes=[("iota",)], tag="iota_c0")
        p.op("gpsimd", lambda e: e.iota(
            it[:, 257:258], pattern=[[1, 1]], base=128, channel_multiplier=1,
            allow_small_or_imprecise_dtypes=True),
            writes=[("iota",)], tag="iota_c1")

    @property
    def iota_row(self):
        return self.iota_t.ap()[:, 0:256]

    @property
    def iota_c0(self):
        return self.iota_t.ap()[:, 256:257]

    @property
    def iota_c1(self):
        return self.iota_t.ap()[:, 257:258]


def emit_ln(C, rms=False, nat=False, nat_ap=None, nat_keys=None):
    p, scr = C.prog, C.scr
    st, stk = scr.alloc(p, 128, F32, "ln_stats")
    s_sum, s_sq = st[:, 0:16], st[:, 16:32]
    mean, sval = st[:, 32:48], st[:, 48:64]
    t1, t2, t3 = st[:, 64:80], st[:, 80:96], st[:, 96:112]
    xnf = []
    for i in range(2):
        a, k = scr.alloc(p, D, F32, f"xnf{i}")
        xnf.append((a, k))
    xnT, xnTk = scr.alloc(p, 8 * SEQ, BF16, "xnT")
    out = {"xnT": xnT, "xnTk": xnTk}
    if nat:
        if nat_ap is not None:
            out["nat"] = nat_ap
            out["natks"] = nat_keys
        else:
            xnat, xnatk = scr.alloc(p, NT * D, BF16, "xn_nat")
            out["nat"] = xnat
            out["natks"] = [xnatk]

    for t in range(NT):
        xt = C.x(t)
        p.op("vector", lambda e, o=s_sum[:, t:t + 1], i_=xt:
             e.reduce_sum(o, i_, axis=AX.X),
             reads=[C.xk(t)], writes=[stk], tag=f"ln_sum{t}")
        dump = xnf[t % 2][0]
        p.op("scalar", lambda e, o=dump, i_=xt, a=s_sq[:, t:t + 1]:
             e.activation(o, i_, AF.Square, accum_out=a),
             reads=[C.xk(t)], writes=[stk, xnf[t % 2][1]], tag=f"ln_sq{t}")
    k = 1.0 / D
    p.op("vector", lambda e: e.tensor_scalar_mul(mean, s_sum, k),
         reads=[stk], writes=[stk], tag="ln_mean")
    p.op("vector", lambda e: e.tensor_scalar_mul(t1, s_sq, k),
         reads=[stk], writes=[stk], tag="ln_ex2")
    p.op("vector", lambda e: e.tensor_mul(t2, mean, mean),
         reads=[stk], writes=[stk], tag="ln_m2")
    p.op("vector", lambda e: e.tensor_sub(t3, t1, t2),
         reads=[stk], writes=[stk], tag="ln_var")
    p.op("scalar", lambda e, bb=C.constf(EPS_LN): e.activation(
        t1, t3, AF.Ln, bias=bb),
         reads=[stk, ("cpool",)], writes=[stk], tag="ln_lnv")
    p.op("scalar", lambda e: e.activation(sval, t1, AF.Exp, scale=-0.5),
         reads=[stk], writes=[stk], tag="ln_rstd")
    if rms:
        p.op("vector", lambda e: e.tensor_scalar_add(t2, t3, EPS_LN),
             reads=[stk], writes=[stk], tag="rms_vp")
        p.op("vector", lambda e: e.reciprocal(t1, t2),
             reads=[stk], writes=[stk], tag="rms_rv")
        p.op("vector", lambda e: e.tensor_mul(t2, t3, t1),
             reads=[stk], writes=[stk], tag="rms_q")
        p.op("scalar", lambda e, bb=C.constf(EPS_RMS): e.activation(
            t1, t2, AF.Ln, bias=bb),
             reads=[stk, ("cpool",)], writes=[stk], tag="rms_ln")
        p.op("scalar", lambda e: e.activation(t2, t1, AF.Exp, scale=-0.5),
             reads=[stk], writes=[stk], tag="rms_exp")
        p.op("vector", lambda e: e.tensor_mul(sval, sval, t2),
             reads=[stk], writes=[stk], tag="rms_sval")

    for t in range(NT):
        xf, xfk = xnf[t % 2]
        p.op("vector", lambda e, o=xf, i_=C.x(t), m=mean[:, t:t + 1],
             s=sval[:, t:t + 1]:
             e.tensor_scalar(o, i_, m, s, ALU.subtract, ALU.mult),
             reads=[C.xk(t), stk], writes=[xfk], tag=f"ln_xn{t}")
        if nat:
            natt = out["nat"][:, ts(t, D)]
            natw = (out["natks"] if nat_ap is None
                    else [out["natks"][t * D // 1024 + j]
                          for j in range((D + 1023) // 1024)])
            p.op("vector", lambda e, o=natt, i_=xf:
                 e.tensor_copy(o, i_),
                 reads=[xfk], writes=natw, tag=f"ln_nat{t}")
        for dc in range(8):
            bi = C.bank()
            p.op("tensor", lambda e, o=C.pb[bi][:, 0:128],
                 i_=xf[:, ts(dc, 128)]:
                 e.transpose(o, i_, C.ident),
                 reads=[xfk, ("ident",)], writes=[C.pbk[bi]],
                 tag=f"ln_tp{t}_{dc}")
            emit_copy(C, "scalar" if (dc % 2) else "vector",
                      xnT[:, dc * SEQ + t * 128: dc * SEQ + (t + 1) * 128],
                      C.pb[bi][:, 0:128], [C.pbk[bi]], [xnTk],
                      tag=f"ln_ev{t}_{dc}")
    return out


def emit_copy(C, engine, out, in_, reads, writes, tag=""):
    if engine == "scalar":
        C.prog.op("scalar", lambda e, o=out, i_=in_: e.copy(o, i_),
                  reads=reads, writes=writes, tag=tag)
    else:
        C.prog.op(engine, lambda e, o=out, i_=in_: e.tensor_copy(o, i_),
                  reads=reads, writes=writes, tag=tag)



def emit_l1ffn(C, b):
    p, scr, nc = C.prog, C.scr, C.nc
    scr.reset()
    ln = emit_ln(C, rms=False, nat=False)
    xnT, xnTk = ln["xnT"], ln["xnTk"]

    c_nat, c_natk = scr.alloc(p, NT * 16, F32, "c_nat")
    b1c, b1ck = scr.alloc(p, 256, F32, "b1cols")
    mark0 = scr.mark()

    rw, rwk = scr.alloc(p, 8 * 16, BF16, "rw")
    for dc in range(8):
        p.dma("gpsimd", out=rw[:, ts(dc, 16)],
              in_=C.w["L1_ffn_router_w"][ts(dc, 128), :],
              writes=[rwk], tag=f"rw{dc}")
    brow, browk = scr.alloc(p, 16, BF16, "brow")
    p.dma("gpsimd", out=brow[0:1, :], in_=C.w["rb_row"],
          writes=[browk], tag="brow")

    logits, logitsk = scr.alloc(p, NT * 16, F32, "logits")
    cT, cTk = scr.alloc(p, SEQ, BF16, "cT")
    tops, topsk = scr.alloc(p, 32, F32, "tops")
    top8, negm = tops[:, 0:8], tops[:, 8:9]
    s4c, rsc = tops[:, 9:10], tops[:, 10:11]
    prb, prbk = scr.alloc(p, 48, F32, "prb")
    probs_t, mask_t, pm_t = prb[:, 0:16], prb[:, 16:32], prb[:, 32:48]

    for t in range(NT):
        bi = C.bank()
        ps = C.pb[bi][:, 0:16]
        for dc in range(8):
            p.op("tensor", lambda e, o=ps,
                 l=xnT[:, dc * SEQ + t * 128: dc * SEQ + (t + 1) * 128],
                 r=rw[:, ts(dc, 16)], s=(dc == 0):
                 e.matmul(o, l, r, start=s, stop=False),
                 reads=[xnTk, rwk], writes=[C.pbk[bi]], tag=f"rlog{t}_{dc}")
        p.op("tensor", lambda e, o=ps: e.matmul(
            o, C.ones, brow[0:1, :], start=False, stop=True),
            reads=[("ones",), browk], writes=[C.pbk[bi]], tag=f"rbias{t}")
        p.op("vector", lambda e, o=logits[:, ts(t, 16)], i_=ps:
             e.tensor_copy(o, i_),
             reads=[C.pbk[bi]], writes=[logitsk], tag=f"rlogev{t}")
        lg = logits[:, ts(t, 16)]
        p.op("vector", lambda e, o=top8, i_=lg: e.max(out=o, in_=i_),
             reads=[logitsk], writes=[topsk], tag=f"rtop8{t}")
        p.op("vector", lambda e: e.tensor_scalar_mul(negm, top8[:, 0:1], -1.0),
             reads=[topsk], writes=[topsk], tag=f"rnegm{t}")
        p.op("scalar", lambda e, i_=lg: e.activation(
            probs_t, i_, AF.Exp, bias=negm, accum_out=s4c),
            reads=[logitsk, topsk], writes=[prbk, topsk], tag=f"rexp{t}")
        p.op("vector", lambda e, i_=lg: e.tensor_single_scalar(
            mask_t, i_, top8[:, 3:4], ALU.is_ge),
            reads=[logitsk, topsk], writes=[prbk], tag=f"rmask{t}")
        p.op("vector", lambda e: e.tensor_mul(pm_t, probs_t, mask_t),
             reads=[prbk], writes=[prbk], tag=f"rpm{t}")
        p.op("vector", lambda e: e.reduce_sum(s4c, pm_t, axis=AX.X),
             reads=[prbk], writes=[topsk], tag=f"rs4{t}")
        p.op("vector", lambda e: e.reciprocal(rsc, s4c),
             reads=[topsk], writes=[topsk], tag=f"rrs{t}")
        p.op("vector", lambda e, o=c_nat[:, ts(t, 16)]:
             e.tensor_scalar_mul(o, pm_t, rsc),
             reads=[prbk, topsk], writes=[c_natk], tag=f"rc{t}")
        bj = C.bank()
        p.op("tensor", lambda e, o=C.pb[bj][0:16, 0:128],
             i_=c_nat[:, ts(t, 16)]:
             e.transpose(o, i_, C.ident),
             reads=[c_natk, ("ident",)], writes=[C.pbk[bj]], tag=f"rcT{t}")
        emit_copy(C, "scalar", cT[0:16, t * 128:(t + 1) * 128],
                  C.pb[bj][0:16, 0:128], [C.pbk[bj]], [cTk], tag=f"rcTev{t}")

    if b == DEBUG_B:
        C.debug_dump("xnT", xnT, [xnTk])
        C.debug_dump("logits", logits, [logitsk])
        C.debug_dump("c_nat", c_nat, [c_natk])

    b2sb, b2k = scr.alloc(p, D, BF16, "b2")
    p.dma("gpsimd", out=b2sb[0:16, :], in_=C.w["L1_ffn_b2"],
          writes=[b2k], tag="b2")
    for t in range(NT):
        bi = C.bank()
        for dh in range(2):
            p.op("tensor", lambda e, o=C.pb[bi][:, ts(dh, 512)],
                 l=cT[0:16, ts(t, 128)], r=b2sb[0:16, ts(dh, 512)]:
                 e.matmul(o, l, r, start=True, stop=True),
                 reads=[cTk, b2k], writes=[C.pbk[bi]], tag=f"b2mm{t}_{dh}")
        for dh in range(2):
            p.op("vector", lambda e, o=C.x(t)[:, ts(dh, 512)],
                 i_=C.pb[bi][:, ts(dh, 512)]:
                 e.tensor_add(o, i_, o),
                 reads=[C.pbk[bi], C.xk(t)], writes=[C.xk(t)],
                 tag=f"b2add{t}_{dh}")

    if b == DEBUG_B:
        C.debug_dump("x_postb2", C.x_t.ap(), [C.xk(t) for t in range(NT)])

    t16a, t16ak = scr.alloc(p, 128, F32, "t16a")
    for e_ in range(16):
        p.dma("sync", out=t16a[0:16, :],
              in_=C.w["b1_resh"][ts(e_, 16), :],
              writes=[t16ak], tag=f"b1ld{e_}")
        bi = C.bank()
        p.op("tensor", lambda e, o=C.pb[bi][:, 0:16], i_=t16a[0:16, :]:
             e.transpose(o, i_, C.ident[0:16, 0:16]),
             reads=[t16ak, ("ident",)], writes=[C.pbk[bi]], tag=f"b1tp{e_}")
        p.op("vector", lambda e, o=b1c[:, ts(e_, 16)], i_=C.pb[bi][:, 0:16]:
             e.tensor_copy(o, i_),
             reads=[C.pbk[bi]], writes=[b1ck], tag=f"b1ev{e_}")

    scr.pop(mark0)
    hmid, _hk = scr.alloc(p, 16 * 1024, BF16, "hmid")
    hmidks = [_hk for _ in range(16)]
    w1h = C.w["L1_ffn_w1"]
    w2h = C.w["L1_ffn_w2"]
    for e_ in range(16):
        for dc in range(8):
            p.dma("gpsimd", out=C.wa(2 * dc, 2),
                  in_=w1h[e_, ts(dc, 128), :],
                  writes=C.wak(2 * dc, 2), tag=f"w1_{e_}_{dc}")
        for df in range(16):
            p.dma("gpsimd", out=C.wa(16 + df, 1),
                  in_=w2h[e_, ts(df, 128), :],
                  writes=C.wak(16 + df, 1), tag=f"w2_{e_}_{df}")
        for H in range(2):
            for df in range(16):
                bi = df % 2
                for dc in range(8):
                    for th in range(2):
                        p.op("tensor", lambda e, o=C.pb[bi][:, ts(th, 512)],
                             l=C.wa(2 * dc, 2)[:, ts(df, 128)],
                             r=xnT[:, dc * SEQ + H * 1024 + th * 512:
                                   dc * SEQ + H * 1024 + (th + 1) * 512],
                             s=(dc == 0), st=(dc == 7):
                             e.matmul(o, l, r, start=s, stop=st),
                             reads=C.wak(2 * dc, 2) + [xnTk],
                             writes=[C.pbk[bi]], tag=f"w1mm{e_}_{H}_{df}")
                p.op("scalar", lambda e, o=hmid[:, ts(df, 1024)],
                     i_=C.pb[bi][:, 0:1024], bb=b1c[:, e_ * 16 + df:
                                                    e_ * 16 + df + 1]:
                     e.activation(o, i_, AF.Gelu, bias=bb),
                     reads=[C.pbk[bi], b1ck], writes=[hmidks[df]],
                     tag=f"gelu{e_}_{H}_{df}")
            for t128 in range(8):
                tt = H * 8 + t128
                bj = 2 + (t128 % 2)
                for df in range(16):
                    for dh in range(2):
                        p.op("tensor", lambda e, o=C.pb[bj][:, ts(dh, 512)],
                             l=hmid[:, df * 1024 + t128 * 128:
                                    df * 1024 + (t128 + 1) * 128],
                             r=C.wa(16 + df, 1)[:, ts(dh, 512)],
                             s=(df == 0), st=(df == 15):
                             e.matmul(o, l, r, start=s, stop=st),
                             reads=[hmidks[df]] + C.wak(16 + df, 1),
                             writes=[C.pbk[bj]], tag=f"w2mm{e_}_{tt}_{df}")
                for dh in range(2):
                    p.op("vector", lambda e, o=C.x(tt)[:, ts(dh, 512)],
                         i_=C.pb[bj][:, ts(dh, 512)],
                         cc=c_nat[:, tt * 16 + e_: tt * 16 + e_ + 1]:
                         e.scalar_tensor_tensor(o, i_, cc, o,
                                                ALU.mult, ALU.add),
                         reads=[C.pbk[bj], C.xk(tt), c_natk],
                         writes=[C.xk(tt)], tag=f"stt{e_}_{tt}_{dh}")



PHASE_EMIT = {}
PHASE_EMIT["l1ffn"] = emit_l1ffn

PHASE_WEIGHTS = {
    "l0attn": ["L0_attn_c_attn_w", "cattn_b_resh", "L0_attn_c_proj_w",
               "cproj_b_resh", "L0_attn_wk_c", "L0_attn_wk_e",
               "L0_attn_wv_c", "L0_attn_wv_e"],
    "l0ffn": ["L0_ffn_router_w1", "L0_ffn_router_w2", "L0_ffn_w1",
              "L0_ffn_w2", "L0_ffn_w3"],
    "l1attn": ["WqkvT_perm", "bqkv_resh", "WoT", "bo_resh"],
    "l1ffn": ["L1_ffn_router_w", "rb_row", "L1_ffn_w1", "b1_resh",
              "L1_ffn_w2", "L1_ffn_b2"],
}

WEIGHT_SHAPES = {
    "L0_attn_c_attn_w": (1024, 3072), "cattn_b_resh": (24, 128),
    "L0_attn_c_proj_w": (1024, 1024), "cproj_b_resh": (8, 128),
    "L0_attn_wk_c": (64, 32), "L0_attn_wk_e": (32, 64),
    "L0_attn_wv_c": (64, 32), "L0_attn_wv_e": (32, 64),
    "L0_ffn_router_w1": (1024, 128), "L0_ffn_router_w2": (128, 16),
    "L0_ffn_w1": (16, 1024, 2048), "L0_ffn_w2": (16, 1024, 2048),
    "L0_ffn_w3": (16, 2048, 1024),
    "WqkvT_perm": (1024, 3072), "bqkv_resh": (24, 128),
    "WoT": (1024, 1024), "bo_resh": (8, 128),
    "L1_ffn_router_w": (1024, 16), "rb_row": (1, 16),
    "L1_ffn_w1": (16, 1024, 2048), "b1_resh": (256, 128),
    "L1_ffn_w2": (16, 2048, 1024), "L1_ffn_b2": (16, 1024),
}


def build_program(phases):
    nc = bass.Bass("TRN2", target_bir_lowering=False)
    x_in = nc.dram_tensor("x_in", [NB, SEQ, D], F32, kind="ExternalInput")
    y = nc.dram_tensor("y", [NB, SEQ, D], F32, kind="ExternalOutput")
    names = ["ident_f32"]
    for ph in phases:
        names += PHASE_WEIGHTS[ph]
    prog = Prog(nc)
    C = Ctx(nc, prog)
    C.w["ident_f32"] = nc.dram_tensor(
        "ident_f32", [128, 128], F32, kind="ExternalInput").ap()
    for n in names:
        if n == "ident_f32":
            continue
        C.w[n] = nc.dram_tensor(n, list(WEIGHT_SHAPES[n]), F32,
                                kind="ExternalInput").ap()
    C.emit_consts()
    p = prog
    for b in range(NB):
        src = x_in.ap()[b].rearrange("(c p) d -> p c d", p=128)
        dst_x = C.x_t.ap().rearrange("p (c d) -> p c d", d=D)
        p.dma("sync", out=dst_x, in_=src,
              writes=[C.xk(t) for t in range(NT)], tag=f"xload{b}")
        for ph in phases:
            PHASE_EMIT[ph](C, b)
        dst = y.ap()[b].rearrange("(c p) d -> p c d", p=128)
        src_x = C.x_t.ap().rearrange("p (c d) -> p c d", d=D)
        p.dma("sync", out=dst, in_=src_x,
              reads=[C.xk(t) for t in range(NT)], tag=f"xstore{b}")
        C.scr.reset()
    with nc.Block() as block:
        prog.finalize_and_emit(block)
    return nc, names, prog


def _host_prep(inputs):
    d = {k: np.asarray(v) for k, v in inputs.items()}
    arrs = {}
    arrs["ident_f32"] = np.eye(128, dtype=np.float32)
    f = np.float32
    H_, Dh = 16, 64
    old = np.arange(3 * H_ * Dh).reshape(H_, Dh, 3)
    perm = np.transpose(old, (2, 0, 1)).reshape(-1)
    arrs["WqkvT_perm"] = np.ascontiguousarray(
        d["L1_attn_W_qkv"].astype(f)[perm].T)
    arrs["WoT"] = np.ascontiguousarray(d["L1_attn_W_o"].astype(f).T)
    arrs["bqkv_resh"] = np.ascontiguousarray(
        d["L1_attn_b_qkv"].astype(f)[perm].reshape(24, 128))
    arrs["cattn_b_resh"] = np.ascontiguousarray(
        d["L0_attn_c_attn_b"].astype(f).reshape(24, 128))
    arrs["cproj_b_resh"] = np.ascontiguousarray(
        d["L0_attn_c_proj_b"].astype(f).reshape(8, 128))
    arrs["bo_resh"] = np.ascontiguousarray(
        d["L1_attn_b_o"].astype(f).reshape(8, 128))
    arrs["b1_resh"] = np.ascontiguousarray(
        d["L1_ffn_b1"].astype(f).reshape(256, 128))
    arrs["rb_row"] = np.ascontiguousarray(
        d["L1_ffn_router_b"].astype(f).reshape(1, 16))
    for k, v in d.items():
        arrs.setdefault(k, np.ascontiguousarray(v.astype(f)))
    return arrs


PHASE_GROUPS = [["l0attn", "l0ffn", "l1attn", "l1ffn"]]

_prog_cache = {}


def _get_program(group):
    key = tuple(group)
    if key not in _prog_cache:
        _prog_cache[key] = build_program(list(group))
    return _prog_cache[key]


def kernel(**inputs):
    arrs = _host_prep(inputs)
    x = arrs["L0_attn_hidden_states"]
    shards = [np.ascontiguousarray(x[2 * c:2 * c + 2]) for c in range(8)]
    for group in PHASE_GROUPS:
        nc, names, _ = _get_program(group)
        in_maps = []
        for c in range(8):
            m = {"x_in": shards[c]}
            for n in names:
                m[n] = arrs[n]
            in_maps.append(m)
        res = run_bass_kernel_spmd(nc, in_maps, core_ids=list(range(8)))
        shards = [np.asarray(r["y"]) for r in res.results]
    return np.concatenate(shards, axis=0)



def emit_l1attn(C, b):
    p, scr = C.prog, C.scr
    scr.reset()
    ln = emit_ln(C, rms=True, nat=False)
    xnT, xnTk = ln["xnT"], ln["xnTk"]

    S_ZT, S_QT, S_KT, S_VN, S_PT = 0, 16, 18, 20, 22
    S_WQKV, S_WOT, S_WOSL = 26, 29, 30

    t24, t24k = scr.alloc(p, 128, F32, "t24")
    bqk, bqkk = scr.alloc(p, 32, F32, "bqkcols")
    bo, bok = scr.alloc(p, 8, F32, "bocols")
    p.dma("sync", out=t24[0:24, :], in_=C.w["bqkv_resh"], writes=[t24k],
          tag="bqkld")
    bi = C.bank()
    p.op("tensor", lambda e, o=C.pb[bi][:, 0:24]: e.transpose(
        o, t24[0:24, :], C.ident[0:24, 0:24]),
        reads=[t24k, ("ident",)], writes=[C.pbk[bi]], tag="bqktp")
    p.op("vector", lambda e, i_=C.pb[bi][:, 0:24]: e.tensor_copy(
        bqk[:, 0:24], i_), reads=[C.pbk[bi]], writes=[bqkk], tag="bqkev")
    p.dma("sync", out=t24[0:8, :], in_=C.w["bo_resh"], writes=[t24k],
          tag="bold")
    bi = C.bank()
    p.op("tensor", lambda e, o=C.pb[bi][:, 0:8]: e.transpose(
        o, t24[0:8, :], C.ident[0:8, 0:8]),
        reads=[t24k, ("ident",)], writes=[C.pbk[bi]], tag="botp")
    p.op("vector", lambda e, i_=C.pb[bi][:, 0:8]: e.tensor_copy(
        bo[:, 0:8], i_), reads=[C.pbk[bi]], writes=[bok], tag="boev")

    zero, zerok = scr.alloc(p, 512, F32, "maskzero")
    p.op("vector", lambda e: e.memset(zero, 0.0), writes=[zerok],
         tag="maskzero")
    masks, maskks = [], []
    for i in range(4):
        m, mk = scr.alloc(p, 512, F32, f"mask{i}")
        masks.append(m)
        maskks.append(mk)
        p.op("gpsimd", lambda e, o=m, dj=i * 128: e.affine_select(
            o, zero, pattern=[[1, 512]], compare_op=ALU.is_ge,
            fill=-1e30, base=-dj, channel_multiplier=-1),
            reads=[zerok], writes=[mk], tag=f"mask{i}")

    vf32, vf32k = scr.alloc(p, 2048, F32, "vf32")
    wqsl, wqslk = scr.alloc(p, 1024, F32, "wqslab")
    otmp, otmpk = scr.alloc(p, 2048, F32, "otmp")
    mredh, mredhk, vnsh, vnshk = [], [], [], []
    for h in range(4):
        mm, mk_ = scr.alloc(p, 16, F32, f"mred{h}")
        mredh.append(mm)
        mredhk.append(mk_)
        vv, vk_ = scr.alloc(p, 128, BF16, f"vns{h}")
        vnsh.append(vv)
        vnshk.append(vk_)

    qT, qTk = C.wa(S_QT, 2), [("wa", S_QT), ("wa", S_QT + 1)]
    kT, kTk = C.wa(S_KT, 2), [("wa", S_KT), ("wa", S_KT + 1)]
    vnat, vnatk = C.wa(S_VN, 2), [("wa", S_VN), ("wa", S_VN + 1)]
    pth = [C.wa(S_PT, 2), C.wa(S_PT + 2, 2)]
    pthk = [[("wa", S_PT), ("wa", S_PT + 1)],
            [("wa", S_PT + 2), ("wa", S_PT + 3)]]
    ztb = C.wa(S_ZT, 16)
    ztbk = [("wa", S_ZT + i) for i in range(16)]
    wqkvT = [C.wa(S_WQKV + i, 1) for i in range(3)]
    wqkvTk = [("wa", S_WQKV + i) for i in range(3)]
    wot, wotk = C.wa(S_WOT, 1), [("wa", S_WOT)]
    wosl, woslk = C.wa_f32(S_WOSL, 2), [("wa", S_WOSL), ("wa", S_WOSL + 1)]

    Wq = C.w["WqkvT_perm"]
    Wo = C.w["WoT"]

    for pair in range(8):
        for c in range(3):
            p.dma("gpsimd",
                  out=wqkvT[c].rearrange("p (c8 d) -> p c8 d", d=128),
                  in_=Wq[:, c * 1024 + pair * 128:
                         c * 1024 + (pair + 1) * 128].rearrange(
                      "(c8 p) d -> p c8 d", p=128),
                  writes=[wqkvTk[c]], tag=f"wq{pair}_{c}")
        for c, outb, outk in ((0, qT, qTk), (1, kT, kTk),
                              (2, vf32, [vf32k])):
            for g in range(2):
                bi = C.bank()
                for dc in range(8):
                    for th in range(2):
                        cb = g * 1024 + th * 512
                        p.op("tensor",
                             lambda e, o=C.pb[bi][:, ts(th, 512)],
                             l=wqkvT[c][:, ts(dc, 128)],
                             r=xnT[:, dc * SEQ + cb: dc * SEQ + cb + 512],
                             s=(dc == 0), st=(dc == 7):
                             e.matmul(o, l, r, start=s, stop=st),
                             reads=[wqkvTk[c], xnTk], writes=[C.pbk[bi]],
                             tag=f"qkv{pair}_{c}_{g}")
                bb = bqk[:, c * 8 + pair: c * 8 + pair + 1]
                p.op("scalar", lambda e, o=outb[:, ts(g, 1024)],
                     i_=C.pb[bi][:, 0:1024], bb=bb:
                     e.activation(o, i_, AF.Identity, bias=bb),
                     reads=[C.pbk[bi], bqkk], writes=outk,
                     tag=f"qkvev{pair}_{c}_{g}")
        for kk in range(16):
            bi = C.bank()
            p.op("tensor", lambda e, o=C.pb[bi][:, 0:128],
                 i_=vf32[:, ts(kk, 128)]:
                 e.transpose(o, i_, C.ident),
                 reads=[vf32k, ("ident",)], writes=[C.pbk[bi]],
                 tag=f"vtp{pair}_{kk}")
            emit_copy(C, "vector", vnat[:, ts(kk, 128)],
                      C.pb[bi][:, 0:128], [C.pbk[bi]], vnatk,
                      tag=f"vev{pair}_{kk}")
        def l1_S_first(kb, h):
            hs = slice(h * 64, (h + 1) * 64)
            sp = C.pb[0] if h == 0 else C.pb[3]
            spk = C.pbk[0] if h == 0 else C.pbk[3]
            c0 = kb // 8
            delta = kb * 128 - c0 * 1024
            hb = (delta // 512) * 512
            for th in range(hb // 512, 2):
                p.op("tensor", lambda e, o=sp[:, ts(th, 512)],
                     l=kT[hs, ts(kb, 128)],
                     r=qT[hs, c0 * 1024 + th * 512:
                          c0 * 1024 + (th + 1) * 512]:
                     e.matmul(o, l, r, start=True, stop=True),
                     reads=kTk + qTk, writes=[spk],
                     tag=f"S{pair}_{h}_{kb}_{c0}_{th}")

        def l1_rest_stages(kb, h):
            hs = slice(h * 64, (h + 1) * 64)
            sp = C.pb[0] if h == 0 else C.pb[3]
            spk = C.pbk[0] if h == 0 else C.pbk[3]
            hr = h * 2 + kb % 2
            mr = mredh[hr]
            mrk = mredhk[hr]
            pt = pth[h]
            ptk = pthk[h]
            c0 = kb // 8
            delta = kb * 128 - c0 * 1024
            nch = 2 - c0
            for ci, qc in enumerate(range(c0, 2)):
                hb = (delta // 512) * 512 if qc == c0 else 0
                lo = hb
                if qc != c0:
                    for th in range(2):
                        p.op("tensor", lambda e, o=sp[:, ts(th, 512)],
                             l=kT[hs, ts(kb, 128)],
                             r=qT[hs, qc * 1024 + th * 512:
                                  qc * 1024 + (th + 1) * 512]:
                             e.matmul(o, l, r, start=True, stop=True),
                             reads=kTk + qTk, writes=[spk],
                             tag=f"S{pair}_{h}_{kb}_{qc}_{th}")
                if qc == c0:
                    mi = (delta % 512) // 128
                    p.op("vector", lambda e,
                         o=sp[:, (delta // 512) * 512:
                              (delta // 512) * 512 + 512],
                         m=masks[mi]:
                         e.tensor_add(o, o, m),
                         reads=[spk, maskks[mi]], writes=[spk],
                         tag=f"Sm{pair}_{h}_{kb}")
                    p.op("vector", lambda e,
                         o=mr[:, ci:ci + 1],
                         i_=sp[:, (delta // 512) * 512:
                               (delta // 512) * 512 + 512]:
                         e.reduce_max(o, i_, axis=AX.X),
                         reads=[spk], writes=[mrk],
                         tag=f"Smx{pair}_{h}_{kb}_{qc}")
                    if hb == 0:
                        p.op("vector", lambda e,
                             o=mr[:, 4:5], i_=sp[:, 512:1024:4]:
                             e.reduce_max(o, i_, axis=AX.X),
                             reads=[spk], writes=[mrk],
                             tag=f"Smx2{pair}_{h}_{kb}_{qc}")
                        p.op("vector", lambda e, mr=mr, ci=ci:
                             e.tensor_max(mr[:, ci:ci + 1],
                                          mr[:, ci:ci + 1], mr[:, 4:5]),
                             reads=[mrk], writes=[mrk],
                             tag=f"Smxc{pair}_{h}_{kb}_{qc}")
                else:
                    p.op("vector", lambda e, o=mr[:, ci:ci + 1],
                         i_=sp[:, 0:1024:4]:
                         e.reduce_max(o, i_, axis=AX.X),
                         reads=[spk], writes=[mrk],
                         tag=f"Smx{pair}_{h}_{kb}_{qc}")
                p.op("vector", lambda e, mr=mr, ci=ci: e.tensor_scalar_mul(
                    mr[:, 5:6], mr[:, ci:ci + 1], -1.0),
                    reads=[mrk], writes=[mrk],
                    tag=f"Sng{pair}_{h}_{kb}_{qc}")
                yield
                p.op("scalar", lambda e,
                     o=pt[:, qc * 1024 + lo: (qc + 1) * 1024],
                     i_=sp[:, lo:1024], bb=mr[:, 5:6],
                     a=mr[:, 8 + ci:9 + ci]:
                     e.activation(o, i_, AF.Exp, bias=bb, accum_out=a),
                     reads=[spk, mrk], writes=ptk + [mrk],
                     tag=f"Sex{pair}_{h}_{kb}_{qc}")
                yield
            if nch == 2:
                p.op("vector", lambda e, mr=mr: e.tensor_max(
                    mr[:, 6:7], mr[:, 0:1], mr[:, 1:2]),
                    reads=[mrk], writes=[mrk], tag=f"Scm{pair}_{h}_{kb}")
                p.op("vector", lambda e, mr=mr: e.tensor_scalar_mul(
                    mr[:, 7:8], mr[:, 6:7], -1.0),
                    reads=[mrk], writes=[mrk], tag=f"Scn{pair}_{h}_{kb}")
                p.op("scalar", lambda e, mr=mr: e.activation(
                    mr[:, 10:12], mr[:, 0:2], AF.Exp, bias=mr[:, 7:8]),
                    reads=[mrk], writes=[mrk], tag=f"Sfc{pair}_{h}_{kb}")
                p.op("vector", lambda e, mr=mr: e.tensor_mul(
                    mr[:, 12:14], mr[:, 8:10], mr[:, 10:12]),
                    reads=[mrk], writes=[mrk], tag=f"Sds{pair}_{h}_{kb}")
                p.op("vector", lambda e, mr=mr: e.tensor_add(
                    mr[:, 14:15], mr[:, 12:13], mr[:, 13:14]),
                    reads=[mrk], writes=[mrk], tag=f"Sda{pair}_{h}_{kb}")
                p.op("vector", lambda e, mr=mr: e.reciprocal(
                    mr[:, 15:16], mr[:, 14:15]),
                    reads=[mrk], writes=[mrk], tag=f"Sdr{pair}_{h}_{kb}")
                p.op("vector", lambda e, mr=mr: e.tensor_scalar_mul(
                    mr[:, 10:12], mr[:, 10:12], mr[:, 15:16]),
                    reads=[mrk], writes=[mrk], tag=f"Ssc{pair}_{h}_{kb}")
            else:
                p.op("vector", lambda e, mr=mr: e.reciprocal(
                    mr[:, 11:12], mr[:, 8:9]),
                    reads=[mrk], writes=[mrk], tag=f"Sdr{pair}_{h}_{kb}")
            yield
            for ci, qc in enumerate(range(c0, 2)):
                sccol = mr[:, 10 + ci:11 + ci] if nch == 2 \
                    else mr[:, 11:12]
                p.op("vector", lambda e, o=vnsh[hr][:, ts(ci, 64)],
                     i_=vnat[:, kb * 128 + h * 64:
                             kb * 128 + h * 64 + 64], sc=sccol:
                     e.tensor_scalar_mul(o, i_, sc),
                     reads=vnatk + [mrk], writes=[vnshk[hr]],
                     tag=f"vs{pair}_{h}_{kb}_{qc}")
                zp = C.pb[1 + qc]
                zpk = C.pbk[1 + qc]
                hb_c = (delta // 512) * 512 if qc == c0 else 0
                for th in range(hb_c // 512, 2):
                    tpos = (0, 64) if h == 1 else None
                    p.op("tensor", lambda e,
                         o=zp[hs, ts(th, 512)],
                         l=vnsh[hr][:, ts(ci, 64)],
                         r=pt[:, qc * 1024 + th * 512:
                              qc * 1024 + (th + 1) * 512],
                         s=(kb == 0), stt=(kb == 8 * qc + 4 * th + 3),
                         tp=tpos:
                         e.matmul(o, l, r, start=s, stop=stt,
                                  tile_position=tp),
                         reads=[vnshk[hr]] + ptk, writes=[zpk],
                         tag=f"PV{pair}_{h}_{kb}_{qc}_{th}")

        def l1_task(kb, h):
            l1_S_first(kb, h)
            yield
            for _ in l1_rest_stages(kb, h):
                yield

        gprev = None
        for kb in range(16):
            gA = l1_task(kb, 0)
            gB = l1_task(kb, 1)
            alive = [gA, gB]
            while alive:
                for g in list(alive):
                    try:
                        next(g)
                    except StopIteration:
                        alive.remove(g)
        for h in range(2):
            hs = slice(h * 64, (h + 1) * 64)
            for half in range(2):
                p.op("vector", lambda e,
                     o=ztb[hs, pair * 2048 + half * 1024:
                           pair * 2048 + half * 1024 + 1024],
                     i_=C.pb[1 + half][hs, 0:1024]:
                     e.tensor_copy(o, i_),
                     reads=[C.pbk[1 + half]], writes=ztbk,
                     tag=f"zev{pair}_{h}_{half}")
    if b == DEBUG_B:
        C.debug_dump("ztb_l1", ztb, ztbk)

    for fc in range(8):
        p.dma("gpsimd",
              out=wot.rearrange("p (c8 d) -> p c8 d", d=128),
              in_=Wo[:, ts(fc, 128)].rearrange("(c8 p) d -> p c8 d", p=128),
              writes=wotk, tag=f"wo{fc}")
        po = [C.pb[2 * (fc % 2)], C.pb[2 * (fc % 2) + 1]]
        pok = [C.pbk[2 * (fc % 2)], C.pbk[2 * (fc % 2) + 1]]
        for zc in range(8):
            for th in range(4):
                p.op("tensor", lambda e,
                     o=po[th // 2][:, (th % 2) * 512:(th % 2) * 512 + 512],
                     l=wot[:, ts(zc, 128)],
                     r=ztb[:, zc * 2048 + th * 512: zc * 2048 + (th + 1) * 512],
                     s=(zc == 0), st=(zc == 7):
                     e.matmul(o, l, r, start=s, stop=st),
                     reads=wotk + ztbk, writes=[pok[th // 2]],
                     tag=f"omm{fc}_{zc}_{th}")
        for half in range(2):
            p.op("scalar", lambda e, o=otmp[:, ts(half, 1024)],
                 i_=po[half][:, 0:1024], bb=bo[:, fc:fc + 1]:
                 e.activation(o, i_, AF.Identity, bias=bb),
                 reads=[pok[half], bok], writes=[otmpk],
                 tag=f"oev{fc}_{half}")
        for tk in range(16):
            bi = 2 * ((fc + 1) % 2) + (tk % 2)
            p.op("tensor", lambda e, o=C.pb[bi][:, 0:128],
                 i_=otmp[:, ts(tk, 128)]:
                 e.transpose(o, i_, C.ident),
                 reads=[otmpk, ("ident",)], writes=[C.pbk[bi]],
                 tag=f"otp{fc}_{tk}")
            p.op("vector", lambda e, o=C.x(tk)[:, ts(fc, 128)],
                 i_=C.pb[bi][:, 0:128]:
                 e.tensor_add(o, i_, o),
                 reads=[C.pbk[bi], C.xk(tk)], writes=[C.xk(tk)],
                 tag=f"oadd{fc}_{tk}")


PHASE_EMIT["l1attn"] = emit_l1attn


def emit_l0attn(C, b):
    p, scr = C.prog, C.scr
    scr.reset()
    ln = emit_ln(C, rms=False, nat=False)
    xnT, xnTk = ln["xnT"], ln["xnTk"]

    S_ZT, S_QT, S_KT, S_VR, S_VN, S_PT, S_WQKV, S_CP = \
        0, 16, 18, 20, 22, 24, 26, 29

    t24, t24k = scr.alloc(p, 128, F32, "t24")
    bqk, bqkk = scr.alloc(p, 40, F32, "bqkcols")
    bpr, bprk = scr.alloc(p, 8, F32, "bprcols")
    p.dma("sync", out=t24[0:24, :], in_=C.w["cattn_b_resh"], writes=[t24k],
          tag="battld")
    bi = C.bank()
    p.op("tensor", lambda e, o=C.pb[bi][:, 0:24]: e.transpose(
        o, t24[0:24, :], C.ident[0:24, 0:24]),
        reads=[t24k, ("ident",)], writes=[C.pbk[bi]], tag="battp")
    p.op("vector", lambda e, i_=C.pb[bi][:, 0:24]: e.tensor_copy(
        bqk[:, 0:24], i_), reads=[C.pbk[bi]], writes=[bqkk], tag="battev")
    p.op("vector", lambda e: e.tensor_scalar_mul(
        bqk[:, 24:32], bqk[:, 0:8], 0.125),
        reads=[bqkk], writes=[bqkk], tag="bqscale")
    p.dma("sync", out=t24[0:8, :], in_=C.w["cproj_b_resh"], writes=[t24k],
          tag="bprld")
    bi = C.bank()
    p.op("tensor", lambda e, o=C.pb[bi][:, 0:8]: e.transpose(
        o, t24[0:8, :], C.ident[0:8, 0:8]),
        reads=[t24k, ("ident",)], writes=[C.pbk[bi]], tag="bprtp")
    p.op("vector", lambda e, i_=C.pb[bi][:, 0:8]: e.tensor_copy(
        bpr[:, 0:8], i_), reads=[C.pbk[bi]], writes=[bprk], tag="bprev")

    lat, latk = scr.alloc(p, 4 * 64, BF16, "latw")
    wkc, wke = lat[:, 0:32], lat[0:32, 32:96]
    wvc, wve = lat[:, 96:128], lat[0:32, 128:192]
    p.dma("gpsimd", out=lat[0:64, 0:32], in_=C.w["L0_attn_wk_c"],
          writes=[latk], tag="wkc")
    p.dma("gpsimd", out=lat[64:128, 0:32], in_=C.w["L0_attn_wk_c"],
          writes=[latk], tag="wkc2")
    p.dma("gpsimd", out=lat[0:32, 32:96], in_=C.w["L0_attn_wk_e"],
          writes=[latk], tag="wke")
    p.dma("gpsimd", out=lat[0:64, 96:128], in_=C.w["L0_attn_wv_c"],
          writes=[latk], tag="wvc")
    p.dma("gpsimd", out=lat[64:128, 96:128], in_=C.w["L0_attn_wv_c"],
          writes=[latk], tag="wvc2")
    p.dma("gpsimd", out=lat[0:32, 128:192], in_=C.w["L0_attn_wv_e"],
          writes=[latk], tag="wve")

    zero, zerok = scr.alloc(p, 128, F32, "maskzero")
    p.op("vector", lambda e: e.memset(zero, 0.0), writes=[zerok],
         tag="l0maskzero")
    dmask, dmaskk = scr.alloc(p, 128, F32, "l0diagmask")
    p.op("gpsimd", lambda e: e.affine_select(
        dmask, zero, pattern=[[1, 128]], compare_op=ALU.is_ge,
        fill=-1e30, base=0, channel_multiplier=-1),
        reads=[zerok], writes=[dmaskk], tag="l0dmask")

    vf32, vf32k = scr.alloc(p, 2048, F32, "vf32")
    klatb, klatbk = scr.alloc(p, 2 * 512, BF16, "klat")
    otmp, otmpk = scr.alloc(p, 2048, F32, "otmp")
    mred, mredk = scr.alloc(p, 16, F32, "mred")
    znat, znatk = scr.alloc(p, 1024, F32, "znat")
    mrr, mrrk, vaur, vaurk = [], [], [], []
    for i in range(4):
        a_, k_ = scr.alloc(p, 8, F32, f"mrr{i}")
        mrr.append(a_)
        mrrk.append(k_)
        a_, k_ = scr.alloc(p, 80, BF16, f"vaur{i}")
        vaur.append(a_)
        vaurk.append(k_)

    qT, qTk = C.wa(S_QT, 2), [("wa", S_QT), ("wa", S_QT + 1)]
    kT, kTk = C.wa(S_KT, 2), [("wa", S_KT), ("wa", S_KT + 1)]
    vraw, vrawk = C.wa(S_VR, 2), [("wa", S_VR), ("wa", S_VR + 1)]
    vnat, vnatk = C.wa(S_VN, 2), [("wa", S_VN), ("wa", S_VN + 1)]
    ptr = [C.wa(S_PT, 1), C.wa(S_PT + 1, 1)]
    ptrk = [("wa", S_PT), ("wa", S_PT + 1)]
    wqkvT = [C.wa(S_WQKV + i, 1) for i in range(3)]
    wqkvTk = [("wa", S_WQKV + i) for i in range(3)]
    cpr, cprk = [C.wa(S_CP, 1), C.wa(S_CP + 1, 1)], \
        [("wa", S_CP), ("wa", S_CP + 1)]

    CW = C.w["L0_attn_c_attn_w"]
    CP = C.w["L0_attn_c_proj_w"]
    ztb = C.wa(S_ZT, 16)
    ztbk = [("wa", S_ZT + i) for i in range(16)]

    for pair in range(8):
        for c in range(3):
            p.dma("gpsimd",
                  out=wqkvT[c].rearrange("p (c8 d) -> p c8 d", d=128),
                  in_=CW[:, c * 1024 + pair * 128:
                         c * 1024 + (pair + 1) * 128].rearrange(
                      "(c8 p) d -> p c8 d", p=128),
                  writes=[wqkvTk[c]], tag=f"cw{pair}_{c}")
        for c, outb, outk in ((0, qT, qTk), (1, kT, kTk),
                              (2, vraw, vrawk)):
            for g in range(2):
                bi = C.bank()
                for dc in range(8):
                    for th in range(2):
                        cb = g * 1024 + th * 512
                        p.op("tensor",
                             lambda e, o=C.pb[bi][:, ts(th, 512)],
                             l=wqkvT[c][:, ts(dc, 128)],
                             r=xnT[:, dc * SEQ + cb: dc * SEQ + cb + 512],
                             s=(dc == 0), st=(dc == 7):
                             e.matmul(o, l, r, start=s, stop=st),
                             reads=[wqkvTk[c], xnTk], writes=[C.pbk[bi]],
                             tag=f"aqkv{pair}_{c}_{g}")
                if c == 0:
                    bb = bqk[:, 24 + pair: 25 + pair]
                    sc = 0.125
                else:
                    bb = bqk[:, c * 8 + pair: c * 8 + pair + 1]
                    sc = 1.0
                p.op("scalar", lambda e, o=outb[:, ts(g, 1024)],
                     i_=C.pb[bi][:, 0:1024], bb=bb, sc=sc:
                     e.activation(o, i_, AF.Identity, bias=bb, scale=sc),
                     reads=[C.pbk[bi], bqkk], writes=outk,
                     tag=f"aqkvev{pair}_{c}_{g}")
        for g in range(4):
            gsl = slice(g * 512, (g + 1) * 512)
            for src_, wc, we, dstk, dst in ((kT, wkc, wke, kTk, kT),
                                            (vraw, wvc, wve, None, None)):
                for h in range(2):
                    hs = slice(h * 64, (h + 1) * 64)
                    bi = C.bank()
                    p.op("tensor", lambda e, o=C.pb[bi][0:32, 0:512],
                         l=wc[hs, :], r=src_[hs, gsl]:
                         e.matmul(o, l, r, start=True, stop=True),
                         reads=(kTk if src_ is kT else vrawk) + [latk],
                         writes=[C.pbk[bi]], tag=f"lat{pair}_{g}_{h}")
                    p.op("vector", lambda e, o=klatb[0:32, ts(h, 512)],
                         i_=C.pb[bi][0:32, 0:512]:
                         e.tensor_copy(o, i_),
                         reads=[C.pbk[bi]], writes=[klatbk],
                         tag=f"latev{pair}_{g}_{h}")
                for h in range(2):
                    hs = slice(h * 64, (h + 1) * 64)
                    bi = C.bank()
                    p.op("tensor", lambda e, o=C.pb[bi][0:64, 0:512],
                         l=we, r=klatb[0:32, ts(h, 512)]:
                         e.matmul(o, l, r, start=True, stop=True),
                         reads=[klatbk, latk], writes=[C.pbk[bi]],
                         tag=f"exp{pair}_{g}_{h}")
                    if src_ is kT:
                        p.op("vector", lambda e, o=kT[hs, gsl],
                             i_=C.pb[bi][0:64, 0:512]:
                             e.tensor_copy(o, i_),
                             reads=[C.pbk[bi]], writes=kTk,
                             tag=f"expev{pair}_{g}_{h}")
                    else:
                        p.op("vector", lambda e, o=vf32[hs, gsl],
                             i_=C.pb[bi][0:64, 0:512]:
                             e.tensor_copy(o, i_),
                             reads=[C.pbk[bi]], writes=[vf32k],
                             tag=f"vexpev{pair}_{g}_{h}")
        for kk in range(16):
            bi = C.bank()
            p.op("tensor", lambda e, o=C.pb[bi][:, 0:128],
                 i_=vf32[:, ts(kk, 128)]:
                 e.transpose(o, i_, C.ident),
                 reads=[vf32k, ("ident",)], writes=[C.pbk[bi]],
                 tag=f"avtp{pair}_{kk}")
            emit_copy(C, "vector", vnat[:, ts(kk, 128)],
                      C.pb[bi][:, 0:128], [C.pbk[bi]], vnatk,
                      tag=f"avev{pair}_{kk}")
        for h in range(2):
            hs = slice(h * 64, (h + 1) * 64)
            for zi2 in (1, 2):
                p.op("vector", lambda e, o=C.pb[zi2][:, 0:1024]:
                     e.memset(o, 0.0),
                     writes=[C.pbk[zi2]], tag=f"azclr{pair}_{h}_{zi2}")
            tasks = []
            ti = 0
            for kb in range(16):
                c0 = kb // 8
                delta = kb * 128 - c0 * 1024
                for qc in range(c0, 2):
                    on_diag = (qc == c0)
                    hb = (delta // 512) * 512 if on_diag else 0
                    tasks.append(dict(kb=kb, qc=qc, od=on_diag, hb=hb,
                                      lo=(delta if on_diag else 0),
                                      delta=delta,
                                      si=(0 if ti % 2 == 0 else 3),
                                      ri=ti))
                    ti += 1

            def emit_S(t):
                sp, spk = C.pb[t["si"]], C.pbk[t["si"]]
                for th in range(t["hb"] // 512, 2):
                    p.op("tensor", lambda e, o=sp[:, ts(th, 512)],
                         l=kT[hs, ts(t["kb"], 128)],
                         r=qT[hs, t["qc"] * 1024 + th * 512:
                              t["qc"] * 1024 + (th + 1) * 512]:
                         e.matmul(o, l, r, start=True, stop=True),
                         reads=kTk + qTk, writes=[spk],
                         tag=f"aS{pair}_{h}_{t['kb']}_{t['qc']}_{th}")

            def emit_pre(t):
                kb, qc, delta = t["kb"], t["qc"], t["delta"]
                sp, spk = C.pb[t["si"]], C.pbk[t["si"]]
                ri = t["ri"]
                mr, mrk = mrr[ri % 4], mrrk[ri % 4]
                if t["od"]:
                    p.op("vector", lambda e, mr=mr, o=sp[:, delta:delta + 128]:
                         e.reduce_max(mr[:, 0:1], o, axis=AX.X, negate=True),
                         reads=[spk], writes=[mrk],
                         tag=f"aSmx{pair}_{h}_{kb}_{qc}")
                    if delta + 128 < 1024:
                        p.op("vector", lambda e, mr=mr,
                             i_=sp[:, delta + 128:1024:4]:
                             e.reduce_max(mr[:, 1:2], i_, axis=AX.X,
                                          negate=True),
                             reads=[spk], writes=[mrk],
                             tag=f"aSmx2{pair}_{h}_{kb}_{qc}")
                        p.op("vector", lambda e, mr=mr: e.tensor_tensor(
                            mr[:, 2:3], mr[:, 0:1], mr[:, 1:2], ALU.min),
                            reads=[mrk], writes=[mrk],
                            tag=f"aSmc{pair}_{h}_{kb}_{qc}")
                        t["ng"] = 2
                    else:
                        t["ng"] = 0
                    p.op("vector", lambda e, o=sp[:, delta:delta + 128]:
                         e.tensor_add(o, o, dmask),
                         reads=[spk, dmaskk], writes=[spk],
                         tag=f"aSm{pair}_{h}_{kb}")
                else:
                    p.op("vector", lambda e, mr=mr, i_=sp[:, 0:1024:4]:
                         e.reduce_max(mr[:, 0:1], i_, axis=AX.X, negate=True),
                         reads=[spk], writes=[mrk],
                         tag=f"aSmx{pair}_{h}_{kb}_{qc}")
                    t["ng"] = 0

            def emit_post(t):
                kb, qc, delta, lo = t["kb"], t["qc"], t["delta"], t["lo"]
                sp, spk = C.pb[t["si"]], C.pbk[t["si"]]
                ri = t["ri"]
                mr, mrk = mrr[ri % 4], mrrk[ri % 4]
                va, vak = vaur[ri % 4], vaurk[ri % 4]
                pt, ptk = ptr[ri % 2], ptrk[ri % 2]
                negm = mr[:, t["ng"]:t["ng"] + 1]
                p.op("scalar", lambda e,
                     o=pt[:, lo:1024],
                     i_=sp[:, lo:1024], bb=negm:
                     e.activation(o, i_, AF.Exp, bias=bb),
                     reads=[spk, mrk], writes=[ptk],
                     tag=f"aSex{pair}_{h}_{kb}_{qc}")
                p.op("scalar", lambda e, mr=mr, i_=negm:
                     e.activation(mr[:, 4:5], i_, AF.Exp, scale=-1.0),
                     reads=[mrk], writes=[mrk],
                     tag=f"aSem{pair}_{h}_{kb}_{qc}")
                p.op("vector", lambda e, o=va[:, 64:65], mr=mr:
                     e.tensor_copy(o, mr[:, 4:5]),
                     reads=[mrk], writes=[vak],
                     tag=f"ave{pair}_{h}_{kb}_{qc}")
                p.op("vector", lambda e, o=va[:, 0:64],
                     i_=vnat[:, kb * 128 + h * 64:
                             kb * 128 + h * 64 + 64], mr=mr:
                     e.tensor_scalar_mul(o, i_, mr[:, 4:5]),
                     reads=vnatk + [mrk], writes=[vak],
                     tag=f"avs{pair}_{h}_{kb}_{qc}")
                for qt in range(max(kb, qc * 8), qc * 8 + 8):
                    zi = 1 if qt < 8 else 2
                    zcol = (qt % 4) * 128 + ((qt % 8) // 4) * 512
                    p.op("tensor", lambda e,
                         o=C.pb[zi][:, zcol:zcol + 65],
                         l=pt[:, (qt - qc * 8) * 128:
                               (qt - qc * 8 + 1) * 128],
                         r=va[:, 0:65]:
                         e.matmul(o, l, r, start=False, stop=False,
                                  skip_group_check=True),
                         reads=[ptk, vak], writes=[C.pbk[zi]],
                         tag=f"aPV{pair}_{h}_{kb}_{qt}")

            prev = None
            for t in tasks:
                emit_S(t)
                emit_pre(t)
                if prev is not None:
                    emit_post(prev)
                prev = t
            emit_post(prev)
            for qt in range(16):
                zi = 1 if qt < 8 else 2
                zcol = (qt % 4) * 128 + ((qt % 8) // 4) * 512
                p.op("vector", lambda e, i_=C.pb[zi][:, zcol + 64:zcol + 65]:
                     e.reciprocal(mred[:, 4:5], i_),
                     reads=[C.pbk[zi]], writes=[mredk],
                     tag=f"azr{pair}_{h}_{qt}")
                p.op("vector", lambda e, o=znat[:, ts(qt, 64)],
                     i_=C.pb[zi][:, zcol:zcol + 64]:
                     e.tensor_scalar_mul(o, i_, mred[:, 4:5]),
                     reads=[C.pbk[zi], mredk], writes=[znatk],
                     tag=f"azn{pair}_{h}_{qt}")
            for qt in range(16):
                si = 0 if (qt % 2 == 0) else 3
                p.op("tensor", lambda e, o=C.pb[si][0:64, 0:128],
                     i_=znat[:, ts(qt, 64)]:
                     e.transpose(o, i_, C.ident),
                     reads=[znatk, ("ident",)], writes=[C.pbk[si]],
                     tag=f"aztp{pair}_{h}_{qt}")
                p.op("vector", lambda e,
                     o=ztb[hs, pair * 2048 + qt * 128:
                           pair * 2048 + (qt + 1) * 128],
                     i_=C.pb[si][0:64, 0:128]:
                     e.tensor_copy(o, i_),
                     reads=[C.pbk[si]],
                     writes=[("wa", S_ZT + pair * 2 + qt // 8)],
                     tag=f"azev{pair}_{h}_{qt}")

    if b == DEBUG_B:
        C.debug_dump("ztb_l0", ztb, ztbk)
    for fc in range(8):
        po = [C.pb[2 * (fc % 2)], C.pb[2 * (fc % 2) + 1]]
        pok = [C.pbk[2 * (fc % 2)], C.pbk[2 * (fc % 2) + 1]]
        p.dma("gpsimd",
              out=cpr[fc % 2].rearrange("p (c8 d) -> p c8 d", d=128),
              in_=CP[:, ts(fc, 128)].rearrange("(c8 p) d -> p c8 d", p=128),
              writes=[cprk[fc % 2]], tag=f"acp{fc}")
        for zc in range(8):
            for th in range(4):
                p.op("tensor", lambda e,
                     o=po[th // 2][:, (th % 2) * 512:(th % 2) * 512 + 512],
                     l=cpr[fc % 2][:, ts(zc, 128)],
                     r=ztb[:, zc * 2048 + th * 512: zc * 2048 + (th + 1) * 512],
                     s=(zc == 0), st=(zc == 7):
                     e.matmul(o, l, r, start=s, stop=st),
                     reads=[cprk[fc % 2]] + ztbk, writes=[pok[th // 2]],
                     tag=f"aomm{fc}_{zc}_{th}")
        for half in range(2):
            p.op("scalar", lambda e, o=otmp[:, ts(half, 1024)],
                 i_=po[half][:, 0:1024], bb=bpr[:, fc:fc + 1]:
                 e.activation(o, i_, AF.Identity, bias=bb),
                 reads=[pok[half], bprk], writes=[otmpk],
                 tag=f"aoev{fc}_{half}")
        for tk in range(16):
            bi = 2 * ((fc + 1) % 2) + (tk % 2)
            p.op("tensor", lambda e, o=C.pb[bi][:, 0:128],
                 i_=otmp[:, ts(tk, 128)]:
                 e.transpose(o, i_, C.ident),
                 reads=[otmpk, ("ident",)], writes=[C.pbk[bi]],
                 tag=f"aotp{fc}_{tk}")
            p.op("vector", lambda e, o=C.x(tk)[:, ts(fc, 128)],
                 i_=C.pb[bi][:, 0:128]:
                 e.tensor_add(o, i_, o),
                 reads=[C.pbk[bi], C.xk(tk)], writes=[C.xk(tk)],
                 tag=f"aoadd{fc}_{tk}")


PHASE_EMIT["l0attn"] = emit_l0attn


def emit_l0ffn(C, b):
    p, scr = C.prog, C.scr
    scr.reset()

    S_NAT, S_W1, S_W2 = 0, 16, 24
    natk = [("wa", S_NAT + i) for i in range(16)]
    nat = C.wa(S_NAT, 16)
    w1b = C.wa(S_W1, 8)
    w2b = C.wa(S_W2, 8)
    w1qk = [[("wa", S_W1 + 2 * q), ("wa", S_W1 + 2 * q + 1)]
            for q in range(4)]
    w2qk = [[("wa", S_W2 + 2 * q), ("wa", S_W2 + 2 * q + 1)]
            for q in range(4)]

    pk3, pk3k = scr.alloc(p, 2048, F32, "pack3")
    rbc, rbck = scr.alloc(p, 2048, F32, "rank_bcast")
    rkc, rkck = scr.alloc(p, 256, F32, "rank_cols")
    wcl, wclk = scr.alloc(p, 256, BF16, "w_cols")
    ewc, ewck = scr.alloc(p, 64, F32, "ew_top8")
    mark0 = scr.mark()
    pk1, pk1k = scr.alloc(p, 2048, F32, "pack1")
    pk2, pk2k = scr.alloc(p, 2048, F32, "pack2")

    logits = pk1[0:16, :]
    zap = pk2[0:16, :]
    cum3 = pk3[0:16, :]
    top8 = ewc[0:16, 0:8]
    ew2 = ewc[:, 16:18]

    ln = emit_ln(C, rms=False, nat=True, nat_ap=nat, nat_keys=natk)
    xnT, xnTk = ln["xnT"], ln["xnTk"]

    rw1, rw1k = scr.alloc(p, 1024, BF16, "rw1")
    rw2, rw2k = scr.alloc(p, 16, BF16, "rw2")
    r1T, r1Tk = scr.alloc(p, 512, BF16, "r1T")
    for dc in range(8):
        p.dma("gpsimd", out=rw1[:, ts(dc, 128)],
              in_=C.w["L0_ffn_router_w1"][ts(dc, 128), :],
              writes=[rw1k], tag=f"rw1{dc}")
    p.dma("gpsimd", out=rw2, in_=C.w["L0_ffn_router_w2"],
          writes=[rw2k], tag="rw2")
    for tc in range(4):
        bi = C.bank()
        for dc in range(8):
            p.op("tensor", lambda e, o=C.pb[bi][:, 0:512],
                 l=rw1[:, ts(dc, 128)],
                 r=xnT[:, dc * SEQ + tc * 512: dc * SEQ + (tc + 1) * 512],
                 s=(dc == 0), st=(dc == 7):
                 e.matmul(o, l, r, start=s, stop=st),
                 reads=[rw1k, xnTk], writes=[C.pbk[bi]], tag=f"r1{tc}")
        p.op("scalar", lambda e, o=r1T,
             i_=C.pb[bi][:, 0:512]:
             e.activation(o, i_, AF.Silu),
             reads=[C.pbk[bi]], writes=[r1Tk], tag=f"r1ev{tc}")
        bj = C.bank()
        p.op("tensor", lambda e, o=C.pb[bj][0:16, 0:512],
             l=rw2, r=r1T:
             e.matmul(o, l, r, start=True, stop=True),
             reads=[rw2k, r1Tk], writes=[C.pbk[bj]], tag=f"r2{tc}")
        p.op("vector", lambda e, o=logits[:, ts(tc, 512)],
             i_=C.pb[bj][0:16, 0:512]:
             e.tensor_copy(o, i_),
             reads=[C.pbk[bj]], writes=[pk1k], tag=f"r2ev{tc}")

    p.op("vector", lambda e: e.reduce_max(
        ewc[0:16, 8:9], logits, axis=AX.X, negate=True),
        reads=[pk1k], writes=[ewck], tag="smax")
    p.op("scalar", lambda e: e.activation(
        logits, logits, AF.Exp, bias=ewc[0:16, 8:9],
        accum_out=ewc[0:16, 9:10]),
        reads=[pk1k, ewck], writes=[pk1k, ewck], tag="sexp")
    p.op("vector", lambda e: e.reciprocal(ewc[0:16, 10:11],
                                          ewc[0:16, 9:10]),
         reads=[ewck], writes=[ewck], tag="srec")
    p.op("vector", lambda e: e.tensor_scalar_mul(
        logits, logits, ewc[0:16, 10:11]),
        reads=[pk1k, ewck], writes=[pk1k], tag="snorm")
    p.op("vector", lambda e: e.tensor_copy(zap, logits),
         reads=[pk1k], writes=[pk2k], tag="zap0")
    for r in range(32):
        p.op("vector", lambda e: e.max(out=ewc[0:16, 0:8], in_=zap),
             reads=[pk2k], writes=[ewck], tag=f"top8_{r}")
        p.op("vector", lambda e: e.match_replace(
            out=zap, in_to_replace=ewc[0:16, 0:8], in_values=zap,
            imm_value=0.0),
            reads=[pk2k, ewck], writes=[pk2k], tag=f"zap_{r}")
    p.op("vector", lambda e: e.tensor_sub(logits, logits, zap),
         reads=[pk1k, pk2k], writes=[pk1k], tag="psel")
    p.op("vector", lambda e: e.reduce_sum(ewc[0:16, 11:12], logits,
                                          axis=AX.X),
         reads=[pk1k], writes=[ewck], tag="ssum")
    p.op("vector", lambda e: e.tensor_scalar_add(
        ewc[0:16, 12:13], ewc[0:16, 11:12], 1e-9),
        reads=[ewck], writes=[ewck], tag="ssum2")
    p.op("vector", lambda e: e.reciprocal(ewc[0:16, 13:14],
                                          ewc[0:16, 12:13]),
         reads=[ewck], writes=[ewck], tag="srs")
    p.op("vector", lambda e: e.tensor_single_scalar(
        zap, logits, 0.0, ALU.is_gt),
        reads=[pk1k], writes=[pk2k], tag="mask0")
    pr = [(zap, pk2k), (cum3, pk3k)]
    src_i = 0
    for k in range(11):
        s = 1 << k
        aa, ak = pr[src_i]
        bb, bk = pr[1 - src_i]
        p.op("vector", lambda e, aa=aa, bb=bb, s=s: e.tensor_copy(
            bb[:, 0:s], aa[:, 0:s]),
            reads=[ak], writes=[bk], tag=f"cumc{k}")
        p.op("vector", lambda e, aa=aa, bb=bb, s=s: e.tensor_add(
            bb[:, s:2048], aa[:, s:2048], aa[:, 0:2048 - s]),
            reads=[ak], writes=[bk], tag=f"cuma{k}")
        src_i = 1 - src_i
    assert src_i == 1
    p.op("vector", lambda e: e.tensor_single_scalar(
        zap, logits, 0.0, ALU.is_gt),
        reads=[pk1k], writes=[pk2k], tag="mask2")
    p.op("vector", lambda e: e.tensor_mul(cum3, cum3, zap),
         reads=[pk3k, pk2k], writes=[pk3k], tag="rankm")
    p.op("vector", lambda e: e.tensor_scalar_sub(cum3, cum3, 1.0),
         reads=[pk3k], writes=[pk3k], tag="rank1")
    rankp = cum3
    p.op("vector", lambda e: e.tensor_scalar_mul(
        logits, logits, ewc[0:16, 13:14]),
        reads=[pk1k, ewck], writes=[pk1k], tag="pseln")
    pseln = logits
    for t in range(NT):
        bi = C.bank()
        p.op("tensor", lambda e, o=C.pb[bi][:, 0:16],
             i_=rankp[:, ts(t, 128)]:
             e.transpose(o, i_, C.ident[0:16, 0:16]),
             reads=[pk3k, ("ident",)], writes=[C.pbk[bi]], tag=f"rktp{t}")
        p.op("vector", lambda e, o=rkc[:, ts(t, 16)],
             i_=C.pb[bi][:, 0:16]:
             e.tensor_copy(o, i_),
             reads=[C.pbk[bi]], writes=[rkck], tag=f"rkev{t}")
        bj = C.bank()
        p.op("tensor", lambda e, o=C.pb[bj][:, 0:16],
             i_=pseln[:, ts(t, 128)]:
             e.transpose(o, i_, C.ident[0:16, 0:16]),
             reads=[pk1k, ("ident",)], writes=[C.pbk[bj]], tag=f"wctp{t}")
        p.op("vector", lambda e, o=wcl[:, ts(t, 16)],
             i_=C.pb[bj][:, 0:16]:
             e.tensor_copy(o, i_),
             reads=[C.pbk[bj]], writes=[wclk], tag=f"wcev{t}")

    scr.pop(mark0)
    einT, einTk = scr.alloc(p, 2048, BF16, "einT")
    hbuf, hk = scr.alloc(p, 16 * 256, BF16, "hbuf")
    eoT, eoTk = scr.alloc(p, 1024, F32, "eoT")
    sring, sringk = scr.alloc(p, 2 * 256, BF16, "silu_ring")
    ones_f, ones_fk = scr.alloc(p, 128, F32, "ones_f")
    selb, selbk = [], []
    for cc in range(2):
        a_, k_ = scr.alloc(p, 2048, BF16, f"sel{cc}")
        selb.append(a_)
        selbk.append([k_])
    selT, selTk0 = scr.alloc(p, 4096, BF16, "selT")
    selTk = [selTk0] * 4
    eon, eonk = [], []
    for cc in range(2):
        a_, k_ = scr.alloc(p, 1024, BF16, f"eon{cc}")
        eon.append(a_)
        eonk.append(k_)
    w3r, w3rk = [], []
    for i in range(2):
        a_, k_ = scr.alloc(p, 4096, BF16, f"w3r{i}")
        w3r.append(a_)
        w3rk.append(k_)
    p.op("vector", lambda e: e.memset(ones_f[0:1, :], 1.0),
         writes=[ones_fk], tag="onesf")

    w1h, w2h, w3h = C.w["L0_ffn_w1"], C.w["L0_ffn_w2"], C.w["L0_ffn_w3"]

    for e_ in range(16):
        for t in range(NT):
            p.op("vector", lambda e, o=selT[:, ts(t, 256)],
                 rc=rkc[:, t * 16 + e_: t * 16 + e_ + 1]:
                 e.tensor_single_scalar(o, C.iota_row, rc, ALU.is_equal),
                 reads=[rkck, ("iota",)], writes=[selTk[t // 4]],
                 tag=f"selT{e_}_{t}")
        p.dma("gpsimd", out=rbc[0:1, :], in_=cum3[e_:e_ + 1, :],
              reads=[pk3k], writes=[rbck], tag=f"rrow{e_}")
        for g in range(4):
            bi = C.bank()
            p.op("tensor", lambda e, o=C.pb[bi][:, 0:512],
                 l=ones_f[0:1, :], r=rbc[0:1, ts(g, 512)]:
                 e.matmul(o, l, r, start=True, stop=True),
                 reads=[ones_fk, rbck], writes=[C.pbk[bi]],
                 tag=f"rbc{e_}_{g}")
            p.op("vector", lambda e, o=rbc[:, ts(g, 512)],
                 i_=C.pb[bi][:, 0:512]:
                 e.tensor_copy(o, i_),
                 reads=[C.pbk[bi]], writes=[rbck], tag=f"rbcev{e_}_{g}")
        for cc in range(2):
            p.op("vector", lambda e, o=selb[cc][:, 0:2048],
                 ic=(C.iota_c0 if cc == 0 else C.iota_c1):
                 e.tensor_single_scalar(o, rbc, ic, ALU.is_equal),
                 reads=[rbck, ("iota",)], writes=selbk[cc],
                 tag=f"sel{e_}_{cc}")
        for cc in range(2):
            bi = C.bank()
            for t in range(NT):
                p.op("tensor", lambda e, o=C.pb[bi][:, 0:1],
                     l=selT[:, t * 256 + cc * 128: t * 256 + cc * 128 + 128],
                     r=wcl[:, t * 16 + e_: t * 16 + e_ + 1],
                     s=(t == 0), st=(t == NT - 1):
                     e.matmul(o, l, r, start=s, stop=st),
                     reads=[selTk[t // 4], wclk], writes=[C.pbk[bi]],
                     tag=f"ew{e_}_{cc}_{t}")
            p.op("vector", lambda e, o=ew2[:, cc:cc + 1],
                 i_=C.pb[bi][:, 0:1]:
                 e.tensor_copy(o, i_),
                 reads=[C.pbk[bi]], writes=[ewck], tag=f"ewev{e_}_{cc}")
        for dc in range(8):
            bi = C.bank()
            for t in range(NT):
                p.op("tensor", lambda e, o=C.pb[bi][:, 0:256],
                     l=nat[:, t * 1024 + dc * 128: t * 1024 + (dc + 1) * 128],
                     r=selT[:, ts(t, 256)],
                     s=(t == 0), st=(t == NT - 1):
                     e.matmul(o, l, r, start=s, stop=st),
                     reads=[natk[t], selTk[t // 4]], writes=[C.pbk[bi]],
                     tag=f"gth{e_}_{dc}_{t}")
            p.op("vector", lambda e, o=einT[:, ts(dc, 256)],
                 i_=C.pb[bi][:, 0:256]:
                 e.tensor_copy(o, i_),
                 reads=[C.pbk[bi]], writes=[einTk], tag=f"gthev{e_}_{dc}")
        for half in range(2):
            for q in range(4):
                p.dma("gpsimd",
                      out=w1b[:, q * 2048:(q + 1) * 2048].rearrange(
                          "p (c d) -> p c d", d=1024),
                      in_=w1h[e_, q * 256:(q + 1) * 256,
                              half * 1024:(half + 1) * 1024].rearrange(
                          "(c p) d -> p c d", p=128),
                      writes=w1qk[q], tag=f"w1{e_}_{half}_{q}")
                p.dma("gpsimd",
                      out=w2b[:, q * 2048:(q + 1) * 2048].rearrange(
                          "p (c d) -> p c d", d=1024),
                      in_=w2h[e_, q * 256:(q + 1) * 256,
                              half * 1024:(half + 1) * 1024].rearrange(
                          "(c p) d -> p c d", p=128),
                      writes=w2qk[q], tag=f"w2{e_}_{half}_{q}")
            for dc in range(8):
                for df in range(8):
                    pg = C.pb[df // 4]
                    pv = C.pb[2 + df // 4]
                    po = (df % 4) * 256
                    p.op("tensor", lambda e, o=pg[:, po:po + 256],
                         l=w1b[:, dc * 1024 + df * 128:
                               dc * 1024 + (df + 1) * 128],
                         r=einT[:, ts(dc, 256)],
                         s=(dc == 0), st=(dc == 7):
                         e.matmul(o, l, r, start=s, stop=st),
                         reads=w1qk[dc // 2] + [einTk],
                         writes=[C.pbk[df // 4]], tag=f"g{e_}_{half}_{df}")
                    p.op("tensor", lambda e, o=pv[:, po:po + 256],
                         l=w2b[:, dc * 1024 + df * 128:
                               dc * 1024 + (df + 1) * 128],
                         r=einT[:, ts(dc, 256)],
                         s=(dc == 0), st=(dc == 7):
                         e.matmul(o, l, r, start=s, stop=st),
                         reads=w2qk[dc // 2] + [einTk],
                         writes=[C.pbk[2 + df // 4]],
                         tag=f"v{e_}_{half}_{df}")
            for df in range(8):
                pg = C.pb[df // 4]
                pv = C.pb[2 + df // 4]
                po = (df % 4) * 256
                p.op("scalar", lambda e, o=sring[:, ts(df % 2, 256)],
                     i_=pg[:, po:po + 256]:
                     e.activation(o, i_, AF.Silu),
                     reads=[C.pbk[df // 4]], writes=[sringk],
                     tag=f"silu{e_}_{half}_{df}")
                p.op("vector", lambda e,
                     o=hbuf[:, (half * 8 + df) * 256:
                            (half * 8 + df + 1) * 256],
                     i_=pv[:, po:po + 256], g_=sring[:, ts(df % 2, 256)]:
                     e.tensor_mul(o, i_, g_),
                     reads=[C.pbk[2 + df // 4], sringk], writes=[hk],
                     tag=f"h{e_}_{half}_{df}")
        for dh in range(2):
            for P in range(2):
                p.dma("gpsimd",
                      out=w3r[P].rearrange("p (c d) -> p c d", d=512),
                      in_=w3h[e_, P * 1024:(P + 1) * 1024,
                              dh * 512:(dh + 1) * 512].rearrange(
                          "(c p) d -> p c d", p=128),
                      writes=[w3rk[P]], tag=f"w3{e_}_{dh}_{P}")
            for dffc in range(16):
                for dq in range(4):
                    pq = C.pb[dq // 2]
                    po = (dq % 2) * 256
                    p.op("tensor", lambda e, o=pq[:, po:po + 256],
                         l=w3r[dffc // 8][:, (dffc % 8) * 512 + dq * 128:
                                          (dffc % 8) * 512 + (dq + 1) * 128],
                         r=hbuf[:, ts(dffc, 256)],
                         s=(dffc == 0), st=(dffc == 15):
                         e.matmul(o, l, r, start=s, stop=st),
                         reads=[w3rk[dffc // 8], hk],
                         writes=[C.pbk[dq // 2]], tag=f"w3m{e_}_{dh}_{dq}")
            for dq in range(4):
                pq = C.pb[dq // 2]
                po = (dq % 2) * 256
                p.op("vector", lambda e, o=eoT[:, ts(dq, 256)],
                     i_=pq[:, po:po + 256]:
                     e.tensor_copy(o, i_),
                     reads=[C.pbk[dq // 2]], writes=[eoTk],
                     tag=f"w3ev{e_}_{dh}_{dq}")
            for dq in range(4):
                for cc in range(2):
                    bi = 2 + (dq % 2)
                    p.op("tensor", lambda e, o=C.pb[bi][:, 512:640],
                         i_=eoT[:, dq * 256 + cc * 128:
                                dq * 256 + (cc + 1) * 128]:
                         e.transpose(o, i_, C.ident),
                         reads=[eoTk, ("ident",)], writes=[C.pbk[bi]],
                         tag=f"eontp{e_}_{dh}_{dq}_{cc}")
                    dcol = dh * 512 + dq * 128
                    p.op("vector", lambda e,
                         o=eon[cc][:, dcol:dcol + 128],
                         i_=C.pb[bi][:, 512:640],
                         sc=ew2[:, cc:cc + 1]:
                         e.tensor_scalar_mul(o, i_, sc),
                         reads=[C.pbk[bi], ewck], writes=[eonk[cc]],
                         tag=f"eonev{e_}_{dh}_{dq}_{cc}")
        for t in range(NT):
            bi = 2 + (t % 2)
            for cc in range(2):
                for dh in range(2):
                    p.op("tensor", lambda e,
                         o=C.pb[bi][:, ts(dh, 512)],
                         l=selb[cc][:, ts(t, 128)],
                         r=eon[cc][:, ts(dh, 512)],
                         s=(cc == 0), st=(cc == 1):
                         e.matmul(o, l, r, start=s, stop=st),
                         reads=selbk[cc] + [eonk[cc]],
                         writes=[C.pbk[bi]], tag=f"sc{e_}_{t}_{cc}_{dh}")
            p.op("vector", lambda e, o=C.x(t), i_=C.pb[bi][:, 0:1024]:
                 e.tensor_add(o, i_, o),
                 reads=[C.pbk[bi], C.xk(t)], writes=[C.xk(t)],
                 tag=f"scadd{e_}_{t}")


PHASE_EMIT["l0ffn"] = emit_l0ffn
